# version 07
# baseline (speedup 1.0000x reference)
import numpy as np
import ml_dtypes
import concourse.bass as bass
import concourse.mybir as mybir
from concourse.bass_utils import run_bass_kernel_spmd

F32 = mybir.dt.float32
BF16 = mybir.dt.bfloat16
I32 = mybir.dt.int32
AF = mybir.ActivationFunctionType
ALU = mybir.AluOpType
AX = mybir.AxisListType
BF16NP = ml_dtypes.bfloat16

NCORES = 8
BATCH = 16
L = 2048
D = 1024
SEQ = 2
T = SEQ * L

ENGINES = ("sync", "tensor", "vector", "scalar", "gpsimd")


class RecInst:
    __slots__ = ("ctx", "item")

    def __init__(self, ctx, item):
        self.ctx = ctx
        self.item = item

    def inc(self, name, n=1):
        self.ctx.cnt[name] = self.ctx.cnt.get(name, 0) + n
        self.item[3] = (name, n)
        return self.ctx.cnt[name]


class Rec:
    __slots__ = ("ctx", "eng")

    def __init__(self, ctx, eng):
        object.__setattr__(self, "ctx", ctx)
        object.__setattr__(self, "eng", eng)

    def wait(self, name, val=None):
        ctx = self.ctx
        if val is None:
            val = ctx.cnt.get(name, 0)
        if val <= 0:
            return
        if name in ("dma", "dst") and self.eng != "sync":
            tok = "ldk" if name == "dma" else "stk"
            if ctx.flushed.get(name, -1) < val:
                ctx.plan["sync"].append(["wait_ge", (name, ctx.cnt.get(name, 0)), {}, None])
                ctx.plan["sync"].append(["sem_inc", (tok, 1), {}, None])
                ctx.cnt[tok] = ctx.cnt.get(tok, 0) + 1
                ctx.flushed[name] = ctx.cnt.get(name, 0)
                ctx.lastwait[("sync", name)] = ctx.cnt.get(name, 0)
            self.wait(tok, ctx.cnt[tok])
            return
        key = (self.eng, name)
        if ctx.lastwait.get(key, -1) >= val:
            return
        ctx.lastwait[key] = val
        ctx.plan[self.eng].append(["wait_ge", (name, val), {}, None])

    def __getattr__(self, meth):
        ctx = self.ctx
        plan = ctx.plan[self.eng]

        def f(*a, **k):
            item = [meth, a, k, None]
            plan.append(item)
            return RecInst(ctx, item)

        return f


class Ctx:
    def __init__(self, nc):
        self.nc = nc
        self.plan = {e: [] for e in ENGINES}
        self.cnt = {}
        self.lastwait = {}
        self.flushed = {}
        self.sems = {}
        self._sem_guards = []
        self._vsep_tile = nc.alloc_sbuf_tensor("ctx_vsep", [1, 16], F32)
        self.sy = Rec(self, "sync")
        self.te = Rec(self, "tensor")
        self.ve = Rec(self, "vector")
        self.sc = Rec(self, "scalar")
        self.gp = Rec(self, "gpsimd")

    def vsep(self):
        t = self._vsep_tile
        self.ve.tensor_copy(out=t[0:1, 0:8], in_=t[0:1, 8:16])
        self.ve.tensor_copy(out=t[0:1, 0:8], in_=t[0:1, 8:16])

    def fence(self):
        snap = {k: self.cnt.get(k, 0) for k in ("te", "ve", "se", "gp", "dma", "dst")}
        for rec, own in ((self.te, "te"), (self.ve, "ve"), (self.sc, "se"),
                         (self.gp, "gp"), (self.sy, None)):
            for k, v in snap.items():
                if k == own:
                    continue
                rec.wait(k, v)

    def _sem(self, name):
        if name not in self.sems:
            g = self.nc.semaphore(name)
            self.sems[name] = g.__enter__()
            self._sem_guards.append(g)
        return self.sems[name]

    def emit(self):
        for e in ENGINES:
            for meth, a, k, incr in self.plan[e]:
                if meth in ("wait_ge", "sem_inc"):
                    self._sem(a[0])
                if incr is not None:
                    self._sem(incr[0])
        nc = self.nc
        with nc.Block() as block:
            for e in ENGINES:
                if not self.plan[e]:
                    continue

                def runner(eng, e=e):
                    for meth, a, k, incr in self.plan[e]:
                        if meth == "wait_ge":
                            eng.wait_ge(self.sems[a[0]], a[1])
                            continue
                        if meth == "sem_inc":
                            eng.sem_inc(self.sems[a[0]], a[1])
                            continue
                        inst = getattr(eng, meth)(*a, **k)
                        if incr is not None:
                            inst.then_inc(self.sems[incr[0]], incr[1])

                getattr(block, e)(runner)
        for g in self._sem_guards:
            g.__exit__(None, None, None)


def host_consts():
    return {
        "c_ones_f": np.ones((128, 1), np.float32),
        "c_ones_b": np.ones((128, 1), BF16NP),
        "c_onesrow_f": np.ones((1, 128), np.float32),
        "c_onesrow_b": np.ones((1, 128), BF16NP),
        "c_scal": np.array([[1e-5, 1.0, 0.0, 0.0, 0.0, 0.0, 0.0, 0.0]], np.float32),
    }


def declare_consts(nc):
    c = {}
    c["ones_f"] = nc.dram_tensor("c_ones_f", [128, 1], F32, kind="ExternalInput").ap()
    c["ones_b"] = nc.dram_tensor("c_ones_b", [128, 1], BF16, kind="ExternalInput").ap()
    c["onesrow_f"] = nc.dram_tensor("c_onesrow_f", [1, 128], F32, kind="ExternalInput").ap()
    c["onesrow_b"] = nc.dram_tensor("c_onesrow_b", [1, 128], BF16, kind="ExternalInput").ap()
    c["scal"] = nc.dram_tensor("c_scal", [1, 8], F32, kind="ExternalInput").ap()
    return c


def load_consts(ctx, sb):
    sy = ctx.sy
    t = {}
    t["ones_f"] = sb("ct_ones_f", [128, 1], F32)
    t["ones_b"] = sb("ct_ones_b", [128, 1], BF16)
    t["onesrow_f"] = sb("ct_onesrow_f", [1, 128], F32)
    t["onesrow_b"] = sb("ct_onesrow_b", [1, 128], BF16)
    t["scal"] = sb("ct_scal", [1, 8], F32)
    sy.dma_start(out=t["ones_f"][:], in_=ctx.consts["ones_f"]).inc("dma", 16)
    sy.dma_start(out=t["ones_b"][:], in_=ctx.consts["ones_b"]).inc("dma", 16)
    sy.dma_start(out=t["onesrow_f"][:], in_=ctx.consts["onesrow_f"]).inc("dma", 16)
    sy.dma_start(out=t["onesrow_b"][:], in_=ctx.consts["onesrow_b"]).inc("dma", 16)
    sy.dma_start(out=t["scal"][:], in_=ctx.consts["scal"]).inc("dma", 16)
    t["eps"] = t["scal"][0:1, 0:1]
    t["one"] = t["scal"][0:1, 1:2]
    return t


def emit_ln(ctx, ct, x_hbm, xhat, sb, pb, ntok=T):
    nc = ctx.nc
    sy, te, ve, sc = ctx.sy, ctx.te, ctx.ve, ctx.sc
    ctx.ln_id = getattr(ctx, "ln_id", 0) + 1
    P = f"ln{ctx.ln_id}_"
    xt = [sb(f"{P}xt{i}", [128, 512], F32) for i in range(8)]
    sq = sb(P + "sq", [128, 4096], F32)
    s_mean = sb(P + "mean", [1, 512], F32)
    s_var = sb(P + "var", [1, 512], F32)
    s_rstd = sb(P + "rstd", [1, 512], F32)
    s_mr = sb(P + "mr", [1, 512], F32)
    p_s1 = pb(P + "p_s1", [1, 512], F32)
    p_s2 = pb(P + "p_s2", [1, 512], F32)
    p_mr = pb(P + "p_mr", [128, 512], F32)
    p_rs = pb(P + "p_rs", [128, 512], F32)

    for tc in range(ntok // 512):
        sl = slice(tc * 512, (tc + 1) * 512)
        sy.wait("ve")
        for ft in range(8):
            sy.dma_start(out=xt[ft][:],
                         in_=x_hbm[ft * 128:(ft + 1) * 128, sl]).inc("dma", 16)
        te.wait("dma")
        te.wait("ve")
        for ft in range(8):
            te.matmul(p_s1[:], ct["ones_f"][:], xt[ft][:],
                      start=(ft == 0), stop=(ft == 7))
        sc.wait("dma")
        sc.wait("te")
        for ft in range(8):
            si = sc.activation(out=sq[:, ft * 512:(ft + 1) * 512], in_=xt[ft][:],
                               func=AF.Square)
        si.inc("se")
        te.wait("se")
        for ft in range(8):
            mi = te.matmul(p_s2[:], ct["ones_f"][:], sq[:, ft * 512:(ft + 1) * 512],
                           start=(ft == 0), stop=(ft == 7))
        mi.inc("te")
        ve.wait("te")
        ve.tensor_scalar_mul(out=s_mean[:], in0=p_s1[:], scalar1=1.0 / D)
        ve.tensor_mul(out=s_var[:], in0=s_mean[:], in1=s_mean[:])
        ve.scalar_tensor_tensor(out=s_var[:], in0=p_s2[:], scalar=1.0 / D,
                                in1=s_var[:], op0=ALU.mult,
                                op1=ALU.subtract).inc("ve")
        sc.wait("ve")
        sc.activation(out=s_var[:], in_=s_var[:], func=AF.Ln, bias=ct["eps"])
        sc.activation(out=s_rstd[:], in_=s_var[:], func=AF.Exp,
                      scale=-0.5).inc("se")
        ve.wait("se")
        ve.tensor_mul(out=s_mr[:], in0=s_mean[:], in1=s_rstd[:]).inc("ve")
        te.wait("ve")
        te.matmul(p_mr[:], ct["onesrow_f"][:], s_mr[:], start=True, stop=True)
        te.matmul(p_rs[:], ct["onesrow_f"][:], s_rstd[:], start=True,
                  stop=True).inc("te")
        ve.wait("te")
        for ft in range(8):
            ve.tensor_mul(out=xt[ft][:], in0=xt[ft][:], in1=p_rs[:])
            vi = ve.tensor_sub(out=xhat[ft][:, sl], in0=xt[ft][:], in1=p_mr[:])
        vi.inc("ve")
    ctx.fence()


def emit_l0a(ctx, ct, x_hbm, wqkv, wo, y_hbm):
    from contextlib import ExitStack
    nc = ctx.nc
    sy, te, ve, sc, gp = ctx.sy, ctx.te, ctx.ve, ctx.sc, ctx.gp

    with ExitStack() as es:
        def sb(name, shape, dtype):
            return es.enter_context(nc.sbuf_tensor(name, shape, dtype))

        xhat = [sb(f"l0a_xhat{i}", [128, T], BF16) for i in range(8)]
        with ExitStack() as ln_es:
            def sbl(name, shape, dtype):
                return ln_es.enter_context(nc.sbuf_tensor(name, shape, dtype))
            def pbl(name, shape, dtype):
                return ln_es.enter_context(nc.psum_tensor(name, shape, dtype))
            emit_ln(ctx, ct, x_hbm, xhat, sbl, pbl)

        hp = [sb(f"l0a_hp{i}", [128, L], BF16) for i in range(8)]
        wstage = [sb("l0a_wstage0", [128, 1024], F32), sb("l0a_wstage1", [128, 512], F32)]
        wq_sb = [sb(f"l0a_wq{i}", [128, 512], BF16) for i in range(2)]
        wk_sb = [sb(f"l0a_wk{i}", [128, 512], BF16) for i in range(2)]
        wv_sb = [sb(f"l0a_wv{i}", [128, 512], BF16) for i in range(2)]
        qop = [sb(f"l0a_qop{i}", [65, L], BF16) for i in range(2)]
        kop = [sb(f"l0a_kop{i}", [65, L], BF16) for i in range(2)]
        vop_all = [sb(f"l0a_vop{i}", [128, 16 * 65], BF16) for i in range(2)]
        sqn = [sb(f"l0a_sqn{i}", [64, L], BF16) for i in range(2)]
        n_k = [sb(f"l0a_nk{i}", [1, L], F32) for i in range(2)]
        b2 = [sb(f"l0a_b2{i}", [1, 1], F32) for i in range(2)]
        c_row = n_k
        dinv = sb("l0a_dinv", [1, 1024], F32)
        attnT = [sb(f"l0a_attnT{i}", [128, 512], BF16) for i in range(2)]
        o_sb = sb("l0a_osb", [65, 1024], F32)
        drep = sb("l0a_drep", [64, 1024], F32)
        wo_sb = [sb(f"l0a_wo{i}", [128, 1024], BF16) for i in range(8)]
        xt_res = sb("l0a_xt", [128, 512], F32)
        yt_res = sb("l0a_yt", [128, 512], F32)

        def pb(name, shape, dtype):
            return es.enter_context(nc.psum_tensor(name, shape, dtype))
        p_rot = [pb(f"l0a_p_r{i}", [128, 512], F32) for i in range(3)]
        p_s = [pb("l0a_p_s0", [128, 512], F32), pb("l0a_p_s1", [128, 512], F32)]
        p_o = pb("l0a_p_o", [65, 1024], F32)
        p_dr = pb("l0a_p_dr", [128, 512], F32)
        p_n = p_dr

        vop = [[va[:, tt * 65:(tt + 1) * 65] for tt in range(16)] for va in vop_all]

        for par in range(2):
            gp.memset(kop[par][64:65, :], 1.0)
            for tt in range(16):
                g = gp.memset(vop_all[par][:, tt * 65 + 64: tt * 65 + 65], 1.0)
        g.inc("gp")
        te.wait("gp")
        sc.wait("gp")

        for i in range(8):
            sy.wait("ve")
            sy.dma_start(out=wstage[0][:], in_=wo[i * 128:(i + 1) * 128, :]).inc("dma", 16)
            ve.wait("dma")
            ve.tensor_copy(out=wo_sb[i][:], in_=wstage[0][:]).inc("ve")

        st = [dict(wrdy=0, pdone=0) for _ in range(2)]
        rot_free = [0, 0, 0]
        rot_i = [0]

        def emit_wload(h, s):
            par = h % 2
            for wdst, a in ((wq_sb[par], 0), (wk_sb[par], 1), (wv_sb[par], 2)):
                sy.wait("te", st[par]["pdone"])
                sy.wait("ve")
                src = bass.AP(wqkv.tensor, (h * 3 + a) * 1024 * 64,
                              [[64, 128], [128 * 64, 8], [1, 64]])
                sy.dma_start(out=wstage[1][:], in_=src).inc("dma", 16)
                ve.wait("dma")
                ve.wait("te", st[par]["pdone"])
                ve.tensor_copy(out=wdst[:], in_=wstage[1][:]).inc("ve")
            st[par]["wrdy"] = ctx.cnt["ve"]

        def emit_proj(h, s):
            par = h % 2
            ts0 = s * L
            te.wait("ve", st[par]["wrdy"])
            jobs = ([("q", qc) for qc in range(4)] + [("k", qc) for qc in range(4)]
                    + [("v", tt) for tt in range(16)])
            for kind, idx in jobs:
                ri = rot_i[0]
                rot_i[0] = (ri + 1) % 3
                pbuf = p_rot[ri]
                te.wait("se", rot_free[ri])
                if kind in ("q", "k"):
                    qsl = slice(ts0 + idx * 512, ts0 + (idx + 1) * 512)
                    lsl = slice(idx * 512, (idx + 1) * 512)
                    wsb = wq_sb[par] if kind == "q" else wk_sb[par]
                    for kt in range(8):
                        mi = te.matmul(pbuf[0:64, :], wsb[:, kt * 64:(kt + 1) * 64],
                                       xhat[kt][:, qsl], start=(kt == 0), stop=(kt == 7))
                    mi.inc("te")
                    sc.wait("te")
                    dsttile = qop[par] if kind == "q" else kop[par]
                    rot_free[ri] = sc.activation(
                        out=dsttile[0:64, lsl], in_=pbuf[0:64, :], func=AF.Copy,
                        scale=(0.125 if kind == "q" else 1.0)).inc("se")
                else:
                    tsl = slice(ts0 + idx * 128, ts0 + (idx + 1) * 128)
                    for kt in range(8):
                        mi = te.matmul(pbuf[:, 0:64], xhat[kt][:, tsl],
                                       wv_sb[par][:, kt * 64:(kt + 1) * 64],
                                       start=(kt == 0), stop=(kt == 7))
                    mi.inc("te")
                    sc.wait("te")
                    rot_free[ri] = sc.copy(out=vop[par][idx][:, 0:64],
                                           in_=pbuf[:, 0:64]).inc("se")
            st[par]["pdone"] = ctx.cnt["te"]

        def emit_cbound(h, s):
            par = h % 2
            sc.activation(out=sqn[par][:], in_=kop[par][0:64, :],
                          func=AF.Square).inc("se")
            te.wait("se")
            for qc in range(4):
                te.wait("ve")
                te.matmul(p_n[0:1, 0:512], ct["ones_b"][0:64, :],
                          sqn[par][:, qc * 512:(qc + 1) * 512],
                          start=True, stop=True).inc("te")
                ve.wait("te")
                ve.tensor_copy(out=n_k[par][0:1, qc * 512:(qc + 1) * 512],
                               in_=p_n[0:1, 0:512]).inc("ve")
            ctx.vsep()
            ve.reduce_max(out=b2[par][:], in_=n_k[par][:], axis=AX.X).inc("ve")
            sc.wait("te")
            sc.activation(out=sqn[par][:], in_=qop[par][0:64, :],
                          func=AF.Square).inc("se")
            te.wait("se")
            te.wait("ve")
            for qc in range(4):
                te.wait("se")
                te.matmul(p_n[0:1, 0:512], ct["ones_b"][0:64, :],
                          sqn[par][:, qc * 512:(qc + 1) * 512],
                          start=True, stop=True).inc("te")
                sc.wait("te")
                sc.wait("ve")
                sc.activation(out=c_row[par][0:1, qc * 512:(qc + 1) * 512],
                              in_=p_n[0:1, 0:512], func=AF.Sqrt,
                              scale=b2[par][:]).inc("se")
            sc.mul(out=qop[par][64:65, :], in_=c_row[par][:], mul=-1.0).inc("se")

        def emit_attn(h, s):
            par = h % 2
            exp_cnt = [0, 0]
            pv_cnt = [0, 0]
            cbase = ctx.cnt.get("se", 0)
            for qh in range(2):
                for kt in range(16):
                    for qq in range(2):
                        buf = (kt * 2 + qq) % 2
                        te.wait("se", cbase)
                        te.wait("se", exp_cnt[buf])
                        te.matmul(p_s[buf][:],
                                  kop[par][:, kt * 128:(kt + 1) * 128],
                                  qop[par][:, qh * 1024 + qq * 512:
                                      qh * 1024 + (qq + 1) * 512],
                                  start=True, stop=True).inc("te")
                        tmm = ctx.cnt["te"]
                        sc.wait("te", pv_cnt[buf])
                        sc.wait("te", tmm)
                        exp_cnt[buf] = sc.activation(out=attnT[buf][:],
                                                     in_=p_s[buf][:],
                                                     func=AF.Exp).inc("se")
                        te.wait("se", exp_cnt[buf])
                        mi = te.matmul(p_o[:, qq * 512:(qq + 1) * 512],
                                       vop[par][kt][:, 0:65],
                                       attnT[buf][:],
                                       start=(kt == 0), stop=(kt == 15))
                        pv_cnt[buf] = mi.inc("te")
                ve.wait("te")
                ve.reciprocal(out=dinv[:], in_=p_o[64:65, :]).inc("ve")
                te.wait("ve")
                te.matmul(p_dr[:], ct["onesrow_f"][0:1, :], dinv[0:1, 0:512],
                          start=True, stop=True).inc("te")
                sc.wait("te")
                sc.copy(out=drep[:, 0:512], in_=p_dr[0:64, :]).inc("se")
                te.wait("se")
                te.matmul(p_dr[:], ct["onesrow_f"][0:1, :],
                          dinv[0:1, 512:1024], start=True, stop=True).inc("te")
                sc.wait("te")
                sc.copy(out=drep[:, 512:1024], in_=p_dr[0:64, :])
                sc.copy(out=o_sb[:], in_=p_o[:]).inc("se")
                ve.wait("se")
                dsttile = hp[h // 2][(h % 2) * 64:(h % 2) * 64 + 64,
                                     qh * 1024:(qh + 1) * 1024]
                ve.tensor_mul(out=dsttile, in0=o_sb[0:64, :], in1=drep[:]).inc("ve")
                te.wait("ve")

        for s in range(SEQ):
            for h in range(16):
                emit_wload(h, s)
                emit_proj(h, s)
                if h >= 1:
                    emit_attn(h - 1, s)
                emit_cbound(h, s)
            emit_attn(15, s)

            ts0 = s * L
            hpr = ctx.cnt["ve"]
            wo_free = [0, 0, 0]
            xb_free = [0, 0, 0]
            yt_store = [0, 0, 0]
            xt3 = [xt_res, yt_res, sb(f"l0a_xt3_{s}", [128, 512], F32)]
            yt3 = [sb(f"l0a_yt3_{i}_{s}", [128, 512], F32) for i in range(3)]
            u = 0
            for tc in range(4):
                tsl = slice(ts0 + tc * 512, ts0 + (tc + 1) * 512)
                tloc = slice(tc * 512, (tc + 1) * 512)
                for jc in range(8):
                    ri = u % 3
                    u += 1
                    sy.wait("ve", xb_free[ri])
                    sy.dma_start(out=xt3[ri][:],
                                 in_=x_hbm[jc * 128:(jc + 1) * 128, tsl]).inc("dma", 16)
                    te.wait("ve", max(hpr, wo_free[ri]))
                    for p in range(8):
                        mi = te.matmul(p_rot[ri][:], wo_sb[p][:, jc * 128:(jc + 1) * 128],
                                       hp[p][:, tloc], start=(p == 0), stop=(p == 7))
                    mi.inc("te")
                    ve.wait("te")
                    ve.wait("dma")
                    ve.wait("dst", yt_store[ri])
                    ve.tensor_add(out=yt3[ri][:], in0=xt3[ri][:], in1=p_rot[ri][:]).inc("ve")
                    wo_free[ri] = ctx.cnt["ve"]
                    xb_free[ri] = ctx.cnt["ve"]
                    sy.wait("ve")
                    sy.dma_start(out=y_hbm[jc * 128:(jc + 1) * 128, tsl],
                                 in_=yt3[ri][:]).inc("dst", 16)
                    yt_store[ri] = ctx.cnt["dst"]
            te.wait("ve")


def emit_ffn0(ctx, ct, x_hbm, wgT, wuT, wdT, y_hbm, ytmp, tag=""):
    from contextlib import ExitStack
    nc = ctx.nc
    sy, te, ve, sc = ctx.sy, ctx.te, ctx.ve, ctx.sc

    with ExitStack() as es:
        def sb(name, shape, dtype):
            return es.enter_context(nc.sbuf_tensor(name + tag, shape, dtype))
        def pb(name, shape, dtype):
            return es.enter_context(nc.psum_tensor(name + tag, shape, dtype))

        xhat = [sb(f"f0_xhat{i}", [128, T], BF16) for i in range(8)]
        with ExitStack() as ln_es:
            emit_ln(ctx, ct, x_hbm, xhat,
                    lambda n, s, d: ln_es.enter_context(nc.sbuf_tensor(n + tag, s, d)),
                    lambda n, s, d: ln_es.enter_context(nc.psum_tensor(n + tag, s, d)))

        wg_sb = [sb(f"f0_wg{i}", [128, 1536], BF16) for i in range(8)]
        wu_sb = [sb(f"f0_wu{i}", [128, 1536], BF16) for i in range(8)]
        wd_sb = [sb(f"f0_wd{i}", [128, 1024], BF16) for i in range(12)]
        wint = sb("f0_wint", [128, 1536], I32)
        g_sb = [sb("f0_g0", [128, 512], F32), sb("f0_g1", [128, 512], F32)]
        h_act = [sb(f"f0_h{i}", [128, 512], BF16) for i in range(12)]
        xt = sb("f0_xt", [128, 512], F32)
        yt = sb("f0_yt", [128, 512], F32)
        p_g = [pb("f0_p_g0", [128, 512], F32), pb("f0_p_g1", [128, 512], F32)]
        p_u = [pb("f0_p_u0", [128, 512], F32), pb("f0_p_u1", [128, 512], F32)]
        p_y = [pb("f0_p_y0", [128, 512], F32), pb("f0_p_y1", [128, 512], F32)]

        for half in range(2):
            hof = half * 1536
            for grp, wsb_list, whbm, rows, cols in (
                (0, wg_sb, wgT, 1024, 1536),
                (1, wu_sb, wuT, 1024, 1536),
                (2, wd_sb, wdT, 3072, 1024),
            ):
                nt = len(wsb_list)
                for i in range(nt):
                    sy.wait("ve")
                    if grp < 2:
                        src = whbm[i * 128:(i + 1) * 128, hof:hof + 1536]
                        w = 1536
                    else:
                        src = whbm[hof + i * 128: hof + (i + 1) * 128, :]
                        w = 1024
                    sy.dma_start(out=wint[:, 0:w], in_=src).inc("dma", 16)
                    ve.wait("dma")
                    ve.wait("te")
                    ve.tensor_scalar_mul(out=wsb_list[i][:, 0:w], in0=wint[:, 0:w],
                                         scalar1=1.0 / 127.0).inc("ve")
            wrdy = ctx.cnt["ve"]
            te.wait("ve", wrdy)
            silu_free = [0, 0]
            gsb_free = [0, 0]
            mult_free = [0, 0]
            py_free = [0, 0]
            for tc in range(8):
                tsl = slice(tc * 512, (tc + 1) * 512)
                for ht in range(12):
                    ri = ht % 2
                    te.wait("se", silu_free[ri])
                    for kt in range(8):
                        mi = te.matmul(p_g[ri][:], wg_sb[kt][:, ht * 128:(ht + 1) * 128],
                                       xhat[kt][:, tsl], start=(kt == 0), stop=(kt == 7))
                    mi.inc("te")
                    sc.wait("te")
                    sc.wait("ve", gsb_free[ri])
                    silu_free[ri] = sc.activation(out=g_sb[ri][:], in_=p_g[ri][:],
                                                  func=AF.Silu).inc("se")
                    te.wait("ve", mult_free[ri])
                    for kt in range(8):
                        mi = te.matmul(p_u[ri][:], wu_sb[kt][:, ht * 128:(ht + 1) * 128],
                                       xhat[kt][:, tsl], start=(kt == 0), stop=(kt == 7))
                    mi.inc("te")
                    ve.wait("te")
                    ve.wait("se", silu_free[ri])
                    mult_free[ri] = ve.tensor_mul(out=h_act[ht][:], in0=g_sb[ri][:],
                                                  in1=p_u[ri][:]).inc("ve")
                    gsb_free[ri] = mult_free[ri]
                hrdy = ctx.cnt["ve"]
                for jc in range(8):
                    ri = jc % 2
                    jsl = slice(jc * 128, (jc + 1) * 128)
                    sy.wait("ve")
                    if half == 0:
                        sy.dma_start(out=xt[:], in_=x_hbm[jsl, tsl]).inc("dma", 16)
                    else:
                        sy.dma_start(out=xt[:], in_=ytmp[jsl, tsl]).inc("dma", 16)
                    te.wait("ve", max(hrdy, py_free[ri]))
                    for ht in range(12):
                        mi = te.matmul(p_y[ri][:], wd_sb[ht][:, jsl], h_act[ht][:],
                                       start=(ht == 0), stop=(ht == 11))
                    mi.inc("te")
                    ve.wait("te")
                    ve.wait("dma")
                    ve.tensor_add(out=yt[:], in0=xt[:], in1=p_y[ri][:]).inc("ve")
                    py_free[ri] = ctx.cnt["ve"]
                    sy.wait("ve")
                    if half == 0:
                        sy.dma_start(out=ytmp[jsl, tsl], in_=yt[:]).inc("dst", 16)
                    else:
                        sy.dma_start(out=y_hbm[jsl, tsl], in_=yt[:]).inc("dst", 16)
                    ve.wait("dst")
        ctx.fence()


def build_ffn0():
    nc = bass.Bass("TRN2", target_bir_lowering=False)
    ctx = Ctx(nc)
    ctx.consts = declare_consts(nc)
    x_hbm = nc.dram_tensor("x", [D, T], F32, kind="ExternalInput").ap()
    wgT = nc.dram_tensor("wgT", [1024, 3072], I32, kind="ExternalInput").ap()
    wuT = nc.dram_tensor("wuT", [1024, 3072], I32, kind="ExternalInput").ap()
    wdT = nc.dram_tensor("wdT", [3072, 1024], I32, kind="ExternalInput").ap()
    y_hbm = nc.dram_tensor("y", [D, T], F32, kind="ExternalOutput").ap()
    ytmp = nc.dram_tensor("ytmp", [D, T], F32).ap()

    from contextlib import ExitStack
    with ExitStack() as es:
        def sb(name, shape, dtype):
            return es.enter_context(nc.sbuf_tensor(name, shape, dtype))
        ct = load_consts(ctx, sb)
        emit_ffn0(ctx, ct, x_hbm, wgT, wuT, wdT, y_hbm, ytmp)
    ctx.sy.wait("dma")
    ctx.sy.wait("dst")
    ctx.emit()
    return nc



def emit_mamba(ctx, ct, x_hbm, w, y_hbm, scr):
    from contextlib import ExitStack
    import os as _os
    PAR = set((_os.environ.get("KPAR") or "").split(","))
    nc = ctx.nc
    sy, te, ve, sc, gp = ctx.sy, ctx.te, ctx.ve, ctx.sc, ctx.gp

    with ExitStack() as es:
        def sb(name, shape, dtype):
            return es.enter_context(nc.sbuf_tensor(name, shape, dtype))
        def pb(name, shape, dtype):
            return es.enter_context(nc.psum_tensor(name, shape, dtype))

        cols = sb("mb_cols", [128, 160], F32)
        sy.dma_start(out=cols[:, 0:64], in_=w["convw"]).inc("dma", 16)
        sy.dma_start(out=cols[:, 64:80], in_=w["convb"]).inc("dma", 16)
        sy.dma_start(out=cols[:, 80:96], in_=w["bdt"]).inc("dma", 16)
        sy.dma_start(out=cols[:, 96:112], in_=w["dcol"]).inc("dma", 16)
        a_sb = sb("mb_asb", [128, 256], F32)
        sy.dma_start(out=a_sb[:], in_=w["acols"]).inc("dma", 16)
        sel16 = sb("mb_sel16", [16, 2048], BF16)
        sy.dma_start(out=sel16[:], in_=w["sel16"]).inc("dma", 16)
        sc.wait("dma")
        sc.activation(out=a_sb[:], in_=a_sb[:], func=AF.Exp).inc("se")
        ve.wait("se")
        ve.tensor_scalar_mul(out=a_sb[:], in0=a_sb[:], scalar1=-1.0).inc("ve")
        wxdt = [sb(f"mb_wxdt{i}", [128, 64], BF16) for i in range(16)]
        wxB = [sb(f"mb_wxB{i}", [128, 16], BF16) for i in range(16)]
        wxC = [sb(f"mb_wxC{i}", [128, 16], BF16) for i in range(16)]
        wdt_sb = sb("mb_wdt", [64, 2048], BF16)
        wsts_es = ExitStack()
        wst_s = wsts_es.enter_context(nc.sbuf_tensor("mb_wsts", [128, 1024], F32))
        for i in range(16):
            sy.wait("ve")
            sy.dma_start(out=wst_s[:, 0:64], in_=w["wxdt"][i * 128:(i + 1) * 128, :]).inc("dma", 16)
            sy.dma_start(out=wst_s[:, 64:80], in_=w["wxB"][i * 128:(i + 1) * 128, :]).inc("dma", 16)
            sy.dma_start(out=wst_s[:, 80:96], in_=w["wxC"][i * 128:(i + 1) * 128, :]).inc("dma", 16)
            ve.wait("dma")
            ve.tensor_copy(out=wxdt[i][:], in_=wst_s[:, 0:64])
            ve.tensor_copy(out=wxB[i][:], in_=wst_s[:, 64:80])
            ve.tensor_copy(out=wxC[i][:], in_=wst_s[:, 80:96]).inc("ve")
        for hc in range(2):
            sy.wait("ve")
            sy.dma_start(out=wst_s[0:64, :],
                         in_=w["wdt"][:, hc * 1024:(hc + 1) * 1024]).inc("dma", 16)
            ve.wait("dma")
            ve.tensor_copy(out=wdt_sb[:, hc * 1024:(hc + 1) * 1024],
                           in_=wst_s[0:64, :]).inc("ve")
        ctx.te.wait("ve")
        wsts_es.close()

        with ExitStack() as axh:
          xhat = [axh.enter_context(nc.sbuf_tensor(f"mb_xhat{i}", [128, T], BF16))
                  for i in range(8)]
          with ExitStack() as ln_es:
            emit_ln(ctx, ct, x_hbm, xhat,
                    lambda n, s, d: ln_es.enter_context(nc.sbuf_tensor(n, s, d)),
                    lambda n, s, d: ln_es.enter_context(nc.psum_tensor(n, s, d)))

          for s in range(SEQ):
            ts0 = s * L
            if True:
              with ExitStack() as pa:
                  def sba(n, sh, d):
                      return pa.enter_context(nc.sbuf_tensor(f"{n}_s{s}", sh, d))
                  def pba(n, sh, d):
                      return pa.enter_context(nc.psum_tensor(f"{n}_s{s}", sh, d))
                  win_sb = [sba(f"mb_win{i}", [128, 4096], BF16) for i in range(8)]
                  wst = sba("mb_wst", [128, 1024], F32)
                  xs_t = sba("mb_xs", [128, L], F32)
                  acc = sba("mb_acc", [128, L], F32)
                  u_t = sba("mb_u", [128, L], BF16)
                  sil_t = sba("mb_sil", [128, 512], BF16)
                  dt_acc = sba("mb_dtacc", [64, L], F32)
                  b_acc = sba("mb_bacc", [16, L], F32)
                  c_acc = sba("mb_cacc", [16, L], F32)
                  dt_bf = sba("mb_dtbf", [64, L], BF16)
                  bb_bf = sba("mb_bbbf", [16, L], BF16)
                  cc_bf = sba("mb_ccbf", [16, L], BF16)
                  p_xz = pba("mb_p_xz", [128, 512], F32)
                  p_d = pba("mb_p_d", [64, 512], F32)
                  p_b = pba("mb_p_b", [16, 512], F32)
                  p_c = pba("mb_p_c", [16, 512], F32)

                  for i in range(8):
                      for hc in range(4):
                          sy.wait("ve")
                          sy.dma_start(out=wst[:],
                                       in_=w["win"][i * 128:(i + 1) * 128,
                                                    hc * 1024:(hc + 1) * 1024]).inc("dma", 16)
                          ve.wait("dma")
                          ve.wait("te")
                          ve.tensor_copy(out=win_sb[i][:, hc * 1024:(hc + 1) * 1024],
                                         in_=wst[:]).inc("ve")
                  gp.wait("ve")
                  gp.wait("se")
                  gp.wait("te")
                  gp.wait("dma")
                  gp.memset(dt_acc[:], 0.0)
                  gp.memset(b_acc[:], 0.0)
                  gp.memset(c_acc[:], 0.0).inc("gp")
                  ve.wait("gp")
                  wrdy = ctx.cnt["ve"]
                  te.wait("ve", wrdy)

                  for mt in range(32):
                      is_xs = mt < 16
                      for tc in range(4):
                          tl = slice(tc * 512, (tc + 1) * 512)
                          tg = slice(ts0 + tc * 512, ts0 + (tc + 1) * 512)
                          te.wait("se")
                          for kt in range(8):
                              mi = te.matmul(p_xz[:], win_sb[kt][:, mt * 128:(mt + 1) * 128],
                                             xhat[kt][:, tg], start=(kt == 0), stop=(kt == 7))
                          mi.inc("te")
                          sc.wait("te")
                          if is_xs:
                              sc.wait("ve")
                              sc.copy(out=xs_t[:, tl], in_=p_xz[:]).inc("se")
                          else:
                              sc.activation(out=sil_t[:], in_=p_xz[:],
                                            func=AF.Silu).inc("se")
                              sy.wait("se")
                              sy.dma_start(out=scr["sil"][(mt - 16) * 128:(mt - 15) * 128,
                                                          ts0 + tc * 512: ts0 + (tc + 1) * 512],
                                           in_=sil_t[:]).inc("dst", 16)
                              sc.wait("dst")
                      if is_xs:
                          dt2 = mt
                          ve.wait("se")
                          ve.tensor_scalar_mul(out=acc[:], in0=xs_t[:],
                                               scalar1=cols[:, dt2 * 4 + 3: dt2 * 4 + 4])
                          for k in (1, 2, 3):
                              ve.scalar_tensor_tensor(out=acc[:, k:L], in0=xs_t[:, 0:L - k],
                                                      scalar=cols[:, dt2 * 4 + 3 - k: dt2 * 4 + 4 - k],
                                                      in1=acc[:, k:L], op0=ALU.mult,
                                                      op1=ALU.add)
                          ve.nop()
                          ctx.plan["vector"].pop()
                          ctx.plan["vector"][-1][3] = None
                          vi = RecInst(ctx, ctx.plan["vector"][-1])
                          vi.inc("ve")
                          sc.wait("ve")
                          sc.activation(out=u_t[:], in_=acc[:], func=AF.Silu,
                                        bias=cols[:, 64 + dt2: 65 + dt2]).inc("se")
                          sy.wait("se")
                          sy.dma_start(out=scr["u"][dt2 * 128:(dt2 + 1) * 128,
                                                    ts0:ts0 + L], in_=u_t[:]).inc("dst", 16)
                          te.wait("se")
                          for tc in range(4):
                              tl = slice(tc * 512, (tc + 1) * 512)
                              te.wait("ve")
                              te.matmul(p_d[:], wxdt[dt2][:], u_t[:, tl], start=True, stop=True)
                              te.matmul(p_b[:], wxB[dt2][:], u_t[:, tl], start=True, stop=True)
                              te.matmul(p_c[:], wxC[dt2][:], u_t[:, tl], start=True,
                                        stop=True).inc("te")
                              ve.wait("te")
                              ve.tensor_add(out=dt_acc[:, tl], in0=dt_acc[:, tl], in1=p_d[:])
                              ve.tensor_add(out=b_acc[:, tl], in0=b_acc[:, tl], in1=p_b[:])
                              ve.tensor_add(out=c_acc[:, tl], in0=c_acc[:, tl],
                                            in1=p_c[:]).inc("ve")
                          sy.wait("te")
                      if "A" in PAR:
                          ctx.fence()
                  ve.tensor_copy(out=dt_bf[:], in_=dt_acc[:])
                  ve.tensor_copy(out=bb_bf[:], in_=b_acc[:])
                  ve.tensor_copy(out=cc_bf[:], in_=c_acc[:]).inc("ve")
                  sy.wait("ve")
                  sy.dma_start(out=scr["dtb"][0:64, ts0:ts0 + L], in_=dt_bf[:]).inc("dst", 16)
                  sy.dma_start(out=scr["bc"][0:16, ts0:ts0 + L], in_=bb_bf[:]).inc("dst", 16)
                  sy.dma_start(out=scr["bc"][16:32, ts0:ts0 + L], in_=cc_bf[:]).inc("dst", 16)
                  ctx.fence()

        for s in range(SEQ):
            ts0 = s * L
            with ExitStack() as pbs:
                def sbb(n, sh, d):
                    return pbs.enter_context(nc.sbuf_tensor(f"{n}_s{s}", sh, d))
                def pbb(n, sh, d):
                    return pbs.enter_context(nc.psum_tensor(f"{n}_s{s}", sh, d))
                bb_sb = sbb("mb_bb", [16, L], BF16)
                cc_sb = sbb("mb_cc", [16, L], BF16)
                dt_bf2 = sbb("mb_dtbf2", [64, L], BF16)
                brep = [sbb(f"mb_brep{i}", [128, L], BF16) for i in range(16)]
                crep = [sbb(f"mb_crep{i}", [128, L], BF16) for i in range(16)]
                delta = sbb("mb_delta", [128, L], F32)
                sabs = sbb("mb_sabs", [128, 512], F32)
                du = sbb("mb_du", [128, L], BF16)
                dA = [sbb("mb_dA0", [128, L], BF16), sbb("mb_dA1", [128, L], BF16)]
                Xs = sbb("mb_X", [128, L], BF16)
                yp = sbb("mb_yp", [128, L], BF16)
                yacc = sbb("mb_yacc", [128, L], F32)
                u_t2 = sbb("mb_u2", [128, L], BF16)
                sil2 = sbb("mb_sil2", [128, L], BF16)
                p_rep = pbb("mb_p_rep", [128, 512], F32)
                p_de = pbb("mb_p_de", [128, 512], F32)

                sy.dma_start(out=bb_sb[:], in_=scr["bc"][0:16, ts0:ts0 + L]).inc("dma", 16)
                sy.dma_start(out=cc_sb[:], in_=scr["bc"][16:32, ts0:ts0 + L]).inc("dma", 16)
                sy.dma_start(out=dt_bf2[:], in_=scr["dtb"][0:64, ts0:ts0 + L]).inc("dma", 16)
                te.wait("dma")
                for n in range(16):
                    if "R" in PAR:
                        ctx.fence()
                    for tc in range(4):
                        tl = slice(tc * 512, (tc + 1) * 512)
                        te.wait("se")
                        te.matmul(p_rep[:], sel16[:, n * 128:(n + 1) * 128],
                                  bb_sb[:, tl], start=True, stop=True).inc("te")
                        sc.wait("te")
                        sc.copy(out=brep[n][:, tl], in_=p_rep[:]).inc("se")
                        te.wait("se")
                        te.matmul(p_rep[:], sel16[:, n * 128:(n + 1) * 128],
                                  cc_sb[:, tl], start=True, stop=True).inc("te")
                        sc.wait("te")
                        sc.copy(out=crep[n][:, tl], in_=p_rep[:]).inc("se")

                for dt2 in range(16):
                    bcol = cols[:, 80 + dt2:81 + dt2]
                    sy.wait("ve")
                    sy.dma_start(out=u_t2[:], in_=scr["u"][dt2 * 128:(dt2 + 1) * 128,
                                                           ts0:ts0 + L]).inc("dma", 16)
                    sy.dma_start(out=sil2[:], in_=scr["sil"][dt2 * 128:(dt2 + 1) * 128,
                                                             ts0:ts0 + L]).inc("dma", 16)
                    for tc in range(4):
                        tl = slice(tc * 512, (tc + 1) * 512)
                        te.wait("ve")
                        te.matmul(p_de[:], wdt_sb[:, dt2 * 128:(dt2 + 1) * 128],
                                  dt_bf2[:, tl], start=True, stop=True).inc("te")
                        sc.wait("te")
                        sc.activation(out=sabs[:], in_=p_de[:], func=AF.Abs, bias=bcol)
                        sc.activation(out=sabs[:], in_=sabs[:], func=AF.Exp, scale=-1.0)
                        sc.activation(out=sabs[:], in_=sabs[:], func=AF.Ln, bias=ct["ones_f"][:])
                        sc.activation(out=delta[:, tl], in_=p_de[:], func=AF.Relu,
                                      bias=bcol).inc("se")
                        ve.wait("se")
                        ve.tensor_add(out=delta[:, tl], in0=delta[:, tl],
                                      in1=sabs[:]).inc("ve")
                    ve.wait("dma")
                    ve.tensor_mul(out=du[:], in0=delta[:], in1=u_t2[:]).inc("ve")
                    scan_done = [0, 0]
                    exp_done = [0, 0]
                    gp_done = 0
                    for n in range(16):
                        buf = n % 2
                        acol = a_sb[:, dt2 * 16 + n: dt2 * 16 + n + 1]
                        sc.wait("ve", scan_done[buf])
                        if n == 0:
                            sc.wait("ve")
                        exp_done[buf] = sc.activation(out=dA[buf][:], in_=delta[:],
                                                      func=AF.Exp, scale=acol).inc("se")
                        ve.wait("se", exp_done[buf])
                        ve.wait("gp", gp_done)
                        ve.tensor_mul(out=Xs[:], in0=du[:], in1=brep[n][:])
                        ve.tensor_tensor_scan(out=Xs[:], data0=dA[buf][:], data1=Xs[:],
                                              initial=0.0, op0=ALU.mult, op1=ALU.add)
                        scan_done[buf] = ve.tensor_mul(out=yp[:], in0=Xs[:],
                                                       in1=crep[n][:]).inc("ve")
                        gp.wait("ve", scan_done[buf])
                        if n == 0:
                            gp.wait("dst")
                            gp_done = gp.tensor_copy(out=yacc[:], in_=yp[:]).inc("gp")
                        else:
                            gp_done = gp.tensor_add(out=yacc[:], in0=yacc[:],
                                                    in1=yp[:]).inc("gp")
                    ve.wait("gp", gp_done)
                    ve.scalar_tensor_tensor(out=yacc[:], in0=u_t2[:],
                                            scalar=cols[:, 96 + dt2:97 + dt2],
                                            in1=yacc[:], op0=ALU.mult, op1=ALU.add)
                    ve.tensor_mul(out=yacc[:], in0=yacc[:], in1=sil2[:]).inc("ve")
                    gp.wait("ve")
                    gp.dma_start(out=scr["y3"][dt2 * 128:(dt2 + 1) * 128, ts0:ts0 + L],
                                 in_=yacc[:]).inc("dst", 16)
                    ve.wait("dst")
                    if "B" in PAR:
                        ctx.fence()
                ctx.fence()

            with ExitStack() as pc:
                def sbc(n, sh, d):
                    return pc.enter_context(nc.sbuf_tensor(f"{n}_s{s}", sh, d))
                def pbc(n, sh, d):
                    return pc.enter_context(nc.psum_tensor(f"{n}_s{s}", sh, d))
                wout_sb = [sbc(f"mb_wo{i}", [128, 1024], BF16) for i in range(16)]
                wst2 = sbc("mb_wst2", [128, 1024], F32)
                y3c = [sbc(f"mb_y3c{i}", [128, 512], BF16) for i in range(16)]
                xt = sbc("mb_xt", [128, 512], F32)
                yt = sbc("mb_yt", [128, 512], F32)
                p_y = pbc("mb_p_y", [128, 512], F32)
                for i in range(16):
                    sy.wait("ve")
                    sy.dma_start(out=wst2[:], in_=w["wout"][i * 128:(i + 1) * 128, :]).inc("dma", 16)
                    ve.wait("dma")
                    ve.wait("te")
                    ve.tensor_copy(out=wout_sb[i][:], in_=wst2[:]).inc("ve")
                for tc in range(4):
                    tg = slice(ts0 + tc * 512, ts0 + (tc + 1) * 512)
                    te.wait("te")
                    sy.wait("te")
                    for i in range(16):
                        sy.dma_start(out=y3c[i][:], in_=scr["y3"][i * 128:(i + 1) * 128,
                                                                  tg]).inc("dma", 16)
                    for jc in range(8):
                        jsl = slice(jc * 128, (jc + 1) * 128)
                        sy.wait("ve")
                        sy.dma_start(out=xt[:], in_=x_hbm[jsl, tg]).inc("dma", 16)
                        te.wait("dma")
                        te.wait("ve")
                        for i in range(16):
                            mi = te.matmul(p_y[:], wout_sb[i][:, jsl], y3c[i][:],
                                           start=(i == 0), stop=(i == 15))
                        mi.inc("te")
                        ve.wait("te")
                        ve.wait("dma")
                        ve.tensor_add(out=yt[:], in0=xt[:], in1=p_y[:]).inc("ve")
                        sy.wait("ve")
                        sy.dma_start(out=y_hbm[jsl, tg], in_=yt[:]).inc("dst", 16)
                        ve.wait("dst")
                        if "C" in PAR:
                            ctx.fence()
                ctx.fence()


def build_mamba(debug=False):
    nc = bass.Bass("TRN2", target_bir_lowering=False)
    ctx = Ctx(nc)
    ctx.consts = declare_consts(nc)
    x_hbm = nc.dram_tensor("x", [D, T], F32, kind="ExternalInput").ap()
    w = {}
    w["win"] = nc.dram_tensor("win", [1024, 4096], F32, kind="ExternalInput").ap()
    w["convw"] = nc.dram_tensor("convw", [128, 64], F32, kind="ExternalInput").ap()
    w["convb"] = nc.dram_tensor("convb", [128, 16], F32, kind="ExternalInput").ap()
    w["wxdt"] = nc.dram_tensor("wxdt", [2048, 64], F32, kind="ExternalInput").ap()
    w["wxB"] = nc.dram_tensor("wxB", [2048, 16], F32, kind="ExternalInput").ap()
    w["wxC"] = nc.dram_tensor("wxC", [2048, 16], F32, kind="ExternalInput").ap()
    w["wdt"] = nc.dram_tensor("wdt", [64, 2048], F32, kind="ExternalInput").ap()
    w["bdt"] = nc.dram_tensor("bdt", [128, 16], F32, kind="ExternalInput").ap()
    w["acols"] = nc.dram_tensor("acols", [128, 256], F32, kind="ExternalInput").ap()
    w["dcol"] = nc.dram_tensor("dcol", [128, 16], F32, kind="ExternalInput").ap()
    w["wout"] = nc.dram_tensor("wout", [2048, 1024], F32, kind="ExternalInput").ap()
    w["sel16"] = nc.dram_tensor("sel16", [16, 2048], BF16, kind="ExternalInput").ap()
    y_hbm = nc.dram_tensor("y", [D, T], F32, kind="ExternalOutput").ap()
    scr = {}
    kind = "ExternalOutput" if debug else "Internal"
    scr["u"] = nc.dram_tensor("scr_u", [2048, T], BF16, kind=kind).ap()
    scr["sil"] = nc.dram_tensor("scr_sil", [2048, T], BF16, kind=kind).ap()
    scr["y3"] = nc.dram_tensor("scr_y3", [2048, T], BF16, kind=kind).ap()
    scr["dtb"] = nc.dram_tensor("scr_dtb", [64, T], BF16, kind=kind).ap()
    scr["bc"] = nc.dram_tensor("scr_bc", [32, T], BF16, kind=kind).ap()

    from contextlib import ExitStack
    with ExitStack() as es:
        def sb(name, shape, dtype):
            return es.enter_context(nc.sbuf_tensor(name, shape, dtype))
        ct = load_consts(ctx, sb)
        emit_mamba(ctx, ct, x_hbm, w, y_hbm, scr)
    ctx.sy.wait("dma")
    ctx.sy.wait("dst")
    ctx.emit()
    return nc


def mamba_host_inputs(inputs):
    m = {}
    m["win"] = np.ascontiguousarray(np.asarray(inputs["L0_attn_b_W_in"], np.float32))
    cw = np.asarray(inputs["L0_attn_b_conv_w"], np.float32)
    m["convw"] = np.ascontiguousarray(cw.reshape(16, 128, 4).transpose(1, 0, 2).reshape(128, 64))
    m["convb"] = np.ascontiguousarray(np.asarray(inputs["L0_attn_b_conv_b"], np.float32).reshape(16, 128).T)
    wx = np.asarray(inputs["L0_attn_b_W_x"], np.float32)
    m["wxdt"] = np.ascontiguousarray(wx[:, 0:64])
    m["wxB"] = np.ascontiguousarray(wx[:, 64:80])
    m["wxC"] = np.ascontiguousarray(wx[:, 80:96])
    m["wdt"] = np.ascontiguousarray(np.asarray(inputs["L0_attn_b_W_dt"], np.float32))
    m["bdt"] = np.ascontiguousarray(np.asarray(inputs["L0_attn_b_b_dt"], np.float32).reshape(16, 128).T)
    m["acols"] = np.ascontiguousarray(np.asarray(inputs["L0_attn_b_A_log"], np.float32)
                                      .reshape(16, 128, 16).transpose(1, 0, 2).reshape(128, 256))
    m["dcol"] = np.ascontiguousarray(np.asarray(inputs["L0_attn_b_D"], np.float32).reshape(16, 128).T)
    m["wout"] = np.ascontiguousarray(np.asarray(inputs["L0_attn_b_W_out"], np.float32))
    sel16 = np.zeros((16, 2048), BF16NP)
    for n in range(16):
        sel16[n, n * 128:(n + 1) * 128] = 1.0
    m["sel16"] = sel16
    return m



def emit_l1a(ctx, ct, x_hbm, w, y_hbm):
    from contextlib import ExitStack
    nc = ctx.nc
    sy, te, ve, sc, gp = ctx.sy, ctx.te, ctx.ve, ctx.sc, ctx.gp
    WS = (4.0, 16.0, 64.0, 256.0)
    ESC = 2.0 / float(np.sqrt(2048.0))

    with ExitStack() as es:
        def sb(name, shape, dtype):
            return es.enter_context(nc.sbuf_tensor(name, shape, dtype))

        xhat = [sb(f"a1_xhat{i}", [128, T], BF16) for i in range(8)]
        with ExitStack() as ln_es:
            emit_ln(ctx, ct, x_hbm, xhat,
                    lambda n, s, d: ln_es.enter_context(nc.sbuf_tensor(n, s, d)),
                    lambda n, s, d: ln_es.enter_context(nc.psum_tensor(n, s, d)))

        r3 = sb("a1_r3", [128, 4096], F32)
        sy.dma_start(out=r3[:], in_=w["r3"]).inc("dma", 16)
        bvrow = sb("a1_bvrow", [1, 3072], BF16)
        with nc.sbuf_tensor("a1_bstage", [1, 3072], F32) as bstage:
            sy.dma_start(out=bstage[:], in_=w["qkvb_row"]).inc("dma", 16)
            ve.wait("dma")
            ve.tensor_copy(out=bvrow[:], in_=bstage[:]).inc("ve")
            ve.wait("ve")
        qkvb = sb("a1_qkvb", [128, 24], F32)
        sy.dma_start(out=qkvb[:], in_=w["qkvb_cols"]).inc("dma", 16)
        dx0 = [sb(f"a1_dx0_{s}", [1, L], F32) for s in range(SEQ)]

        def pb(name, shape, dtype):
            return es.enter_context(nc.psum_tensor(name, shape, dtype))
        p_qv = pb("a1_p_qv", [128, 512], F32)
        p_s = pb("a1_p_s", [128, 512], F32)
        p_oa = pb("a1_p_oa", [128, 1024], F32)
        p_ob = pb("a1_p_ob", [128, 1024], F32)
        p_nd = pb("a1_p_nd", [1, 1024], F32)

        with ExitStack() as dxs:
            def sbd(n, sh, d):
                return dxs.enter_context(nc.sbuf_tensor(n, sh, d))
            offw_sb = sbd("a1_offw", [128, 2048], BF16)
            wst = sbd("a1_wst", [128, 2048], F32)
            lin2_sb = sbd("a1_lin2", [128, 2], BF16)
            rt = sbd("a1_rt", [128, 512], BF16)
            z1 = sbd("a1_z1", [128, L], BF16)
            z1b = sbd("a1_z1b", [128, L], BF16)
            lb = sbd("a1_lb", [1, 1], F32)
            sy.dma_start(out=lb[:], in_=w["lin2_b"]).inc("dma", 16)
            src = bass.AP(w["offwT"].tensor, 0, [[256, 128], [128 * 256, 8], [1, 256]])
            sy.dma_start(out=wst[:], in_=src).inc("dma", 16)
            ve.wait("dma")
            ve.tensor_copy(out=offw_sb[:], in_=wst[:]).inc("ve")
            sy.wait("ve")
            sy.dma_start(out=wst[0:128, 0:2], in_=w["lin2wT"]).inc("dma", 16)
            ve.wait("dma")
            ve.tensor_copy(out=lin2_sb[:], in_=wst[0:128, 0:2]).inc("ve")
            te.wait("ve")
            for s in range(SEQ):
                ts0 = s * L
                for mh in range(2):
                    dst = z1 if mh == 0 else z1b
                    for tc in range(4):
                        tg = slice(ts0 + tc * 512, ts0 + (tc + 1) * 512)
                        tl = slice(tc * 512, (tc + 1) * 512)
                        for kt in range(8):
                            sc.wait("te")
                            sc.activation(out=rt[:], in_=xhat[kt][:, tg],
                                          func=AF.Relu).inc("se")
                            te.wait("se")
                            te.matmul(p_qv[:], offw_sb[:, kt * 256 + mh * 128:
                                                       kt * 256 + (mh + 1) * 128],
                                      rt[:], start=(kt == 0),
                                      stop=(kt == 7)).inc("te")
                        sc.wait("te")
                        sc.copy(out=dst[:, tl], in_=p_qv[:]).inc("se")
                te.wait("se")
                for tc in range(4):
                    tl = slice(tc * 512, (tc + 1) * 512)
                    te.wait("se")
                    te.matmul(p_nd[:, 0:512], lin2_sb[:, 0:1], z1[:, tl],
                              start=True, stop=False)
                    te.matmul(p_nd[:, 0:512], lin2_sb[:, 1:2], z1b[:, tl],
                              start=False, stop=True).inc("te")
                    sc.wait("te")
                    sc.activation(out=dx0[s][0:1, tl], in_=p_nd[0:1, 0:512],
                                  func=AF.Sigmoid, bias=lb[:]).inc("se")
                ve.wait("se")
                ve.tensor_scalar_mul(out=dx0[s][:], in0=dx0[s][:],
                                     scalar1=float(L)).inc("ve")
            ctx.fence()

        wstage = sb("a1_wstage", [128, 2048], F32)
        wq_sb = sb("a1_wq", [128, 2048], BF16)
        wk_sb = sb("a1_wk", [128, 2048], BF16)
        wv_sb = sb("a1_wv", [128, 2048], BF16)
        qop = [sb(f"a1_qop{i}", [128, L], BF16) for i in range(2)]
        kop = [sb(f"a1_kop{i}", [128, L], BF16) for i in range(2)]
        vop_all = sb("a1_vop", [128, 16 * 256], BF16)
        negc = sb("a1_negc", [1, L], BF16)
        sqn = sb("a1_sqn", [128, L], BF16)
        sqn2 = sb("a1_sqn2", [128, L], BF16)
        n_k = sb("a1_nk", [1, L], F32)
        b2 = sb("a1_b2", [1, 1], F32)
        dxrep = sb("a1_dxrep", [128, L], F32)
        attnT = sb("a1_attnT", [128, 1024], BF16)
        mskt = sb("a1_mskt", [128, 1024], BF16)
        drep = sb("a1_drep", [128, 1024], F32)
        xt2 = sb("a1_xt2", [128, 1024], F32)
        yt2 = sb("a1_yt2", [128, 1024], F32)
        vop = [vop_all[:, tt * 256:(tt + 1) * 256] for tt in range(16)]

        for g in range(4):
            for a, wdst in ((0, wq_sb), (1, wk_sb), (2, wv_sb)):
                coff = a * 1024 + g * 256
                sy.wait("ve")
                src = bass.AP(w["qkvwT"].tensor, coff,
                              [[3072, 128], [128 * 3072, 8], [1, 256]])
                sy.dma_start(out=wstage[:], in_=src).inc("dma", 16)
                ve.wait("dma")
                ve.wait("te")
                ve.tensor_copy(out=wdst[:], in_=wstage[:]).inc("ve")
            wrdy = ctx.cnt["ve"]
            for s in range(SEQ):
                ts0 = s * L
                te.wait("ve", wrdy)
                for mh in range(2):
                    bq = qkvb[:, g * 2 + mh: g * 2 + mh + 1]
                    bk = qkvb[:, 8 + g * 2 + mh: 8 + g * 2 + mh + 1]
                    for tc in range(4):
                        tg = slice(ts0 + tc * 512, ts0 + (tc + 1) * 512)
                        tl = slice(tc * 512, (tc + 1) * 512)
                        te.wait("se")
                        for kt in range(8):
                            mi = te.matmul(p_qv[:], wq_sb[:, kt * 256 + mh * 128:
                                                          kt * 256 + (mh + 1) * 128],
                                           xhat[kt][:, tg], start=(kt == 0), stop=(kt == 7))
                        mi.inc("te")
                        ve.wait("te")
                        ve.tensor_scalar_add(out=qop[mh][:, tl], in0=p_qv[:],
                                             scalar1=bq).inc("ve")
                        te.wait("ve")
                        for kt in range(8):
                            mi = te.matmul(p_qv[:], wk_sb[:, kt * 256 + mh * 128:
                                                          kt * 256 + (mh + 1) * 128],
                                           xhat[kt][:, tg], start=(kt == 0), stop=(kt == 7))
                        mi.inc("te")
                        ve.wait("te")
                        ve.tensor_scalar_add(out=kop[mh][:, tl], in0=p_qv[:],
                                             scalar1=bk).inc("ve")
                        te.wait("ve")
                for tt in range(16):
                    tsl = slice(ts0 + tt * 128, ts0 + (tt + 1) * 128)
                    for mh in range(2):
                        for kt in range(8):
                            te.matmul(p_qv[:, 0:128], xhat[kt][:, tsl],
                                      wv_sb[:, kt * 256 + mh * 128: kt * 256 + (mh + 1) * 128],
                                      start=(kt == 0), stop=False)
                        mi = te.matmul(p_qv[:, 0:128], ct["onesrow_b"][0:1, 0:128],
                                       bvrow[0:1, 2048 + g * 256 + mh * 128:
                                             2048 + g * 256 + (mh + 1) * 128],
                                       start=False, stop=True)
                        mi.inc("te")
                        sc.wait("te")
                        sc.copy(out=vop[tt][:, mh * 128:(mh + 1) * 128],
                                in_=p_qv[:, 0:128]).inc("se")
                        te.wait("se")
                sc.wait("ve")
                sc.activation(out=sqn[:], in_=kop[0][:], func=AF.Square)
                sc.activation(out=sqn2[:], in_=kop[1][:], func=AF.Square).inc("se")
                te.wait("se")
                for qc in range(4):
                    qsl = slice(qc * 512, (qc + 1) * 512)
                    te.wait("ve")
                    te.matmul(p_nd[:, 0:512], ct["ones_b"][0:128, :], sqn[:, qsl],
                              start=True, stop=False)
                    te.matmul(p_nd[:, 0:512], ct["ones_b"][0:128, :], sqn2[:, qsl],
                              start=False, stop=True).inc("te")
                    ve.wait("te")
                    ve.tensor_copy(out=n_k[0:1, qsl], in_=p_nd[0:1, 0:512]).inc("ve")
                ctx.vsep()
                ve.reduce_max(out=b2[:], in_=n_k[:], axis=AX.X).inc("ve")
                sc.wait("te")
                sc.wait("ve")
                sc.activation(out=sqn[:], in_=qop[0][:], func=AF.Square)
                sc.activation(out=sqn2[:], in_=qop[1][:], func=AF.Square).inc("se")
                te.wait("se")
                te.wait("ve")
                for qc in range(4):
                    qsl = slice(qc * 512, (qc + 1) * 512)
                    te.wait("se")
                    te.matmul(p_nd[:, 0:512], ct["ones_b"][0:128, :], sqn[:, qsl],
                              start=True, stop=False)
                    te.matmul(p_nd[:, 0:512], ct["ones_b"][0:128, :], sqn2[:, qsl],
                              start=False, stop=True).inc("te")
                    sc.wait("te")
                    sc.wait("ve")
                    sc.activation(out=n_k[0:1, qsl], in_=p_nd[0:1, 0:512],
                                  func=AF.Sqrt, scale=b2[:]).inc("se")
                sc.mul(out=negc[:], in_=n_k[:], mul=-1.0).inc("se")
                ve.wait("se")
                ve.tensor_scalar_add(out=n_k[:], in0=dx0[s][:],
                                     scalar1=WS[g]).inc("ve")
                te.wait("ve")
                for qc in range(4):
                    qsl = slice(qc * 512, (qc + 1) * 512)
                    te.wait("se")
                    te.matmul(p_qv[:], ct["onesrow_f"][0:1, :], n_k[0:1, qsl],
                              start=True, stop=True).inc("te")
                    sc.wait("te")
                    sc.copy(out=dxrep[:, qsl], in_=p_qv[:]).inc("se")
                if getattr(ctx, "dbg", None) is not None and g == 0 and s == 0:
                    sy.wait("se")
                    sy.dma_start(out=ctx.dbg["dxrep"], in_=dxrep[:]).inc("dma", 16)
                    sy.dma_start(out=ctx.dbg["negc"], in_=negc[:]).inc("dma", 16)
                    sy.dma_start(out=ctx.dbg["dx0"], in_=dx0[0][:]).inc("dma", 16)
                for qh in range(2):
                    qof = qh * 1024
                    for kt in range(16):
                        te.wait("se")
                        te.wait("ve")
                        for qq in range(2):
                            qsl = slice(qof + qq * 512, qof + (qq + 1) * 512)
                            te.matmul(p_s[:], kop[0][:, kt * 128:(kt + 1) * 128],
                                      qop[0][:, qsl], start=True, stop=False)
                            te.matmul(p_s[:], kop[1][:, kt * 128:(kt + 1) * 128],
                                      qop[1][:, qsl], start=False, stop=False)
                            mi = te.matmul(p_s[:], ct["onesrow_b"][0:1, 0:128],
                                           negc[0:1, qsl], start=False, stop=True)
                            mi.inc("te")
                            sc.wait("te")
                            sc.activation(out=attnT[:, qq * 512:(qq + 1) * 512],
                                          in_=p_s[:], func=AF.Exp,
                                          scale=ESC).inc("se")
                            te.wait("se")
                        ve.wait("se")
                        c0 = 2048 - kt * 128 + qof
                        ve.tensor_tensor(out=mskt[:], in0=r3[:, c0:c0 + 1024],
                                         in1=dxrep[:, qof:qof + 1024], op=ALU.is_le)
                        if getattr(ctx, "dbg", None) is not None and g == 0 and s == 0 and qh == 0 and kt == 0:
                            ve.tensor_copy(out=ctx.dbg_msk_sb[:], in_=mskt[:])
                            ve.tensor_copy(out=ctx.dbg_att_sb[:], in_=attnT[:]).inc("ve")
                            sy.wait("ve")
                            sy.dma_start(out=ctx.dbg["msk"], in_=ctx.dbg_msk_sb[:]).inc("dma", 16)
                            sy.dma_start(out=ctx.dbg["att"], in_=ctx.dbg_att_sb[:]).inc("dma", 16)
                        ve.tensor_mul(out=attnT[:], in0=attnT[:], in1=mskt[:]).inc("ve")
                        te.wait("ve")
                        for qq in range(2):
                            asl = slice(qq * 512, (qq + 1) * 512)
                            psl = slice(qq * 512, (qq + 1) * 512)
                            te.matmul(p_oa[:, psl], vop[kt][:, 0:128], attnT[:, asl],
                                      start=(kt == 0), stop=(kt == 15))
                            te.matmul(p_ob[:, psl], vop[kt][:, 128:256], attnT[:, asl],
                                      start=(kt == 0), stop=(kt == 15))
                            mi = te.matmul(p_nd[:, psl], ct["ones_b"][0:128, :], attnT[:, asl],
                                           start=(kt == 0), stop=(kt == 15))
                        mi.inc("te")
                    ve.wait("te")
                    if getattr(ctx, "dbg", None) is not None and g == 0 and s == 0 and qh == 0:
                        ve.tensor_copy(out=ctx.dbg_msk_sb[0:1, :], in_=p_nd[:]).inc("ve")
                        sy.wait("ve")
                        sy.dma_start(out=ctx.dbg["den"], in_=ctx.dbg_msk_sb[0:1, :]).inc("dma", 16)
                    ve.reciprocal(out=n_k[0:1, 0:1024], in_=p_nd[:]).inc("ve")
                    te.wait("ve")
                    for qq in range(2):
                        qsl = slice(qq * 512, (qq + 1) * 512)
                        te.wait("se")
                        te.matmul(p_qv[:], ct["onesrow_f"][0:1, :], n_k[0:1, qsl],
                                  start=True, stop=True).inc("te")
                        sc.wait("te")
                        sc.copy(out=drep[:, qsl], in_=p_qv[:]).inc("se")
                    for half, p_o in ((0, p_oa), (1, p_ob)):
                        jrow = (g * 2 + half) * 128
                        tg = slice(ts0 + qof, ts0 + qof + 1024)
                        sy.wait("ve")
                        sy.dma_start(out=xt2[:], in_=x_hbm[jrow:jrow + 128, tg]).inc("dma", 16)
                        ve.wait("se")
                        ve.tensor_mul(out=yt2[:], in0=p_o[:], in1=drep[:])
                        ve.wait("dma")
                        ve.tensor_add(out=yt2[:], in0=yt2[:], in1=xt2[:]).inc("ve")
                        sy.wait("ve")
                        sy.dma_start(out=y_hbm[jrow:jrow + 128, tg], in_=yt2[:]).inc("dst", 16)
                        ve.wait("dst")
                    te.wait("ve")
        ctx.fence()


def build_l1a(debug=False):
    nc = bass.Bass("TRN2", target_bir_lowering=False)
    ctx = Ctx(nc)
    ctx.consts = declare_consts(nc)
    if debug:
        ctx.dbg = {
            "msk": nc.dram_tensor("dbg_msk", [128, 1024], F32, kind="ExternalOutput").ap(),
            "att": nc.dram_tensor("dbg_att", [128, 1024], F32, kind="ExternalOutput").ap(),
            "dxrep": nc.dram_tensor("dbg_dxrep", [128, L], F32, kind="ExternalOutput").ap(),
            "negc": nc.dram_tensor("dbg_negc", [1, L], BF16, kind="ExternalOutput").ap(),
            "dx0": nc.dram_tensor("dbg_dx0", [1, L], F32, kind="ExternalOutput").ap(),
            "den": nc.dram_tensor("dbg_den", [1, 1024], F32, kind="ExternalOutput").ap(),
        }
    else:
        ctx.dbg = None
    x_hbm = nc.dram_tensor("x", [D, T], F32, kind="ExternalInput").ap()
    w = {}
    w["qkvwT"] = nc.dram_tensor("qkvwT", [1024, 3072], F32, kind="ExternalInput").ap()
    w["qkvb_row"] = nc.dram_tensor("qkvb_row", [1, 3072], F32, kind="ExternalInput").ap()
    w["qkvb_cols"] = nc.dram_tensor("qkvb_cols", [128, 24], F32, kind="ExternalInput").ap()
    w["offwT"] = nc.dram_tensor("offwT", [1024, 256], F32, kind="ExternalInput").ap()
    w["lin2wT"] = nc.dram_tensor("lin2wT", [128, 2], F32, kind="ExternalInput").ap()
    w["lin2_b"] = nc.dram_tensor("lin2_b", [1, 1], F32, kind="ExternalInput").ap()
    w["r3"] = nc.dram_tensor("r3", [128, 4096], F32, kind="ExternalInput").ap()
    y_hbm = nc.dram_tensor("y", [D, T], F32, kind="ExternalOutput").ap()

    from contextlib import ExitStack
    with ExitStack() as es:
        def sb(name, shape, dtype):
            return es.enter_context(nc.sbuf_tensor(name, shape, dtype))
        ct = load_consts(ctx, sb)
        if ctx.dbg is not None:
            ctx.dbg_msk_sb = sb("dbg_msk_sb", [128, 1024], F32)
            ctx.dbg_att_sb = sb("dbg_att_sb", [128, 1024], F32)
        emit_l1a(ctx, ct, x_hbm, w, y_hbm)
    ctx.sy.wait("dma")
    ctx.sy.wait("dst")
    ctx.emit()
    return nc


def l1a_host_inputs(inputs):
    m = {}
    m["qkvwT"] = np.ascontiguousarray(np.asarray(inputs["L1_attn_a_qkv_w"], np.float32).T)
    qb = np.asarray(inputs["L1_attn_a_qkv_b"], np.float32)
    m["qkvb_row"] = np.ascontiguousarray(qb.reshape(1, 3072))
    m["qkvb_cols"] = np.ascontiguousarray(qb.reshape(24, 128).T)
    m["offwT"] = np.ascontiguousarray(np.asarray(inputs["L1_attn_a_off_w"], np.float32).T)
    m["lin2wT"] = np.ascontiguousarray(np.asarray(inputs["L1_attn_a_lin2_w"], np.float32).reshape(2, 128).T)
    m["lin2_b"] = np.ascontiguousarray(np.asarray(inputs["L1_attn_a_lin2_b"], np.float32).reshape(1, 1))
    p = np.arange(128, dtype=np.float64)[:, None]
    c = np.arange(4096, dtype=np.float64)[None, :]
    v = p - c + 2048.0
    r3 = np.where(v < 0, 2e18, v).astype(np.float32)
    m["r3"] = r3
    return m



def emit_moe(ctx, ct, x_hbm, w, y_hbm, scr, tag="", skip_wconv=False):
    from contextlib import ExitStack
    nc = ctx.nc
    sy, te, ve, sc, gp = ctx.sy, ctx.te, ctx.ve, ctx.sc, ctx.gp

    with ExitStack() as es:
        def sb(name, shape, dtype):
            return es.enter_context(nc.sbuf_tensor(name + tag, shape, dtype))

        maskT = sb("mo_maskT", [8, T], F32)
        wg_sb = sb("mo_wg", [128, 64], BF16)
        bgrep = sb("mo_bgrep", [128, 8], F32)
        b1c = sb("mo_b1c", [128, 256], F32)
        b2e = sb("mo_b2e", [1, 1024], BF16)
        identf = sb("mo_identf", [128, 128], F32)
        onesrow512 = sb("mo_ones512", [1, 512], BF16)

        with ExitStack() as p0:
            def sb0(n, sh, d):
                return p0.enter_context(nc.sbuf_tensor(n + tag, sh, d))
            def pb0(n, sh, d):
                return p0.enter_context(nc.psum_tensor(n + tag, sh, d))
            xhat = [sb0(f"mo_xhat{i}", [128, T], BF16) for i in range(8)]
            emit_ln(ctx, ct, x_hbm, xhat, sb0, pb0)

            wst = sb0("mo_wst", [128, 4096], F32)
            wstb = sb0("mo_wstb", [128, 4096], BF16)
            lg = sb0("mo_lg", [128, 8], F32)
            Et = sb0("mo_Et", [128, 8], F32)
            mrow = sb0("mo_m", [128, 1], F32)
            negm = sb0("mo_negm", [128, 1], F32)
            srow = sb0("mo_s", [128, 1], F32)
            p_g = pb0("mo_p_g", [128, 8], F32)
            if getattr(ctx, "dbg_lg", None) is not None:
                ctx.dbg_lg_sb = sb0("dbg_lg_sb", [128, 8], F32)
                ctx.dbg_m_sb = sb0("dbg_m_sb", [128, 1], F32)
            p_tr = pb0("mo_p_tr", [128, 128], F32)

            sy.dma_start(out=identf[:], in_=w["identf"]).inc("dma", 16)
            sy.dma_start(out=onesrow512[:], in_=w["ones512b"]).inc("dma", 16)
            sy.dma_start(out=b1c[:], in_=w["b1c"]).inc("dma", 16)
            for kt in range(8):
                gp.dma_start(out=wg_sb[:, kt * 8:(kt + 1) * 8],
                             in_=w["Wg"][kt * 128:(kt + 1) * 128, :]).inc("dma", 16)
            bgrow = sb0("mo_bgrow", [1, 8], F32)
            sy.dma_start(out=bgrow[:], in_=w["bg"]).inc("dma", 16)
            te.wait("dma")
            te.matmul(p_g[:], ct["onesrow_f"][0:1, :], bgrow[:],
                      start=True, stop=True).inc("te")
            sc.wait("te")
            sc.copy(out=bgrep[:], in_=p_g[:]).inc("se")
            te.wait("se")

            for i in range(8):
                sy.wait("ve")
                sy.dma_start(out=scr["xh"][i * 128:(i + 1) * 128, :],
                             in_=xhat[i][:]).inc("dst", 16)
            for ts in range(T // 128):
                tsl = slice(ts * 128, (ts + 1) * 128)
                te.wait("ve")
                te.wait("se")
                for kt in range(8):
                    mi = te.matmul(p_g[:], xhat[kt][:, tsl], wg_sb[:, kt * 8:(kt + 1) * 8],
                                   start=(kt == 0), stop=(kt == 7))
                mi.inc("te")
                ve.wait("te")
                ve.wait("se")
                ve.tensor_add(out=lg[:], in0=p_g[:], in1=bgrep[:])
                ctx.vsep()
                ve.reduce_max(out=mrow[:], in_=lg[:], axis=AX.X)
                ctx.vsep()
                ve.tensor_scalar_mul(out=negm[:], in0=mrow[:], scalar1=-1.0).inc("ve")
                if getattr(ctx, "dbg_lg", None) is not None and ts == 0:
                    ve.tensor_copy(out=ctx.dbg_lg_sb[:], in_=lg[:])
                    ve.tensor_copy(out=ctx.dbg_m_sb[:], in_=mrow[:]).inc("ve")
                sc.wait("ve")
                sc.activation(out=Et[:], in_=lg[:], func=AF.Exp, bias=negm[:],
                              accum_out=srow[:]).inc("se")
                ve.wait("se")
                ve.reciprocal(out=srow[:], in_=srow[:])
                ctx.vsep()
                ve.tensor_scalar(out=lg[:], in0=lg[:], scalar1=mrow[:],
                                 scalar2=srow[:], op0=ALU.is_ge,
                                 op1=ALU.mult).inc("ve")
                te.wait("ve")
                te.transpose(p_tr[0:8, :], lg[:, 0:8] if False else lg[:],
                             identf[:]).inc("te")
                sc.wait("te")
                sc.copy(out=maskT[:, tsl], in_=p_tr[0:8, 0:128]).inc("se")
                te.wait("se")
                ctx.fence()
            if getattr(ctx, "dbg_maskT", None) is not None:
                sy.wait("se")
                sy.wait("ve")
                sy.dma_start(out=ctx.dbg_maskT, in_=maskT[:]).inc("dst", 16)
                sy.dma_start(out=ctx.dbg_lg, in_=ctx.dbg_lg_sb[:]).inc("dst", 16)
                sy.dma_start(out=ctx.dbg_m, in_=ctx.dbg_m_sb[:]).inc("dst", 16)
            ctx.fence()

        if not skip_wconv:
            with ExitStack() as p1:
                def sb1(n, sh, d):
                    return p1.enter_context(nc.sbuf_tensor(n + tag, sh, d))
                wst = sb1("mo_wst1", [128, 4096], F32)
                wstb = sb1("mo_wstb1", [128, 4096], BF16)
                for e in range(8):
                    for i in range(8):
                        sy.wait("ve")
                        sy.dma_start(out=wst[:], in_=w["W1"][e, i * 128:(i + 1) * 128, :]).inc("dma", 16)
                        ve.wait("dma")
                        ve.tensor_copy(out=wstb[:], in_=wst[:]).inc("ve")
                        sy.wait("ve")
                        sy.dma_start(out=scr["w1c"][(e * 8 + i) * 128:(e * 8 + i + 1) * 128, :],
                                     in_=wstb[:]).inc("dst", 16)
                    for i in range(8):
                        sy.wait("ve")
                        sy.dma_start(out=wst[:], in_=bass.AP(w["W2"].tensor,
                                     (e * 4096 + i * 512) * 1024,
                                     [[1024, 128], [128 * 1024, 4], [1, 1024]])).inc("dma", 16)
                        ve.wait("dma")
                        ve.tensor_copy(out=wstb[:], in_=wst[:]).inc("ve")
                        sy.wait("ve")
                        sy.dma_start(out=scr["w2c"][(e * 8 + i) * 128:(e * 8 + i + 1) * 128, :],
                                     in_=wstb[:]).inc("dst", 16)
                ctx.fence()

        sy.dma_start(out=y_hbm, in_=x_hbm).inc("dst", 16)
        ctx.fence()

        with ExitStack() as p3:
            def sb3(n, sh, d):
                return p3.enter_context(nc.sbuf_tensor(n + tag, sh, d))
            def pb3(n, sh, d):
                return p3.enter_context(nc.psum_tensor(n + tag, sh, d))
            w1_sb = [sb3(f"mo_w1_{i}", [128, 4096], BF16) for i in range(8)]
            w2_sb = [sb3(f"mo_w2_{i}", [128, 1024], BF16) for i in range(32)]
            xb = [sb3(f"mo_xb{i}", [128, 512], BF16) for i in range(8)]
            h_act = [sb3(f"mo_h{i}", [128, 512], BF16) for i in range(32)]
            mwrep = sb3("mo_mwrep", [128, 512], F32)
            yt = sb3("mo_yt", [128, 512], F32)
            ot = sb3("mo_ot", [128, 512], F32)
            p_h = [pb3(f"mo_p_h{i}", [128, 512], F32) for i in range(2)]
            p_o = [pb3(f"mo_p_o{i}", [128, 512], F32) for i in range(2)]
            p_mw = pb3("mo_p_mw", [128, 512], F32)

            for e in range(8):
                gp.wait("te")
                gp.dma_start(out=b2e[:],
                             in_=w["b2row"][0:1, e * 1024:(e + 1) * 1024]).inc("dma", 16)
                sy.wait("te")
                for i in range(8):
                    sy.dma_start(out=w1_sb[i][:],
                                 in_=scr["w1c"][(e * 8 + i) * 128:(e * 8 + i + 1) * 128,
                                                :]).inc("dma", 16)
                for i in range(32):
                    sy.dma_start(out=w2_sb[i][:],
                                 in_=scr["w2c"][(e * 8 + i // 4) * 128 + (i % 4) * 0:
                                                (e * 8 + i // 4) * 128 + 128,
                                                (i % 4) * 1024:(i % 4) * 1024 + 1024]
                                 ).inc("dma", 16)
                hsilu = [0, 0]
                for tblk in range(8):
                    tbl = slice(tblk * 512, (tblk + 1) * 512)
                    sy.wait("te")
                    for kt in range(8):
                        sy.dma_start(out=xb[kt][:],
                                     in_=scr["xh"][kt * 128:(kt + 1) * 128, tbl]).inc("dma", 16)
                    te.wait("dma")
                    te.wait("se")
                    te.wait("ve")
                    te.matmul(p_mw[:], w["sel8_sb"][:, e * 128:(e + 1) * 128],
                              maskT[:, tbl], start=True, stop=True).inc("te")
                    sc.wait("te")
                    sc.copy(out=mwrep[:], in_=p_mw[:]).inc("se")
                    te.wait("dma")
                    for hg in range(32):
                        pb = p_h[hg % 2]
                        if hg >= 2:
                            te.wait("se", hsilu[hg % 2])
                        for kt in range(8):
                            mi = te.matmul(pb[:], w1_sb[kt][:, hg * 128:(hg + 1) * 128],
                                           xb[kt][:], start=(kt == 0), stop=(kt == 7))
                        mi.inc("te")
                        sc.wait("te")
                        hsilu[hg % 2] = sc.activation(
                            out=h_act[hg][:], in_=pb[:], func=AF.Silu,
                            bias=b1c[:, e * 32 + hg: e * 32 + hg + 1]).inc("se")
                    te.wait("se")
                    for jt in range(8):
                        po = p_o[jt % 2]
                        te.wait("ve")
                        for i in range(32):
                            te.matmul(po[:], w2_sb[i][:, jt * 128:(jt + 1) * 128],
                                      h_act[i][:], start=(i == 0), stop=False)
                        mi = te.matmul(po[:], b2e[0:1, jt * 128:(jt + 1) * 128],
                                       onesrow512[:], start=False, stop=True)
                        mi.inc("te")
                        sy.wait("ve")
                        sy.dma_start(out=yt[:] if False else yt[:],
                                     in_=y_hbm[jt * 128:(jt + 1) * 128, tbl]).inc("dma", 16)
                        ve.wait("te")
                        ve.wait("se")
                        ve.tensor_mul(out=ot[:], in0=po[:], in1=mwrep[:]).inc("ve")
                        ve.wait("dma")
                        ve.tensor_add(out=yt[:], in0=yt[:], in1=ot[:]).inc("ve")
                        sy.wait("ve")
                        sy.dma_start(out=y_hbm[jt * 128:(jt + 1) * 128, tbl],
                                     in_=yt[:]).inc("dst", 16)
                        ve.wait("dst")
            ctx.fence()


def build_moe(debug=False):
    nc = bass.Bass("TRN2", target_bir_lowering=False)
    ctx = Ctx(nc)
    ctx.consts = declare_consts(nc)
    x_hbm = nc.dram_tensor("x", [D, T], F32, kind="ExternalInput").ap()
    w = {}
    w["Wg"] = nc.dram_tensor("Wg", [1024, 8], F32, kind="ExternalInput").ap()
    w["bg"] = nc.dram_tensor("bg", [1, 8], F32, kind="ExternalInput").ap()
    w["W1"] = nc.dram_tensor("W1", [8, 1024, 4096], F32, kind="ExternalInput").ap()
    w["b1c"] = nc.dram_tensor("b1c", [128, 256], F32, kind="ExternalInput").ap()
    w["W2"] = nc.dram_tensor("W2", [8, 4096, 1024], F32, kind="ExternalInput").ap()
    w["b2row"] = nc.dram_tensor("b2row", [1, 8192], F32, kind="ExternalInput").ap()
    w["identf"] = nc.dram_tensor("identf", [128, 128], F32, kind="ExternalInput").ap()
    w["ones512b"] = nc.dram_tensor("ones512b", [1, 512], BF16, kind="ExternalInput").ap()
    w["sel8"] = nc.dram_tensor("sel8", [8, 1024], F32, kind="ExternalInput").ap()
    y_hbm = nc.dram_tensor("y", [D, T], F32, kind="ExternalOutput").ap()
    scr = {}
    kind = "ExternalOutput" if debug else "Internal"
    scr["xh"] = nc.dram_tensor("scr_xh", [1024, T], BF16, kind=kind).ap()
    scr["w1c"] = nc.dram_tensor("scr_w1c", [8192, 4096], BF16, kind=kind).ap()
    scr["w2c"] = nc.dram_tensor("scr_w2c", [8192, 4096], BF16, kind=kind).ap()
    if debug:
        ctx.dbg_maskT = nc.dram_tensor("dbg_maskT", [8, T], F32, kind="ExternalOutput").ap()
        ctx.dbg_lg = nc.dram_tensor("dbg_lg", [128, 8], F32, kind="ExternalOutput").ap()
        ctx.dbg_m = nc.dram_tensor("dbg_m", [128, 1], F32, kind="ExternalOutput").ap()

    from contextlib import ExitStack
    with ExitStack() as es:
        def sb(name, shape, dtype):
            return es.enter_context(nc.sbuf_tensor(name, shape, dtype))
        ct = load_consts(ctx, sb)
        sel8_sb = sb("mo_sel8", [8, 1024], F32)
        ctx.sy.dma_start(out=sel8_sb[:], in_=w["sel8"]).inc("dma", 16)
        w["sel8_sb"] = sel8_sb
        emit_moe(ctx, ct, x_hbm, w, y_hbm, scr)
    ctx.sy.wait("dma")
    ctx.sy.wait("dst")
    ctx.emit()
    return nc


def moe_host_inputs(inputs):
    m = {}
    m["Wg"] = np.ascontiguousarray(np.asarray(inputs["L1_ffn_Wg"], np.float32))
    m["bg"] = np.ascontiguousarray(np.asarray(inputs["L1_ffn_bg"], np.float32).reshape(1, 8))
    m["W1"] = np.ascontiguousarray(np.asarray(inputs["L1_ffn_W1"], np.float32))
    m["b1c"] = np.ascontiguousarray(np.asarray(inputs["L1_ffn_b1"], np.float32)
                                    .reshape(8, 32, 128).transpose(2, 0, 1).reshape(128, 256))
    m["W2"] = np.ascontiguousarray(np.asarray(inputs["L1_ffn_W2"], np.float32))
    m["b2row"] = np.ascontiguousarray(np.asarray(inputs["L1_ffn_b2"], np.float32).reshape(1, 8192))
    m["identf"] = np.eye(128, dtype=np.float32)
    m["ones512b"] = np.ones((1, 512), BF16NP)
    sel8 = np.zeros((8, 1024), np.float32)
    for e in range(8):
        sel8[e, e * 128:(e + 1) * 128] = 1.0
    m["sel8"] = sel8
    return m



def emit_l1b(ctx, ct, x_hbm, w, y_hbm, scr):
    from contextlib import ExitStack
    nc = ctx.nc
    sy, te, ve, sc, gp = ctx.sy, ctx.te, ctx.ve, ctx.sc, ctx.gp
    QSC = 1.0 / 32.0

    with ExitStack() as es:
        def sb(name, shape, dtype):
            return es.enter_context(nc.sbuf_tensor(name, shape, dtype))

        xhat = [sb(f"b1_xhat{i}", [128, T], BF16) for i in range(8)]
        with ExitStack() as ln_es:
            emit_ln(ctx, ct, x_hbm, xhat,
                    lambda n, s, d: ln_es.enter_context(nc.sbuf_tensor(n, s, d)),
                    lambda n, s, d: ln_es.enter_context(nc.psum_tensor(n, s, d)))

        with ExitStack() as ap:
            def sba(n, sh, d):
                return ap.enter_context(nc.sbuf_tensor(n, sh, d))
            def pba(n, sh, d):
                return ap.enter_context(nc.psum_tensor(n, sh, d))
            wstage = sba("b1_wstage", [128, 512], F32)
            wq_sb = sba("b1_wq", [128, 512], BF16)
            wk_sb = sba("b1_wk", [128, 512], BF16)
            wv_sb = sba("b1_wv", [128, 512], BF16)
            qop = sba("b1_qop", [65, L], BF16)
            kop = sba("b1_kop", [65, L], BF16)
            vop_all = sba("b1_vop", [128, 16 * 65], BF16)
            sqn = sba("b1_sqn", [64, L], BF16)
            n_k = sba("b1_nk", [1, L], F32)
            b2 = sba("b1_b2", [1, 1], F32)
            c_row = sba("b1_crow", [1, L], F32)
            dinv = sba("b1_dinv", [1, 1024], F32)
            attnT = [sba(f"b1_attnT{i}", [128, 512], BF16) for i in range(2)]
            o_sb = sba("b1_osb", [65, 1024], F32)
            drep = sba("b1_drep", [64, 1024], F32)
            gstage = sba("b1_gstage", [128, 128], F32)
            gbf = sba("b1_gbf", [128, 128], BF16)
            adg = sba("b1_adg", [128, 128], BF16)
            tblk = sba("b1_tblk", [128, 31 * 128], BF16)
            hot = sba("b1_hot", [64, 1024], F32)
            hot_bf = sba("b1_hotbf", [64, 1024], BF16)

            p_qv = pba("b1_p_qv", [128, 512], F32)
            p_q2 = pba("b1_p_q2", [128, 512], F32)
            p_rot = [p_qv, p_q2]
            p_s = [pba("b1_p_sc0", [128, 512], F32), pba("b1_p_sc1", [128, 512], F32)]
            p_o = pba("b1_p_o", [65, 1024], F32)
            p_b = pba("b1_p_b", [64, 1024], F32)
            p_dr = p_qv
            p_n = p_qv

            vop = [vop_all[:, tt * 65:(tt + 1) * 65] for tt in range(16)]
            gp.memset(kop[64:65, :], 1.0)
            for tt in range(16):
                g = gp.memset(vop_all[:, tt * 65 + 64: tt * 65 + 65], 1.0)
            g.inc("gp")
            te.wait("gp")
            sc.wait("gp")
            sy.dma_start(out=adg[:], in_=w["antidiag"]).inc("dma", 16)

            for h in range(16):
                for dlt in range(31):
                    c0 = 128 * (dlt - 15) + 1920
                    sy.wait("ve")
                    src = bass.AP(w["tableT"].tensor, h * 4095 + c0, [[1, 128], [1, 128]])
                    sy.dma_start(out=gstage[:], in_=src).inc("dma", 16)
                    ve.wait("dma")
                    ve.wait("te")
                    ve.tensor_copy(out=gbf[:], in_=gstage[:]).inc("ve")
                    te.wait("ve")
                    te.wait("se")
                    te.matmul(p_dr[:, 0:128], adg[:], gbf[:], start=True,
                              stop=True).inc("te")
                    sc.wait("te")
                    sc.copy(out=tblk[:, dlt * 128:(dlt + 1) * 128],
                            in_=p_dr[:, 0:128]).inc("se")
                for wdst, whbm in ((wq_sb, w["wqT"]), (wk_sb, w["wkT"]), (wv_sb, w["wvT"])):
                    sy.wait("te")
                    sy.wait("ve")
                    src = bass.AP(whbm.tensor, h * 64,
                                  [[1024, 128], [128 * 1024, 8], [1, 64]])
                    sy.dma_start(out=wstage[:], in_=src).inc("dma", 16)
                    ve.wait("dma")
                    ve.wait("te")
                    ve.tensor_copy(out=wdst[:], in_=wstage[:]).inc("ve")
                wrdy = ctx.cnt["ve"]
                for s in range(SEQ):
                    ts0 = s * L
                    te.wait("ve", wrdy)
                    rot_free = [0, 0]
                    ri = 0
                    jobs = []
                    for qc in range(4):
                        jobs.append(("q", qc))
                        jobs.append(("k", qc))
                    for tt in range(16):
                        jobs.append(("v", tt))
                    for kind, idx in jobs:
                        pbuf = p_rot[ri]
                        te.wait("se", rot_free[ri])
                        if kind in ("q", "k"):
                            qsl = slice(ts0 + idx * 512, ts0 + (idx + 1) * 512)
                            lsl = slice(idx * 512, (idx + 1) * 512)
                            wsb = wq_sb if kind == "q" else wk_sb
                            for kt in range(8):
                                mi = te.matmul(pbuf[0:64, :], wsb[:, kt * 64:(kt + 1) * 64],
                                               xhat[kt][:, qsl], start=(kt == 0), stop=(kt == 7))
                            mi.inc("te")
                            sc.wait("te")
                            dst = qop if kind == "q" else kop
                            rot_free[ri] = sc.activation(
                                out=dst[0:64, lsl], in_=pbuf[0:64, :], func=AF.Copy,
                                scale=(QSC if kind == "q" else 1.0)).inc("se")
                        else:
                            tsl = slice(ts0 + idx * 128, ts0 + (idx + 1) * 128)
                            for kt in range(8):
                                mi = te.matmul(pbuf[:, 0:64], xhat[kt][:, tsl],
                                               wv_sb[:, kt * 64:(kt + 1) * 64],
                                               start=(kt == 0), stop=(kt == 7))
                            mi.inc("te")
                            sc.wait("te")
                            rot_free[ri] = sc.copy(out=vop[idx][:, 0:64],
                                                   in_=pbuf[:, 0:64]).inc("se")
                        ri = (ri + 1) % 2
                    sc.activation(out=sqn[:], in_=kop[0:64, :], func=AF.Square).inc("se")
                    te.wait("se")
                    for qc in range(4):
                        te.wait("ve")
                        te.matmul(p_n[0:1, 0:512], ct["ones_b"][0:64, :],
                                  sqn[:, qc * 512:(qc + 1) * 512],
                                  start=True, stop=True).inc("te")
                        ve.wait("te")
                        ve.tensor_copy(out=n_k[0:1, qc * 512:(qc + 1) * 512],
                                       in_=p_n[0:1, 0:512]).inc("ve")
                    ctx.vsep()
                    ve.reduce_max(out=b2[:], in_=n_k[:], axis=AX.X).inc("ve")
                    sc.wait("te")
                    sc.activation(out=sqn[:], in_=qop[0:64, :], func=AF.Square).inc("se")
                    te.wait("se")
                    te.wait("ve")
                    for qc in range(4):
                        te.wait("se")
                        te.matmul(p_n[0:1, 0:512], ct["ones_b"][0:64, :],
                                  sqn[:, qc * 512:(qc + 1) * 512],
                                  start=True, stop=True).inc("te")
                        sc.wait("te")
                        sc.wait("ve")
                        sc.activation(out=c_row[0:1, qc * 512:(qc + 1) * 512],
                                      in_=p_n[0:1, 0:512], func=AF.Sqrt,
                                      scale=b2[:]).inc("se")
                    sc.mul(out=qop[64:65, :], in_=c_row[:], mul=-1.0).inc("se")
                    exp_cnt = [0, 0]
                    pv_cnt = [0, 0]
                    cbase = ctx.cnt.get("se", 0)
                    for qh in range(2):
                        for kt in range(16):
                            for qq in range(2):
                                buf = (kt * 2 + qq) % 2
                                te.wait("se", cbase)
                                te.wait("se", exp_cnt[buf])
                                te.matmul(p_s[buf][:],
                                          kop[:, kt * 128:(kt + 1) * 128],
                                          qop[:, qh * 1024 + qq * 512:
                                              qh * 1024 + (qq + 1) * 512],
                                          start=True, stop=True).inc("te")
                                tmm = ctx.cnt["te"]
                                sc.wait("te", pv_cnt[buf])
                                sc.wait("te", tmm)
                                exp_cnt[buf] = sc.activation(out=attnT[buf][:],
                                                             in_=p_s[buf][:],
                                                             func=AF.Exp).inc("se")
                                te.wait("se", exp_cnt[buf])
                                mi = te.matmul(p_o[:, qq * 512:(qq + 1) * 512],
                                               vop[kt][:, 0:65],
                                               attnT[buf][:],
                                               start=(kt == 0), stop=(kt == 15))
                                pv_cnt[buf] = mi.inc("te")
                            for qb in range(8):
                                dlt = qh * 8 + qb - kt
                                mi = te.matmul(p_b[:, qb * 128:(qb + 1) * 128],
                                               vop[kt][:, 0:64],
                                               tblk[:, (dlt + 15) * 128:(dlt + 16) * 128],
                                               start=(kt == 0), stop=(kt == 15))
                        mi.inc("te")
                        ve.wait("te")
                        ve.reciprocal(out=dinv[:], in_=p_o[64:65, :]).inc("ve")
                        te.wait("ve")
                        te.matmul(p_dr[:], ct["onesrow_f"][0:1, :], dinv[0:1, 0:512],
                                  start=True, stop=True).inc("te")
                        sc.wait("te")
                        sc.copy(out=drep[:, 0:512], in_=p_dr[0:64, :]).inc("se")
                        te.wait("se")
                        te.matmul(p_dr[:], ct["onesrow_f"][0:1, :],
                                  dinv[0:1, 512:1024], start=True, stop=True).inc("te")
                        sc.wait("te")
                        sc.copy(out=drep[:, 512:1024], in_=p_dr[0:64, :])
                        sc.copy(out=o_sb[:], in_=p_o[:]).inc("se")
                        ve.wait("se")
                        ve.tensor_mul(out=hot[:], in0=o_sb[0:64, :], in1=drep[:])
                        ve.tensor_add(out=hot_bf[:], in0=hot[:], in1=p_b[:]).inc("ve")
                        sy.wait("ve")
                        sy.dma_start(out=scr["ho"][h * 64:(h + 1) * 64,
                                                   ts0 + qh * 1024: ts0 + (qh + 1) * 1024],
                                     in_=hot_bf[:]).inc("dst", 16)
                        ve.wait("dst")
                        te.wait("ve")
            ctx.fence()

        with ExitStack() as lp:
            def sbl(n, sh, d):
                return lp.enter_context(nc.sbuf_tensor(n, sh, d))
            def pbl(n, sh, d):
                return lp.enter_context(nc.psum_tensor(n, sh, d))
            hot2 = [sbl(f"b1_ho{i}", [128, 512], BF16) for i in range(8)]
            sq8 = sbl("b1_sq8", [128, 4096], BF16)
            s_mean = sbl("b1_mean", [1, 512], F32)
            s_var = sbl("b1_var", [1, 512], F32)
            s_rstd = sbl("b1_rstd", [1, 512], F32)
            s_mr = sbl("b1_mr", [1, 512], F32)
            gcol = sbl("b1_gcol", [128, 8], F32)
            bcol = sbl("b1_bcol", [128, 8], F32)
            xt = sbl("b1_xt", [128, 512], F32)
            yt = sbl("b1_yt", [128, 512], F32)
            p_s1 = pbl("b1_p_s1", [1, 512], F32)
            p_s2 = pbl("b1_p_s2", [1, 512], F32)
            p_mr = pbl("b1_p_mr", [128, 512], F32)
            p_rs = pbl("b1_p_rs", [128, 512], F32)
            sy.dma_start(out=gcol[:], in_=w["gcol"]).inc("dma", 16)
            sy.dma_start(out=bcol[:], in_=w["bcol"]).inc("dma", 16)
            for tc in range(8):
                sl = slice(tc * 512, (tc + 1) * 512)
                sy.wait("ve")
                for ft in range(8):
                    sy.dma_start(out=hot2[ft][:],
                                 in_=scr["ho"][ft * 128:(ft + 1) * 128, sl]).inc("dma", 16)
                te.wait("dma")
                te.wait("ve")
                for ft in range(8):
                    te.matmul(p_s1[:], ct["ones_b"][:], hot2[ft][:],
                              start=(ft == 0), stop=(ft == 7))
                sc.wait("dma")
                sc.wait("te")
                for ft in range(8):
                    si = sc.activation(out=sq8[:, ft * 512:(ft + 1) * 512],
                                       in_=hot2[ft][:], func=AF.Square)
                si.inc("se")
                te.wait("se")
                for ft in range(8):
                    mi = te.matmul(p_s2[:], ct["ones_b"][:],
                                   sq8[:, ft * 512:(ft + 1) * 512],
                                   start=(ft == 0), stop=(ft == 7))
                mi.inc("te")
                ve.wait("te")
                ve.tensor_scalar_mul(out=s_mean[:], in0=p_s1[:], scalar1=1.0 / D)
                ve.tensor_mul(out=s_var[:], in0=s_mean[:], in1=s_mean[:])
                ve.scalar_tensor_tensor(out=s_var[:], in0=p_s2[:], scalar=1.0 / D,
                                        in1=s_var[:], op0=ALU.mult,
                                        op1=ALU.subtract).inc("ve")
                sc.wait("ve")
                sc.activation(out=s_var[:], in_=s_var[:], func=AF.Ln, bias=ct["eps"])
                sc.activation(out=s_rstd[:], in_=s_var[:], func=AF.Exp,
                              scale=-0.5).inc("se")
                ve.wait("se")
                ve.tensor_mul(out=s_mr[:], in0=s_mean[:], in1=s_rstd[:]).inc("ve")
                te.wait("ve")
                te.matmul(p_mr[:], ct["onesrow_f"][0:1, :], s_mr[:], start=True, stop=True)
                te.matmul(p_rs[:], ct["onesrow_f"][0:1, :], s_rstd[:], start=True,
                          stop=True).inc("te")
                ve.wait("te")
                for ft in range(8):
                    sy.wait("ve")
                    sy.dma_start(out=xt[:], in_=x_hbm[ft * 128:(ft + 1) * 128, sl]).inc("dma", 16)
                    ve.wait("dma")
                    ve.tensor_mul(out=yt[:], in0=hot2[ft][:], in1=p_rs[:])
                    ve.tensor_sub(out=yt[:], in0=yt[:], in1=p_mr[:])
                    ve.tensor_scalar(out=yt[:], in0=yt[:], scalar1=gcol[:, ft:ft + 1],
                                     scalar2=None, op0=ALU.mult)
                    ve.tensor_scalar(out=yt[:], in0=yt[:], scalar1=bcol[:, ft:ft + 1],
                                     scalar2=None, op0=ALU.add)
                    ve.tensor_add(out=yt[:], in0=yt[:], in1=xt[:]).inc("ve")
                    sy.wait("ve")
                    sy.dma_start(out=y_hbm[ft * 128:(ft + 1) * 128, sl],
                                 in_=yt[:]).inc("dst", 16)
                    ve.wait("dst")
            ctx.fence()


def build_l1b():
    nc = bass.Bass("TRN2", target_bir_lowering=False)
    ctx = Ctx(nc)
    ctx.consts = declare_consts(nc)
    x_hbm = nc.dram_tensor("x", [D, T], F32, kind="ExternalInput").ap()
    w = {}
    w["wqT"] = nc.dram_tensor("wqT", [1024, 1024], F32, kind="ExternalInput").ap()
    w["wkT"] = nc.dram_tensor("wkT", [1024, 1024], F32, kind="ExternalInput").ap()
    w["wvT"] = nc.dram_tensor("wvT", [1024, 1024], F32, kind="ExternalInput").ap()
    w["tableT"] = nc.dram_tensor("tableT", [16, 4095], F32, kind="ExternalInput").ap()
    w["antidiag"] = nc.dram_tensor("antidiag", [128, 128], BF16, kind="ExternalInput").ap()
    w["gcol"] = nc.dram_tensor("gcol", [128, 8], F32, kind="ExternalInput").ap()
    w["bcol"] = nc.dram_tensor("bcol", [128, 8], F32, kind="ExternalInput").ap()
    y_hbm = nc.dram_tensor("y", [D, T], F32, kind="ExternalOutput").ap()
    scr = {"ho": nc.dram_tensor("scr_ho", [1024, T], BF16).ap()}

    from contextlib import ExitStack
    with ExitStack() as es:
        def sb(name, shape, dtype):
            return es.enter_context(nc.sbuf_tensor(name, shape, dtype))
        ct = load_consts(ctx, sb)
        emit_l1b(ctx, ct, x_hbm, w, y_hbm, scr)
    ctx.sy.wait("dma")
    ctx.sy.wait("dst")
    ctx.emit()
    return nc


def l1b_host_inputs(inputs):
    m = {}
    m["wqT"] = np.ascontiguousarray(np.asarray(inputs["L1_attn_b_Wq"], np.float32).T)
    m["wkT"] = np.ascontiguousarray(np.asarray(inputs["L1_attn_b_Wk"], np.float32).T)
    m["wvT"] = np.ascontiguousarray(np.asarray(inputs["L1_attn_b_Wv"], np.float32).T)
    tab = np.asarray(inputs["L1_attn_b_bias_table"], np.float32)
    idx = np.minimum(np.arange(4095), tab.shape[0] - 1)
    m["tableT"] = np.ascontiguousarray(tab[idx].T)
    m["antidiag"] = np.ascontiguousarray(np.eye(128, dtype=BF16NP)[::-1])
    m["gcol"] = np.ascontiguousarray(np.asarray(inputs["L1_attn_b_ln_gamma"], np.float32).reshape(8, 128).T)
    m["bcol"] = np.ascontiguousarray(np.asarray(inputs["L1_attn_b_ln_beta"], np.float32).reshape(8, 128).T)
    return m


def build_l0a(debug=False):
    nc = bass.Bass("TRN2", target_bir_lowering=False)
    ctx = Ctx(nc)
    ctx.dbg = debug
    ctx.consts = declare_consts(nc)
    if debug:
        ctx.dbg_xhat = nc.dram_tensor("dbg_xhat", [1024, T], BF16, kind="ExternalOutput").ap()
        ctx.dbg_qop = nc.dram_tensor("dbg_qop", [65, L], BF16, kind="ExternalOutput").ap()
        ctx.dbg_kop = nc.dram_tensor("dbg_kop", [65, L], BF16, kind="ExternalOutput").ap()
        ctx.dbg_att = nc.dram_tensor("dbg_att", [128, 1024], BF16, kind="ExternalOutput").ap()
        ctx.dbg_den = nc.dram_tensor("dbg_den", [1, 1024], F32, kind="ExternalOutput").ap()
        ctx.dbg_nk = nc.dram_tensor("dbg_nk", [1, L], F32, kind="ExternalOutput").ap()
        ctx.dbg_b2 = nc.dram_tensor("dbg_b2", [1, 1], F32, kind="ExternalOutput").ap()
        ctx.dbg_sqn = nc.dram_tensor("dbg_sqn", [64, L], BF16, kind="ExternalOutput").ap()
    x_hbm = nc.dram_tensor("x", [D, T], F32, kind="ExternalInput").ap()
    wqkv = nc.dram_tensor("wqkv", [16, 3, 1024, 64], F32, kind="ExternalInput").ap()
    wo = nc.dram_tensor("wo", [1024, 1024], F32, kind="ExternalInput").ap()
    y_hbm = nc.dram_tensor("y", [D, T], F32, kind="ExternalOutput").ap()

    from contextlib import ExitStack
    with ExitStack() as es:
        def sb(name, shape, dtype):
            return es.enter_context(nc.sbuf_tensor(name, shape, dtype))
        ct = load_consts(ctx, sb)
        emit_l0a(ctx, ct, x_hbm, wqkv, wo, y_hbm)
    ctx.sy.wait("dma")
    ctx.sy.wait("dst")
    ctx.emit()
    return nc



def build_fused():
    nc = bass.Bass("TRN2", target_bir_lowering=False)
    ctx = Ctx(nc)
    ctx.consts = declare_consts(nc)
    ctx.dbg = None

    x_hbm = nc.dram_tensor("x", [D, T], F32, kind="ExternalInput").ap()
    y_hbm = nc.dram_tensor("y", [D, T], F32, kind="ExternalOutput").ap()
    xi = [nc.dram_tensor(f"xi{i}", [D, T], F32).ap() for i in range(7)]

    wqkv = nc.dram_tensor("wqkv", [16, 3, 1024, 64], F32, kind="ExternalInput").ap()
    wo = nc.dram_tensor("wo", [1024, 1024], F32, kind="ExternalInput").ap()
    wgT = nc.dram_tensor("wgT", [1024, 3072], I32, kind="ExternalInput").ap()
    wuT = nc.dram_tensor("wuT", [1024, 3072], I32, kind="ExternalInput").ap()
    wdT = nc.dram_tensor("wdT", [3072, 1024], I32, kind="ExternalInput").ap()
    ytmp = nc.dram_tensor("ytmp", [D, T], F32).ap()
    wm = {}
    wm["win"] = nc.dram_tensor("win", [1024, 4096], F32, kind="ExternalInput").ap()
    wm["convw"] = nc.dram_tensor("convw", [128, 64], F32, kind="ExternalInput").ap()
    wm["convb"] = nc.dram_tensor("convb", [128, 16], F32, kind="ExternalInput").ap()
    wm["wxdt"] = nc.dram_tensor("wxdt", [2048, 64], F32, kind="ExternalInput").ap()
    wm["wxB"] = nc.dram_tensor("wxB", [2048, 16], F32, kind="ExternalInput").ap()
    wm["wxC"] = nc.dram_tensor("wxC", [2048, 16], F32, kind="ExternalInput").ap()
    wm["wdt"] = nc.dram_tensor("wdt", [64, 2048], F32, kind="ExternalInput").ap()
    wm["bdt"] = nc.dram_tensor("bdt", [128, 16], F32, kind="ExternalInput").ap()
    wm["acols"] = nc.dram_tensor("acols", [128, 256], F32, kind="ExternalInput").ap()
    wm["dcol"] = nc.dram_tensor("dcol", [128, 16], F32, kind="ExternalInput").ap()
    wm["wout"] = nc.dram_tensor("wout", [2048, 1024], F32, kind="ExternalInput").ap()
    wm["sel16"] = nc.dram_tensor("sel16", [16, 2048], BF16, kind="ExternalInput").ap()
    scr_m = {}
    scr_m["u"] = nc.dram_tensor("scr_u", [2048, T], BF16).ap()
    scr_m["sil"] = nc.dram_tensor("scr_sil", [2048, T], BF16).ap()
    scr_m["y3"] = nc.dram_tensor("scr_y3", [2048, T], BF16).ap()
    scr_m["dtb"] = nc.dram_tensor("scr_dtb", [64, T], BF16).ap()
    scr_m["bc"] = nc.dram_tensor("scr_bc", [32, T], BF16).ap()
    wa = {}
    wa["qkvwT"] = nc.dram_tensor("qkvwT", [1024, 3072], F32, kind="ExternalInput").ap()
    wa["qkvb_row"] = nc.dram_tensor("qkvb_row", [1, 3072], F32, kind="ExternalInput").ap()
    wa["qkvb_cols"] = nc.dram_tensor("qkvb_cols", [128, 24], F32, kind="ExternalInput").ap()
    wa["offwT"] = nc.dram_tensor("offwT", [1024, 256], F32, kind="ExternalInput").ap()
    wa["lin2wT"] = nc.dram_tensor("lin2wT", [128, 2], F32, kind="ExternalInput").ap()
    wa["lin2_b"] = nc.dram_tensor("lin2_b", [1, 1], F32, kind="ExternalInput").ap()
    wa["r3"] = nc.dram_tensor("r3", [128, 4096], F32, kind="ExternalInput").ap()
    we = {}
    we["Wg"] = nc.dram_tensor("Wg", [1024, 8], F32, kind="ExternalInput").ap()
    we["bg"] = nc.dram_tensor("bg", [1, 8], F32, kind="ExternalInput").ap()
    we["W1"] = nc.dram_tensor("W1", [8, 1024, 4096], F32, kind="ExternalInput").ap()
    we["b1c"] = nc.dram_tensor("b1c", [128, 256], F32, kind="ExternalInput").ap()
    we["W2"] = nc.dram_tensor("W2", [8, 4096, 1024], F32, kind="ExternalInput").ap()
    we["b2row"] = nc.dram_tensor("b2row", [1, 8192], F32, kind="ExternalInput").ap()
    we["identf"] = nc.dram_tensor("identf", [128, 128], F32, kind="ExternalInput").ap()
    we["ones512b"] = nc.dram_tensor("ones512b", [1, 512], BF16, kind="ExternalInput").ap()
    we["sel8"] = nc.dram_tensor("sel8", [8, 1024], F32, kind="ExternalInput").ap()
    scr_e = {}
    scr_e["xh"] = nc.dram_tensor("scr_xh", [1024, T], BF16).ap()
    scr_e["w1c"] = nc.dram_tensor("scr_w1c", [8192, 4096], BF16).ap()
    scr_e["w2c"] = nc.dram_tensor("scr_w2c", [8192, 4096], BF16).ap()
    wb = {}
    wb["wqT"] = nc.dram_tensor("wqT", [1024, 1024], F32, kind="ExternalInput").ap()
    wb["wkT"] = nc.dram_tensor("wkT", [1024, 1024], F32, kind="ExternalInput").ap()
    wb["wvT"] = nc.dram_tensor("wvT", [1024, 1024], F32, kind="ExternalInput").ap()
    wb["tableT"] = nc.dram_tensor("tableT", [16, 4095], F32, kind="ExternalInput").ap()
    wb["antidiag"] = nc.dram_tensor("antidiag", [128, 128], BF16, kind="ExternalInput").ap()
    wb["gcol"] = nc.dram_tensor("gcol", [128, 8], F32, kind="ExternalInput").ap()
    wb["bcol"] = nc.dram_tensor("bcol", [128, 8], F32, kind="ExternalInput").ap()
    scr_b = {"ho": nc.dram_tensor("scr_ho", [1024, T], BF16).ap()}

    from contextlib import ExitStack
    with ExitStack() as es:
        def sb(name, shape, dtype):
            return es.enter_context(nc.sbuf_tensor(name, shape, dtype))
        ct = load_consts(ctx, sb)
        sel8_sb = sb("mo_sel8", [8, 1024], F32)
        ctx.sy.dma_start(out=sel8_sb[:], in_=we["sel8"]).inc("dma", 16)
        we["sel8_sb"] = sel8_sb

        emit_l0a(ctx, ct, x_hbm, wqkv, wo, xi[0])
        ctx.fence()
        emit_ffn0(ctx, ct, xi[0], wgT, wuT, wdT, xi[1], ytmp, tag="A")
        ctx.fence()
        emit_mamba(ctx, ct, xi[1], wm, xi[2], scr_m)
        ctx.fence()
        emit_ffn0(ctx, ct, xi[2], wgT, wuT, wdT, xi[3], ytmp, tag="B")
        ctx.fence()
        emit_l1a(ctx, ct, xi[3], wa, xi[4])
        ctx.fence()
        emit_moe(ctx, ct, xi[4], we, xi[5], scr_e, tag="A")
        ctx.fence()
        emit_l1b(ctx, ct, xi[5], wb, xi[6], scr_b)
        ctx.fence()
        emit_moe(ctx, ct, xi[6], we, y_hbm, scr_e, tag="B", skip_wconv=True)
    ctx.sy.wait("dma")
    ctx.sy.wait("dst")
    ctx.emit()
    return nc


def fused_host_inputs(inputs):
    m = {}
    m.update(host_consts())
    m["wqkv"] = np.ascontiguousarray(np.asarray(inputs["L0_attn_a_w_qkv"], np.float32))
    m["wo"] = np.ascontiguousarray(np.asarray(inputs["L0_attn_a_w_o"], np.float32))
    m["wgT"] = np.ascontiguousarray(np.asarray(inputs["L0_ffn_w_gate"], np.int32).T)
    m["wuT"] = np.ascontiguousarray(np.asarray(inputs["L0_ffn_w_up"], np.int32).T)
    m["wdT"] = np.ascontiguousarray(np.asarray(inputs["L0_ffn_w_down"], np.int32).T)
    m.update(mamba_host_inputs(inputs))
    m.update(l1a_host_inputs(inputs))
    m.update(moe_host_inputs(inputs))
    m.update(l1b_host_inputs(inputs))
    return m


def _to_feat_tok(x):
    b, t, f = x.shape
    return np.ascontiguousarray(x.transpose(2, 0, 1).reshape(f, b * t))


def _from_feat_tok(y):
    return np.ascontiguousarray(np.asarray(y).reshape(D, SEQ, L).transpose(1, 2, 0))


import os
_DBGFIN = os.environ.get("KDBG", "") == "1"


def _ckfin(tag, xcur):
    if _DBGFIN:
        import sys as _s
        for c, a in enumerate(xcur):
            f = np.isfinite(a).mean()
            if f < 1.0:
                print(f"[KDBG] {tag} core{c} finite={f}", file=_s.stderr)
        print(f"[KDBG] {tag} checked", file=_s.stderr)


def kernel_multi(**inputs):
    consts = host_consts()
    x = np.asarray(inputs["L0_attn_a_inputs"], np.float32)

    nc = build_l0a()
    wqkv_h = np.ascontiguousarray(np.asarray(inputs["L0_attn_a_w_qkv"], np.float32))
    wo_h = np.ascontiguousarray(np.asarray(inputs["L0_attn_a_w_o"], np.float32))
    in_maps = []
    for c in range(NCORES):
        m = dict(consts)
        m["x"] = _to_feat_tok(x[2 * c:2 * c + 2])
        m["wqkv"] = wqkv_h
        m["wo"] = wo_h
        in_maps.append(m)
    res = run_bass_kernel_spmd(nc, in_maps, core_ids=list(range(NCORES)))
    xcur = [np.asarray(r["y"]) for r in res.results]
    _ckfin("l0a", xcur)

    nc = build_ffn0()
    wgT_h = np.ascontiguousarray(np.asarray(inputs["L0_ffn_w_gate"], np.int32).T)
    wuT_h = np.ascontiguousarray(np.asarray(inputs["L0_ffn_w_up"], np.int32).T)
    wdT_h = np.ascontiguousarray(np.asarray(inputs["L0_ffn_w_down"], np.int32).T)
    in_maps = []
    for c in range(NCORES):
        m = dict(consts)
        m["x"] = xcur[c]
        m["wgT"] = wgT_h
        m["wuT"] = wuT_h
        m["wdT"] = wdT_h
        in_maps.append(m)
    res = run_bass_kernel_spmd(nc, in_maps, core_ids=list(range(NCORES)))
    xcur = [np.asarray(r["y"]) for r in res.results]
    _ckfin("ffn0a", xcur)

    nc = build_mamba()
    mh = mamba_host_inputs(inputs)
    in_maps = []
    for c in range(NCORES):
        m = dict(consts)
        m.update(mh)
        m["x"] = xcur[c]
        in_maps.append(m)
    res = run_bass_kernel_spmd(nc, in_maps, core_ids=list(range(NCORES)))
    xcur = [np.asarray(r["y"]) for r in res.results]
    _ckfin("mamba", xcur)

    nc = build_ffn0()
    in_maps = []
    for c in range(NCORES):
        m = dict(consts)
        m["x"] = xcur[c]
        m["wgT"] = wgT_h
        m["wuT"] = wuT_h
        m["wdT"] = wdT_h
        in_maps.append(m)
    res = run_bass_kernel_spmd(nc, in_maps, core_ids=list(range(NCORES)))
    xcur = [np.asarray(r["y"]) for r in res.results]
    _ckfin("ffn0b", xcur)

    nc = build_l1a()
    ah = l1a_host_inputs(inputs)
    in_maps = []
    for c in range(NCORES):
        m = dict(consts)
        m.update(ah)
        m["x"] = xcur[c]
        in_maps.append(m)
    res = run_bass_kernel_spmd(nc, in_maps, core_ids=list(range(NCORES)))
    xcur = [np.asarray(r["y"]) for r in res.results]
    _ckfin("l1a", xcur)

    nc = build_moe()
    eh = moe_host_inputs(inputs)
    in_maps = []
    for c in range(NCORES):
        m = dict(consts)
        m.update(eh)
        m["x"] = xcur[c]
        in_maps.append(m)
    res = run_bass_kernel_spmd(nc, in_maps, core_ids=list(range(NCORES)))
    xcur = [np.asarray(r["y"]) for r in res.results]
    _ckfin("moe1", xcur)

    nc = build_l1b()
    bh = l1b_host_inputs(inputs)
    in_maps = []
    for c in range(NCORES):
        m = dict(consts)
        m.update(bh)
        m["x"] = xcur[c]
        in_maps.append(m)
    res = run_bass_kernel_spmd(nc, in_maps, core_ids=list(range(NCORES)))
    xcur = [np.asarray(r["y"]) for r in res.results]
    _ckfin("l1b", xcur)

    nc = build_moe()
    in_maps = []
    for c in range(NCORES):
        m = dict(consts)
        m.update(eh)
        m["x"] = xcur[c]
        in_maps.append(m)
    res = run_bass_kernel_spmd(nc, in_maps, core_ids=list(range(NCORES)))
    xcur = [np.asarray(r["y"]) for r in res.results]
    _ckfin("moe2", xcur)

    x = np.concatenate([_from_feat_tok(y) for y in xcur], axis=0)
    return x


def kernel(**inputs):
    x = np.asarray(inputs["L0_attn_a_inputs"], np.float32)
    nc = build_fused()
    shared = fused_host_inputs(inputs)
    in_maps = []
    for c in range(NCORES):
        m = dict(shared)
        m["x"] = _to_feat_tok(x[2 * c:2 * c + 2])
        in_maps.append(m)
    res = run_bass_kernel_spmd(nc, in_maps, core_ids=list(range(NCORES)))
    return np.concatenate([_from_feat_tok(np.asarray(r["y"])) for r in res.results], axis=0)


# version 08
# speedup vs baseline: 1.0286x; 1.0286x over previous
import numpy as np
import ml_dtypes
import concourse.bass as bass
import concourse.mybir as mybir
from concourse.bass_utils import run_bass_kernel_spmd

F32 = mybir.dt.float32
BF16 = mybir.dt.bfloat16
I32 = mybir.dt.int32
AF = mybir.ActivationFunctionType
ALU = mybir.AluOpType
AX = mybir.AxisListType
BF16NP = ml_dtypes.bfloat16

NCORES = 8
BATCH = 16
L = 2048
D = 1024
SEQ = 2
T = SEQ * L

ENGINES = ("sync", "tensor", "vector", "scalar", "gpsimd")


class RecInst:
    __slots__ = ("ctx", "item")

    def __init__(self, ctx, item):
        self.ctx = ctx
        self.item = item

    def inc(self, name, n=1):
        self.ctx.cnt[name] = self.ctx.cnt.get(name, 0) + n
        self.item[3] = (name, n)
        return self.ctx.cnt[name]


class Rec:
    __slots__ = ("ctx", "eng")

    def __init__(self, ctx, eng):
        object.__setattr__(self, "ctx", ctx)
        object.__setattr__(self, "eng", eng)

    def wait(self, name, val=None):
        ctx = self.ctx
        if val is None:
            val = ctx.cnt.get(name, 0)
        if val <= 0:
            return
        if name in ("dma", "dst") and self.eng != "sync":
            tok = "ldk" if name == "dma" else "stk"
            if ctx.flushed.get(name, -1) < val:
                ctx.plan["sync"].append(["wait_ge", (name, ctx.cnt.get(name, 0)), {}, None])
                ctx.plan["sync"].append(["sem_inc", (tok, 1), {}, None])
                ctx.cnt[tok] = ctx.cnt.get(tok, 0) + 1
                ctx.flushed[name] = ctx.cnt.get(name, 0)
                ctx.lastwait[("sync", name)] = ctx.cnt.get(name, 0)
            self.wait(tok, ctx.cnt[tok])
            return
        key = (self.eng, name)
        if ctx.lastwait.get(key, -1) >= val:
            return
        ctx.lastwait[key] = val
        ctx.plan[self.eng].append(["wait_ge", (name, val), {}, None])

    def __getattr__(self, meth):
        ctx = self.ctx
        plan = ctx.plan[self.eng]

        def f(*a, **k):
            item = [meth, a, k, None]
            plan.append(item)
            return RecInst(ctx, item)

        return f


class Ctx:
    def __init__(self, nc):
        self.nc = nc
        self.plan = {e: [] for e in ENGINES}
        self.cnt = {}
        self.lastwait = {}
        self.flushed = {}
        self.sems = {}
        self._sem_guards = []
        self._vsep_tile = nc.alloc_sbuf_tensor("ctx_vsep", [1, 16], F32)
        self.sy = Rec(self, "sync")
        self.te = Rec(self, "tensor")
        self.ve = Rec(self, "vector")
        self.sc = Rec(self, "scalar")
        self.gp = Rec(self, "gpsimd")

    def vsep(self):
        t = self._vsep_tile
        self.ve.tensor_copy(out=t[0:1, 0:8], in_=t[0:1, 8:16])
        self.ve.tensor_copy(out=t[0:1, 0:8], in_=t[0:1, 8:16])

    def fence(self):
        snap = {k: self.cnt.get(k, 0) for k in ("te", "ve", "se", "gp", "dma", "dst")}
        for rec, own in ((self.te, "te"), (self.ve, "ve"), (self.sc, "se"),
                         (self.gp, "gp"), (self.sy, None)):
            for k, v in snap.items():
                if k == own:
                    continue
                rec.wait(k, v)

    def _sem(self, name):
        if name not in self.sems:
            g = self.nc.semaphore(name)
            self.sems[name] = g.__enter__()
            self._sem_guards.append(g)
        return self.sems[name]

    def emit(self):
        for e in ENGINES:
            for meth, a, k, incr in self.plan[e]:
                if meth in ("wait_ge", "sem_inc"):
                    self._sem(a[0])
                if incr is not None:
                    self._sem(incr[0])
        nc = self.nc
        with nc.Block() as block:
            for e in ENGINES:
                if not self.plan[e]:
                    continue

                def runner(eng, e=e):
                    for meth, a, k, incr in self.plan[e]:
                        if meth == "wait_ge":
                            eng.wait_ge(self.sems[a[0]], a[1])
                            continue
                        if meth == "sem_inc":
                            eng.sem_inc(self.sems[a[0]], a[1])
                            continue
                        inst = getattr(eng, meth)(*a, **k)
                        if incr is not None:
                            inst.then_inc(self.sems[incr[0]], incr[1])

                getattr(block, e)(runner)
        for g in self._sem_guards:
            g.__exit__(None, None, None)


def host_consts():
    return {
        "c_ones_f": np.ones((128, 1), np.float32),
        "c_ones_b": np.ones((128, 1), BF16NP),
        "c_onesrow_f": np.ones((1, 128), np.float32),
        "c_onesrow_b": np.ones((1, 128), BF16NP),
        "c_scal": np.array([[1e-5, 1.0, 0.0, 0.0, 0.0, 0.0, 0.0, 0.0]], np.float32),
    }


def declare_consts(nc):
    c = {}
    c["ones_f"] = nc.dram_tensor("c_ones_f", [128, 1], F32, kind="ExternalInput").ap()
    c["ones_b"] = nc.dram_tensor("c_ones_b", [128, 1], BF16, kind="ExternalInput").ap()
    c["onesrow_f"] = nc.dram_tensor("c_onesrow_f", [1, 128], F32, kind="ExternalInput").ap()
    c["onesrow_b"] = nc.dram_tensor("c_onesrow_b", [1, 128], BF16, kind="ExternalInput").ap()
    c["scal"] = nc.dram_tensor("c_scal", [1, 8], F32, kind="ExternalInput").ap()
    return c


def load_consts(ctx, sb):
    sy = ctx.sy
    t = {}
    t["ones_f"] = sb("ct_ones_f", [128, 1], F32)
    t["ones_b"] = sb("ct_ones_b", [128, 1], BF16)
    t["onesrow_f"] = sb("ct_onesrow_f", [1, 128], F32)
    t["onesrow_b"] = sb("ct_onesrow_b", [1, 128], BF16)
    t["scal"] = sb("ct_scal", [1, 8], F32)
    sy.dma_start(out=t["ones_f"][:], in_=ctx.consts["ones_f"]).inc("dma", 16)
    sy.dma_start(out=t["ones_b"][:], in_=ctx.consts["ones_b"]).inc("dma", 16)
    sy.dma_start(out=t["onesrow_f"][:], in_=ctx.consts["onesrow_f"]).inc("dma", 16)
    sy.dma_start(out=t["onesrow_b"][:], in_=ctx.consts["onesrow_b"]).inc("dma", 16)
    sy.dma_start(out=t["scal"][:], in_=ctx.consts["scal"]).inc("dma", 16)
    t["eps"] = t["scal"][0:1, 0:1]
    t["one"] = t["scal"][0:1, 1:2]
    return t


def emit_ln(ctx, ct, x_hbm, xhat, sb, pb, ntok=T):
    nc = ctx.nc
    sy, te, ve, sc = ctx.sy, ctx.te, ctx.ve, ctx.sc
    ctx.ln_id = getattr(ctx, "ln_id", 0) + 1
    P = f"ln{ctx.ln_id}_"
    xt = [sb(f"{P}xt{i}", [128, 512], F32) for i in range(8)]
    sq = sb(P + "sq", [128, 4096], F32)
    s_mean = sb(P + "mean", [1, 512], F32)
    s_var = sb(P + "var", [1, 512], F32)
    s_rstd = sb(P + "rstd", [1, 512], F32)
    s_mr = sb(P + "mr", [1, 512], F32)
    p_s1 = pb(P + "p_s1", [1, 512], F32)
    p_s2 = pb(P + "p_s2", [1, 512], F32)
    p_mr = pb(P + "p_mr", [128, 512], F32)
    p_rs = pb(P + "p_rs", [128, 512], F32)

    for tc in range(ntok // 512):
        sl = slice(tc * 512, (tc + 1) * 512)
        sy.wait("ve")
        for ft in range(8):
            sy.dma_start(out=xt[ft][:],
                         in_=x_hbm[ft * 128:(ft + 1) * 128, sl]).inc("dma", 16)
        te.wait("dma")
        te.wait("ve")
        for ft in range(8):
            te.matmul(p_s1[:], ct["ones_f"][:], xt[ft][:],
                      start=(ft == 0), stop=(ft == 7))
        sc.wait("dma")
        sc.wait("te")
        for ft in range(8):
            si = sc.activation(out=sq[:, ft * 512:(ft + 1) * 512], in_=xt[ft][:],
                               func=AF.Square)
        si.inc("se")
        te.wait("se")
        for ft in range(8):
            mi = te.matmul(p_s2[:], ct["ones_f"][:], sq[:, ft * 512:(ft + 1) * 512],
                           start=(ft == 0), stop=(ft == 7))
        mi.inc("te")
        ve.wait("te")
        ve.tensor_scalar_mul(out=s_mean[:], in0=p_s1[:], scalar1=1.0 / D)
        ve.tensor_mul(out=s_var[:], in0=s_mean[:], in1=s_mean[:])
        ve.scalar_tensor_tensor(out=s_var[:], in0=p_s2[:], scalar=1.0 / D,
                                in1=s_var[:], op0=ALU.mult,
                                op1=ALU.subtract).inc("ve")
        sc.wait("ve")
        sc.activation(out=s_var[:], in_=s_var[:], func=AF.Ln, bias=ct["eps"])
        sc.activation(out=s_rstd[:], in_=s_var[:], func=AF.Exp,
                      scale=-0.5).inc("se")
        ve.wait("se")
        ve.tensor_mul(out=s_mr[:], in0=s_mean[:], in1=s_rstd[:]).inc("ve")
        te.wait("ve")
        te.matmul(p_mr[:], ct["onesrow_f"][:], s_mr[:], start=True, stop=True)
        te.matmul(p_rs[:], ct["onesrow_f"][:], s_rstd[:], start=True,
                  stop=True).inc("te")
        ve.wait("te")
        for ft in range(8):
            ve.tensor_mul(out=xt[ft][:], in0=xt[ft][:], in1=p_rs[:])
            vi = ve.tensor_sub(out=xhat[ft][:, sl], in0=xt[ft][:], in1=p_mr[:])
        vi.inc("ve")
    ctx.fence()


def emit_l0a(ctx, ct, x_hbm, wqkv, wo, y_hbm):
    from contextlib import ExitStack
    nc = ctx.nc
    sy, te, ve, sc, gp = ctx.sy, ctx.te, ctx.ve, ctx.sc, ctx.gp

    with ExitStack() as es:
        def sb(name, shape, dtype):
            return es.enter_context(nc.sbuf_tensor(name, shape, dtype))

        xhat = [sb(f"l0a_xhat{i}", [128, T], BF16) for i in range(8)]
        with ExitStack() as ln_es:
            def sbl(name, shape, dtype):
                return ln_es.enter_context(nc.sbuf_tensor(name, shape, dtype))
            def pbl(name, shape, dtype):
                return ln_es.enter_context(nc.psum_tensor(name, shape, dtype))
            emit_ln(ctx, ct, x_hbm, xhat, sbl, pbl)

        hp = [sb(f"l0a_hp{i}", [128, L], BF16) for i in range(8)]
        wstage = [sb("l0a_wstage0", [128, 1024], F32), sb("l0a_wstage1", [128, 512], F32)]
        wq_sb = [sb(f"l0a_wq{i}", [128, 512], BF16) for i in range(2)]
        wk_sb = [sb(f"l0a_wk{i}", [128, 512], BF16) for i in range(2)]
        wv_sb = [sb(f"l0a_wv{i}", [128, 512], BF16) for i in range(2)]
        qop = [sb(f"l0a_qop{i}", [65, L], BF16) for i in range(2)]
        kop = [sb(f"l0a_kop{i}", [65, L], BF16) for i in range(2)]
        vop_all = [sb(f"l0a_vop{i}", [128, 16 * 65], BF16) for i in range(2)]
        sqn = [sb(f"l0a_sqn{i}", [64, L], BF16) for i in range(2)]
        n_k = [sb(f"l0a_nk{i}", [1, L], F32) for i in range(2)]
        b2 = [sb(f"l0a_b2{i}", [1, 1], F32) for i in range(2)]
        c_row = n_k
        dinv = sb("l0a_dinv", [1, 1024], F32)
        attnT = [sb(f"l0a_attnT{i}", [128, 512], BF16) for i in range(4)]
        drep = sb("l0a_drep", [64, 1024], F32)
        wo_sb = [sb(f"l0a_wo{i}", [128, 1024], BF16) for i in range(8)]
        xt_res = sb("l0a_xt", [128, 512], F32)
        yt_res = sb("l0a_yt", [128, 512], F32)

        def pb(name, shape, dtype):
            return es.enter_context(nc.psum_tensor(name, shape, dtype))
        p_rot = [pb(f"l0a_p_r{i}", [128, 512], F32) for i in range(3)]
        p_s = [pb("l0a_p_s0", [128, 512], F32), pb("l0a_p_s1", [128, 512], F32)]
        p_o = pb("l0a_p_o", [65, 1024], F32)
        p_dr = pb("l0a_p_dr", [128, 512], F32)
        p_n = p_dr

        vop = [[va[:, tt * 65:(tt + 1) * 65] for tt in range(16)] for va in vop_all]

        for par in range(2):
            gp.memset(kop[par][64:65, :], 1.0)
            for tt in range(16):
                g = gp.memset(vop_all[par][:, tt * 65 + 64: tt * 65 + 65], 1.0)
        g.inc("gp")
        te.wait("gp")
        sc.wait("gp")

        for i in range(8):
            sy.wait("ve")
            sy.dma_start(out=wstage[0][:], in_=wo[i * 128:(i + 1) * 128, :]).inc("dma", 16)
            ve.wait("dma")
            ve.tensor_copy(out=wo_sb[i][:], in_=wstage[0][:]).inc("ve")

        st = [dict(wrdy=0, pdone=0) for _ in range(2)]
        rot_free = [0, 0, 0]
        rot_i = [0]

        def emit_wload(h, s):
            par = h % 2
            for wdst, a in ((wq_sb[par], 0), (wk_sb[par], 1), (wv_sb[par], 2)):
                sy.wait("te", st[par]["pdone"])
                sy.wait("ve")
                src = bass.AP(wqkv.tensor, (h * 3 + a) * 1024 * 64,
                              [[64, 128], [128 * 64, 8], [1, 64]])
                sy.dma_start(out=wstage[1][:], in_=src).inc("dma", 16)
                ve.wait("dma")
                ve.wait("te", st[par]["pdone"])
                ve.tensor_copy(out=wdst[:], in_=wstage[1][:]).inc("ve")
            st[par]["wrdy"] = ctx.cnt["ve"]

        def emit_proj(h, s):
            par = h % 2
            ts0 = s * L
            te.wait("ve", st[par]["wrdy"])
            jobs = ([("q", qc) for qc in range(4)] + [("k", qc) for qc in range(4)]
                    + [("v", tt) for tt in range(16)])
            for kind, idx in jobs:
                ri = rot_i[0]
                rot_i[0] = (ri + 1) % 3
                pbuf = p_rot[ri]
                te.wait("se", rot_free[ri])
                if kind in ("q", "k"):
                    qsl = slice(ts0 + idx * 512, ts0 + (idx + 1) * 512)
                    lsl = slice(idx * 512, (idx + 1) * 512)
                    wsb = wq_sb[par] if kind == "q" else wk_sb[par]
                    for kt in range(8):
                        mi = te.matmul(pbuf[0:64, :], wsb[:, kt * 64:(kt + 1) * 64],
                                       xhat[kt][:, qsl], start=(kt == 0), stop=(kt == 7))
                    mi.inc("te")
                    sc.wait("te")
                    dsttile = qop[par] if kind == "q" else kop[par]
                    rot_free[ri] = sc.activation(
                        out=dsttile[0:64, lsl], in_=pbuf[0:64, :], func=AF.Copy,
                        scale=(0.125 if kind == "q" else 1.0)).inc("se")
                else:
                    tsl = slice(ts0 + idx * 128, ts0 + (idx + 1) * 128)
                    for kt in range(8):
                        mi = te.matmul(pbuf[:, 0:64], xhat[kt][:, tsl],
                                       wv_sb[par][:, kt * 64:(kt + 1) * 64],
                                       start=(kt == 0), stop=(kt == 7))
                    mi.inc("te")
                    sc.wait("te")
                    rot_free[ri] = sc.copy(out=vop[par][idx][:, 0:64],
                                           in_=pbuf[:, 0:64]).inc("se")
            st[par]["pdone"] = ctx.cnt["te"]

        def emit_cbound(h, s):
            par = h % 2
            sc.activation(out=sqn[par][:], in_=kop[par][0:64, :],
                          func=AF.Square).inc("se")
            te.wait("se")
            for qc in range(4):
                te.wait("ve")
                te.matmul(p_n[0:1, 0:512], ct["ones_b"][0:64, :],
                          sqn[par][:, qc * 512:(qc + 1) * 512],
                          start=True, stop=True).inc("te")
                ve.wait("te")
                ve.tensor_copy(out=n_k[par][0:1, qc * 512:(qc + 1) * 512],
                               in_=p_n[0:1, 0:512]).inc("ve")
            ctx.vsep()
            ve.reduce_max(out=b2[par][:], in_=n_k[par][:], axis=AX.X).inc("ve")
            sc.wait("te")
            sc.activation(out=sqn[par][:], in_=qop[par][0:64, :],
                          func=AF.Square).inc("se")
            te.wait("se")
            te.wait("ve")
            for qc in range(4):
                te.wait("se")
                te.matmul(p_n[0:1, 0:512], ct["ones_b"][0:64, :],
                          sqn[par][:, qc * 512:(qc + 1) * 512],
                          start=True, stop=True).inc("te")
                sc.wait("te")
                sc.wait("ve")
                sc.activation(out=c_row[par][0:1, qc * 512:(qc + 1) * 512],
                              in_=p_n[0:1, 0:512], func=AF.Sqrt,
                              scale=b2[par][:]).inc("se")
            sc.mul(out=qop[par][64:65, :], in_=c_row[par][:], mul=-1.0).inc("se")

        def emit_attn(h, s):
            par = h % 2
            p_sx = [p_s[0], p_s[1], p_rot[0], p_rot[1]]
            exp_cnt = [0, 0, 0, 0]
            pv_cnt = [0, 0, 0, 0]
            cbase = ctx.cnt.get("se", 0)
            for qh in range(2):
                for kt in range(16):
                    for qq in range(2):
                        buf = (kt * 2 + qq) % 4
                        te.wait("se", cbase)
                        te.wait("se", exp_cnt[buf])
                        te.matmul(p_sx[buf][:],
                                  kop[par][:, kt * 128:(kt + 1) * 128],
                                  qop[par][:, qh * 1024 + qq * 512:
                                      qh * 1024 + (qq + 1) * 512],
                                  start=True, stop=True).inc("te")
                        tmm = ctx.cnt["te"]
                        sc.wait("te", pv_cnt[buf])
                        sc.wait("te", tmm)
                        exp_cnt[buf] = sc.activation(out=attnT[buf][:],
                                                     in_=p_sx[buf][:],
                                                     func=AF.Exp).inc("se")
                        te.wait("se", exp_cnt[buf])
                        mi = te.matmul(p_o[:, qq * 512:(qq + 1) * 512],
                                       vop[par][kt][:, 0:65],
                                       attnT[buf][:],
                                       start=(kt == 0), stop=(kt == 15))
                        pv_cnt[buf] = mi.inc("te")
                ve.wait("te")
                ve.reciprocal(out=dinv[:], in_=p_o[64:65, :]).inc("ve")
                te.wait("ve")
                te.matmul(p_dr[:], ct["onesrow_f"][0:1, :], dinv[0:1, 0:512],
                          start=True, stop=True).inc("te")
                sc.wait("te")
                sc.copy(out=drep[:, 0:512], in_=p_dr[0:64, :]).inc("se")
                te.wait("se")
                te.matmul(p_dr[:], ct["onesrow_f"][0:1, :],
                          dinv[0:1, 512:1024], start=True, stop=True).inc("te")
                sc.wait("te")
                sc.copy(out=drep[:, 512:1024], in_=p_dr[0:64, :]).inc("se")
                ve.wait("se")
                dsttile = hp[h // 2][(h % 2) * 64:(h % 2) * 64 + 64,
                                     qh * 1024:(qh + 1) * 1024]
                ve.tensor_mul(out=dsttile, in0=p_o[0:64, :], in1=drep[:]).inc("ve")
                te.wait("ve")

        for s in range(SEQ):
            for h in range(16):
                emit_wload(h, s)
                emit_proj(h, s)
                if h >= 1:
                    emit_attn(h - 1, s)
                emit_cbound(h, s)
            emit_attn(15, s)

            ts0 = s * L
            hpr = ctx.cnt["ve"]
            wo_free = [0, 0, 0]
            xb_free = [0, 0, 0]
            yt_store = [0, 0, 0]
            xt3 = [xt_res, yt_res, sb(f"l0a_xt3_{s}", [128, 512], F32)]
            yt3 = [sb(f"l0a_yt3_{i}_{s}", [128, 512], F32) for i in range(3)]
            u = 0
            for tc in range(4):
                tsl = slice(ts0 + tc * 512, ts0 + (tc + 1) * 512)
                tloc = slice(tc * 512, (tc + 1) * 512)
                for jc in range(8):
                    ri = u % 3
                    u += 1
                    sy.wait("ve", xb_free[ri])
                    sy.dma_start(out=xt3[ri][:],
                                 in_=x_hbm[jc * 128:(jc + 1) * 128, tsl]).inc("dma", 16)
                    te.wait("ve", max(hpr, wo_free[ri]))
                    for p in range(8):
                        mi = te.matmul(p_rot[ri][:], wo_sb[p][:, jc * 128:(jc + 1) * 128],
                                       hp[p][:, tloc], start=(p == 0), stop=(p == 7))
                    mi.inc("te")
                    ve.wait("te")
                    ve.wait("dma")
                    ve.wait("dst", yt_store[ri])
                    ve.tensor_add(out=yt3[ri][:], in0=xt3[ri][:], in1=p_rot[ri][:]).inc("ve")
                    wo_free[ri] = ctx.cnt["ve"]
                    xb_free[ri] = ctx.cnt["ve"]
                    sy.wait("ve")
                    sy.dma_start(out=y_hbm[jc * 128:(jc + 1) * 128, tsl],
                                 in_=yt3[ri][:]).inc("dst", 16)
                    yt_store[ri] = ctx.cnt["dst"]
            te.wait("ve")


def emit_ffn0(ctx, ct, x_hbm, wgT, wuT, wdT, y_hbm, ytmp, tag=""):
    from contextlib import ExitStack
    nc = ctx.nc
    sy, te, ve, sc = ctx.sy, ctx.te, ctx.ve, ctx.sc

    with ExitStack() as es:
        def sb(name, shape, dtype):
            return es.enter_context(nc.sbuf_tensor(name + tag, shape, dtype))
        def pb(name, shape, dtype):
            return es.enter_context(nc.psum_tensor(name + tag, shape, dtype))

        xhat = [sb(f"f0_xhat{i}", [128, T], BF16) for i in range(8)]
        with ExitStack() as ln_es:
            emit_ln(ctx, ct, x_hbm, xhat,
                    lambda n, s, d: ln_es.enter_context(nc.sbuf_tensor(n + tag, s, d)),
                    lambda n, s, d: ln_es.enter_context(nc.psum_tensor(n + tag, s, d)))

        wg_sb = [sb(f"f0_wg{i}", [128, 1536], BF16) for i in range(8)]
        wu_sb = [sb(f"f0_wu{i}", [128, 1536], BF16) for i in range(8)]
        wd_sb = [sb(f"f0_wd{i}", [128, 1024], BF16) for i in range(12)]
        wint = sb("f0_wint", [128, 1536], I32)
        g_sb = [sb("f0_g0", [128, 512], F32), sb("f0_g1", [128, 512], F32)]
        h_act = [sb(f"f0_h{i}", [128, 512], BF16) for i in range(12)]
        xt = sb("f0_xt", [128, 512], F32)
        yt = sb("f0_yt", [128, 512], F32)
        p_g = [pb("f0_p_g0", [128, 512], F32), pb("f0_p_g1", [128, 512], F32)]
        p_u = [pb("f0_p_u0", [128, 512], F32), pb("f0_p_u1", [128, 512], F32)]
        p_y = [pb("f0_p_y0", [128, 512], F32), pb("f0_p_y1", [128, 512], F32)]

        for half in range(2):
            hof = half * 1536
            for grp, wsb_list, whbm, rows, cols in (
                (0, wg_sb, wgT, 1024, 1536),
                (1, wu_sb, wuT, 1024, 1536),
                (2, wd_sb, wdT, 3072, 1024),
            ):
                nt = len(wsb_list)
                for i in range(nt):
                    sy.wait("ve")
                    if grp < 2:
                        src = whbm[i * 128:(i + 1) * 128, hof:hof + 1536]
                        w = 1536
                    else:
                        src = whbm[hof + i * 128: hof + (i + 1) * 128, :]
                        w = 1024
                    sy.dma_start(out=wint[:, 0:w], in_=src).inc("dma", 16)
                    ve.wait("dma")
                    ve.wait("te")
                    ve.tensor_scalar_mul(out=wsb_list[i][:, 0:w], in0=wint[:, 0:w],
                                         scalar1=1.0 / 127.0).inc("ve")
            wrdy = ctx.cnt["ve"]
            te.wait("ve", wrdy)
            silu_free = [0, 0]
            gsb_free = [0, 0]
            mult_free = [0, 0]
            py_free = [0, 0]
            for tc in range(8):
                tsl = slice(tc * 512, (tc + 1) * 512)
                for ht in range(12):
                    ri = ht % 2
                    te.wait("se", silu_free[ri])
                    for kt in range(8):
                        mi = te.matmul(p_g[ri][:], wg_sb[kt][:, ht * 128:(ht + 1) * 128],
                                       xhat[kt][:, tsl], start=(kt == 0), stop=(kt == 7))
                    mi.inc("te")
                    sc.wait("te")
                    sc.wait("ve", gsb_free[ri])
                    silu_free[ri] = sc.activation(out=g_sb[ri][:], in_=p_g[ri][:],
                                                  func=AF.Silu).inc("se")
                    te.wait("ve", mult_free[ri])
                    for kt in range(8):
                        mi = te.matmul(p_u[ri][:], wu_sb[kt][:, ht * 128:(ht + 1) * 128],
                                       xhat[kt][:, tsl], start=(kt == 0), stop=(kt == 7))
                    mi.inc("te")
                    ve.wait("te")
                    ve.wait("se", silu_free[ri])
                    mult_free[ri] = ve.tensor_mul(out=h_act[ht][:], in0=g_sb[ri][:],
                                                  in1=p_u[ri][:]).inc("ve")
                    gsb_free[ri] = mult_free[ri]
                hrdy = ctx.cnt["ve"]
                for jc in range(8):
                    ri = jc % 2
                    jsl = slice(jc * 128, (jc + 1) * 128)
                    sy.wait("ve")
                    if half == 0:
                        sy.dma_start(out=xt[:], in_=x_hbm[jsl, tsl]).inc("dma", 16)
                    else:
                        sy.dma_start(out=xt[:], in_=ytmp[jsl, tsl]).inc("dma", 16)
                    te.wait("ve", max(hrdy, py_free[ri]))
                    for ht in range(12):
                        mi = te.matmul(p_y[ri][:], wd_sb[ht][:, jsl], h_act[ht][:],
                                       start=(ht == 0), stop=(ht == 11))
                    mi.inc("te")
                    ve.wait("te")
                    ve.wait("dma")
                    ve.tensor_add(out=yt[:], in0=xt[:], in1=p_y[ri][:]).inc("ve")
                    py_free[ri] = ctx.cnt["ve"]
                    sy.wait("ve")
                    if half == 0:
                        sy.dma_start(out=ytmp[jsl, tsl], in_=yt[:]).inc("dst", 16)
                    else:
                        sy.dma_start(out=y_hbm[jsl, tsl], in_=yt[:]).inc("dst", 16)
                    ve.wait("dst")
        ctx.fence()


def build_ffn0():
    nc = bass.Bass("TRN2", target_bir_lowering=False)
    ctx = Ctx(nc)
    ctx.consts = declare_consts(nc)
    x_hbm = nc.dram_tensor("x", [D, T], F32, kind="ExternalInput").ap()
    wgT = nc.dram_tensor("wgT", [1024, 3072], I32, kind="ExternalInput").ap()
    wuT = nc.dram_tensor("wuT", [1024, 3072], I32, kind="ExternalInput").ap()
    wdT = nc.dram_tensor("wdT", [3072, 1024], I32, kind="ExternalInput").ap()
    y_hbm = nc.dram_tensor("y", [D, T], F32, kind="ExternalOutput").ap()
    ytmp = nc.dram_tensor("ytmp", [D, T], F32).ap()

    from contextlib import ExitStack
    with ExitStack() as es:
        def sb(name, shape, dtype):
            return es.enter_context(nc.sbuf_tensor(name, shape, dtype))
        ct = load_consts(ctx, sb)
        emit_ffn0(ctx, ct, x_hbm, wgT, wuT, wdT, y_hbm, ytmp)
    ctx.sy.wait("dma")
    ctx.sy.wait("dst")
    ctx.emit()
    return nc



def emit_mamba(ctx, ct, x_hbm, w, y_hbm, scr):
    from contextlib import ExitStack
    import os as _os
    PAR = set((_os.environ.get("KPAR") or "").split(","))
    nc = ctx.nc
    sy, te, ve, sc, gp = ctx.sy, ctx.te, ctx.ve, ctx.sc, ctx.gp

    with ExitStack() as es:
        def sb(name, shape, dtype):
            return es.enter_context(nc.sbuf_tensor(name, shape, dtype))
        def pb(name, shape, dtype):
            return es.enter_context(nc.psum_tensor(name, shape, dtype))

        cols = sb("mb_cols", [128, 160], F32)
        sy.dma_start(out=cols[:, 0:64], in_=w["convw"]).inc("dma", 16)
        sy.dma_start(out=cols[:, 64:80], in_=w["convb"]).inc("dma", 16)
        sy.dma_start(out=cols[:, 80:96], in_=w["bdt"]).inc("dma", 16)
        sy.dma_start(out=cols[:, 96:112], in_=w["dcol"]).inc("dma", 16)
        a_sb = sb("mb_asb", [128, 256], F32)
        sy.dma_start(out=a_sb[:], in_=w["acols"]).inc("dma", 16)
        sel16 = sb("mb_sel16", [16, 2048], BF16)
        sy.dma_start(out=sel16[:], in_=w["sel16"]).inc("dma", 16)
        sc.wait("dma")
        sc.activation(out=a_sb[:], in_=a_sb[:], func=AF.Exp).inc("se")
        ve.wait("se")
        ve.tensor_scalar_mul(out=a_sb[:], in0=a_sb[:], scalar1=-1.0).inc("ve")
        wxdt = [sb(f"mb_wxdt{i}", [128, 64], BF16) for i in range(16)]
        wxB = [sb(f"mb_wxB{i}", [128, 16], BF16) for i in range(16)]
        wxC = [sb(f"mb_wxC{i}", [128, 16], BF16) for i in range(16)]
        wdt_sb = sb("mb_wdt", [64, 2048], BF16)
        wsts_es = ExitStack()
        wst_s = wsts_es.enter_context(nc.sbuf_tensor("mb_wsts", [128, 1024], F32))
        for i in range(16):
            sy.wait("ve")
            sy.dma_start(out=wst_s[:, 0:64], in_=w["wxdt"][i * 128:(i + 1) * 128, :]).inc("dma", 16)
            sy.dma_start(out=wst_s[:, 64:80], in_=w["wxB"][i * 128:(i + 1) * 128, :]).inc("dma", 16)
            sy.dma_start(out=wst_s[:, 80:96], in_=w["wxC"][i * 128:(i + 1) * 128, :]).inc("dma", 16)
            ve.wait("dma")
            ve.tensor_copy(out=wxdt[i][:], in_=wst_s[:, 0:64])
            ve.tensor_copy(out=wxB[i][:], in_=wst_s[:, 64:80])
            ve.tensor_copy(out=wxC[i][:], in_=wst_s[:, 80:96]).inc("ve")
        for hc in range(2):
            sy.wait("ve")
            sy.dma_start(out=wst_s[0:64, :],
                         in_=w["wdt"][:, hc * 1024:(hc + 1) * 1024]).inc("dma", 16)
            ve.wait("dma")
            ve.tensor_copy(out=wdt_sb[:, hc * 1024:(hc + 1) * 1024],
                           in_=wst_s[0:64, :]).inc("ve")
        ctx.te.wait("ve")
        wsts_es.close()

        with ExitStack() as axh:
          xhat = [axh.enter_context(nc.sbuf_tensor(f"mb_xhat{i}", [128, T], BF16))
                  for i in range(8)]
          with ExitStack() as ln_es:
            emit_ln(ctx, ct, x_hbm, xhat,
                    lambda n, s, d: ln_es.enter_context(nc.sbuf_tensor(n, s, d)),
                    lambda n, s, d: ln_es.enter_context(nc.psum_tensor(n, s, d)))

          for s in range(SEQ):
            ts0 = s * L
            if True:
              with ExitStack() as pa:
                  def sba(n, sh, d):
                      return pa.enter_context(nc.sbuf_tensor(f"{n}_s{s}", sh, d))
                  def pba(n, sh, d):
                      return pa.enter_context(nc.psum_tensor(f"{n}_s{s}", sh, d))
                  win_sb = [sba(f"mb_win{i}", [128, 4096], BF16) for i in range(8)]
                  wst = sba("mb_wst", [128, 1024], F32)
                  xs_t = sba("mb_xs", [128, L], F32)
                  acc = sba("mb_acc", [128, L], F32)
                  u_t = sba("mb_u", [128, L], BF16)
                  sil_t = sba("mb_sil", [128, 512], BF16)
                  dt_acc = sba("mb_dtacc", [64, L], F32)
                  b_acc = sba("mb_bacc", [16, L], F32)
                  c_acc = sba("mb_cacc", [16, L], F32)
                  dt_bf = sba("mb_dtbf", [64, L], BF16)
                  bb_bf = sba("mb_bbbf", [16, L], BF16)
                  cc_bf = sba("mb_ccbf", [16, L], BF16)
                  p_xz = pba("mb_p_xz", [128, 512], F32)
                  p_d = pba("mb_p_d", [64, 512], F32)
                  p_b = pba("mb_p_b", [16, 512], F32)
                  p_c = pba("mb_p_c", [16, 512], F32)

                  for i in range(8):
                      for hc in range(4):
                          sy.wait("ve")
                          sy.dma_start(out=wst[:],
                                       in_=w["win"][i * 128:(i + 1) * 128,
                                                    hc * 1024:(hc + 1) * 1024]).inc("dma", 16)
                          ve.wait("dma")
                          ve.wait("te")
                          ve.tensor_copy(out=win_sb[i][:, hc * 1024:(hc + 1) * 1024],
                                         in_=wst[:]).inc("ve")
                  gp.wait("ve")
                  gp.wait("se")
                  gp.wait("te")
                  gp.wait("dma")
                  gp.memset(dt_acc[:], 0.0)
                  gp.memset(b_acc[:], 0.0)
                  gp.memset(c_acc[:], 0.0).inc("gp")
                  ve.wait("gp")
                  wrdy = ctx.cnt["ve"]
                  te.wait("ve", wrdy)

                  for mt in range(32):
                      is_xs = mt < 16
                      for tc in range(4):
                          tl = slice(tc * 512, (tc + 1) * 512)
                          tg = slice(ts0 + tc * 512, ts0 + (tc + 1) * 512)
                          te.wait("se")
                          for kt in range(8):
                              mi = te.matmul(p_xz[:], win_sb[kt][:, mt * 128:(mt + 1) * 128],
                                             xhat[kt][:, tg], start=(kt == 0), stop=(kt == 7))
                          mi.inc("te")
                          sc.wait("te")
                          if is_xs:
                              sc.wait("ve")
                              sc.copy(out=xs_t[:, tl], in_=p_xz[:]).inc("se")
                          else:
                              sc.activation(out=sil_t[:], in_=p_xz[:],
                                            func=AF.Silu).inc("se")
                              sy.wait("se")
                              sy.dma_start(out=scr["sil"][(mt - 16) * 128:(mt - 15) * 128,
                                                          ts0 + tc * 512: ts0 + (tc + 1) * 512],
                                           in_=sil_t[:]).inc("dst", 16)
                              sc.wait("dst")
                      if is_xs:
                          dt2 = mt
                          ve.wait("se")
                          ve.tensor_scalar_mul(out=acc[:], in0=xs_t[:],
                                               scalar1=cols[:, dt2 * 4 + 3: dt2 * 4 + 4])
                          for k in (1, 2, 3):
                              ve.scalar_tensor_tensor(out=acc[:, k:L], in0=xs_t[:, 0:L - k],
                                                      scalar=cols[:, dt2 * 4 + 3 - k: dt2 * 4 + 4 - k],
                                                      in1=acc[:, k:L], op0=ALU.mult,
                                                      op1=ALU.add)
                          ve.nop()
                          ctx.plan["vector"].pop()
                          ctx.plan["vector"][-1][3] = None
                          vi = RecInst(ctx, ctx.plan["vector"][-1])
                          vi.inc("ve")
                          sc.wait("ve")
                          sc.activation(out=u_t[:], in_=acc[:], func=AF.Silu,
                                        bias=cols[:, 64 + dt2: 65 + dt2]).inc("se")
                          sy.wait("se")
                          sy.dma_start(out=scr["u"][dt2 * 128:(dt2 + 1) * 128,
                                                    ts0:ts0 + L], in_=u_t[:]).inc("dst", 16)
                          te.wait("se")
                          for tc in range(4):
                              tl = slice(tc * 512, (tc + 1) * 512)
                              te.wait("ve")
                              te.matmul(p_d[:], wxdt[dt2][:], u_t[:, tl], start=True, stop=True)
                              te.matmul(p_b[:], wxB[dt2][:], u_t[:, tl], start=True, stop=True)
                              te.matmul(p_c[:], wxC[dt2][:], u_t[:, tl], start=True,
                                        stop=True).inc("te")
                              ve.wait("te")
                              ve.tensor_add(out=dt_acc[:, tl], in0=dt_acc[:, tl], in1=p_d[:])
                              ve.tensor_add(out=b_acc[:, tl], in0=b_acc[:, tl], in1=p_b[:])
                              ve.tensor_add(out=c_acc[:, tl], in0=c_acc[:, tl],
                                            in1=p_c[:]).inc("ve")
                          sy.wait("te")
                      if "A" in PAR:
                          ctx.fence()
                  ve.tensor_copy(out=dt_bf[:], in_=dt_acc[:])
                  ve.tensor_copy(out=bb_bf[:], in_=b_acc[:])
                  ve.tensor_copy(out=cc_bf[:], in_=c_acc[:]).inc("ve")
                  sy.wait("ve")
                  sy.dma_start(out=scr["dtb"][0:64, ts0:ts0 + L], in_=dt_bf[:]).inc("dst", 16)
                  sy.dma_start(out=scr["bc"][0:16, ts0:ts0 + L], in_=bb_bf[:]).inc("dst", 16)
                  sy.dma_start(out=scr["bc"][16:32, ts0:ts0 + L], in_=cc_bf[:]).inc("dst", 16)
                  ctx.fence()

        for s in range(SEQ):
            ts0 = s * L
            with ExitStack() as pbs:
                def sbb(n, sh, d):
                    return pbs.enter_context(nc.sbuf_tensor(f"{n}_s{s}", sh, d))
                def pbb(n, sh, d):
                    return pbs.enter_context(nc.psum_tensor(f"{n}_s{s}", sh, d))
                bb_sb = sbb("mb_bb", [16, L], BF16)
                cc_sb = sbb("mb_cc", [16, L], BF16)
                dt_bf2 = sbb("mb_dtbf2", [64, L], BF16)
                brep = [sbb(f"mb_brep{i}", [128, L], BF16) for i in range(16)]
                crep = [sbb(f"mb_crep{i}", [128, L], BF16) for i in range(16)]
                delta = sbb("mb_delta", [128, L], F32)
                sabs = sbb("mb_sabs", [128, 512], F32)
                du = sbb("mb_du", [128, L], BF16)
                dA = [sbb("mb_dA0", [128, L], BF16), sbb("mb_dA1", [128, L], BF16)]
                Xs = sbb("mb_X", [128, L], BF16)
                yp = [sbb("mb_yp0", [128, L], BF16), sbb("mb_yp1", [128, L], BF16)]
                yacc = sbb("mb_yacc", [128, L], F32)
                u_t2 = sbb("mb_u2", [128, L], BF16)
                sil2 = sbb("mb_sil2", [128, L], BF16)
                p_rep = pbb("mb_p_rep", [128, 512], F32)
                p_de = pbb("mb_p_de", [128, 512], F32)

                sy.dma_start(out=bb_sb[:], in_=scr["bc"][0:16, ts0:ts0 + L]).inc("dma", 16)
                sy.dma_start(out=cc_sb[:], in_=scr["bc"][16:32, ts0:ts0 + L]).inc("dma", 16)
                sy.dma_start(out=dt_bf2[:], in_=scr["dtb"][0:64, ts0:ts0 + L]).inc("dma", 16)
                te.wait("dma")
                for n in range(16):
                    if "R" in PAR:
                        ctx.fence()
                    for tc in range(4):
                        tl = slice(tc * 512, (tc + 1) * 512)
                        te.wait("se")
                        te.matmul(p_rep[:], sel16[:, n * 128:(n + 1) * 128],
                                  bb_sb[:, tl], start=True, stop=True).inc("te")
                        sc.wait("te")
                        sc.copy(out=brep[n][:, tl], in_=p_rep[:]).inc("se")
                        te.wait("se")
                        te.matmul(p_rep[:], sel16[:, n * 128:(n + 1) * 128],
                                  cc_sb[:, tl], start=True, stop=True).inc("te")
                        sc.wait("te")
                        sc.copy(out=crep[n][:, tl], in_=p_rep[:]).inc("se")

                for dt2 in range(16):
                    bcol = cols[:, 80 + dt2:81 + dt2]
                    sy.wait("ve")
                    sy.dma_start(out=u_t2[:], in_=scr["u"][dt2 * 128:(dt2 + 1) * 128,
                                                           ts0:ts0 + L]).inc("dma", 16)
                    sy.dma_start(out=sil2[:], in_=scr["sil"][dt2 * 128:(dt2 + 1) * 128,
                                                             ts0:ts0 + L]).inc("dma", 16)
                    for tc in range(4):
                        tl = slice(tc * 512, (tc + 1) * 512)
                        te.wait("ve")
                        te.matmul(p_de[:], wdt_sb[:, dt2 * 128:(dt2 + 1) * 128],
                                  dt_bf2[:, tl], start=True, stop=True).inc("te")
                        sc.wait("te")
                        sc.activation(out=sabs[:], in_=p_de[:], func=AF.Abs, bias=bcol)
                        sc.activation(out=sabs[:], in_=sabs[:], func=AF.Exp, scale=-1.0)
                        sc.activation(out=sabs[:], in_=sabs[:], func=AF.Ln, bias=ct["ones_f"][:])
                        sc.activation(out=delta[:, tl], in_=p_de[:], func=AF.Relu,
                                      bias=bcol).inc("se")
                        ve.wait("se")
                        ve.tensor_add(out=delta[:, tl], in0=delta[:, tl],
                                      in1=sabs[:]).inc("ve")
                    ve.wait("dma")
                    ve.tensor_mul(out=du[:], in0=delta[:], in1=u_t2[:]).inc("ve")
                    scan_done = [0, 0]
                    exp_done = [0, 0]
                    gp_done = [0, 0]
                    for n in range(16):
                        buf = n % 2
                        acol = a_sb[:, dt2 * 16 + n: dt2 * 16 + n + 1]
                        sc.wait("ve", scan_done[buf])
                        if n == 0:
                            sc.wait("ve")
                        exp_done[buf] = sc.activation(out=dA[buf][:], in_=delta[:],
                                                      func=AF.Exp, scale=acol).inc("se")
                        ve.wait("se", exp_done[buf])
                        ve.wait("gp", gp_done[buf])
                        ve.tensor_mul(out=Xs[:], in0=du[:], in1=brep[n][:])
                        ve.tensor_tensor_scan(out=Xs[:], data0=dA[buf][:], data1=Xs[:],
                                              initial=0.0, op0=ALU.mult, op1=ALU.add)
                        scan_done[buf] = ve.tensor_mul(out=yp[buf][:], in0=Xs[:],
                                                       in1=crep[n][:]).inc("ve")
                        gp.wait("ve", scan_done[buf])
                        if n == 0:
                            gp.wait("dst")
                            gp_done[buf] = gp.tensor_copy(out=yacc[:],
                                                          in_=yp[buf][:]).inc("gp")
                        else:
                            gp_done[buf] = gp.tensor_add(out=yacc[:], in0=yacc[:],
                                                         in1=yp[buf][:]).inc("gp")
                    ve.wait("gp", gp_done[15 % 2])
                    ve.wait("gp", gp_done[14 % 2])
                    ve.scalar_tensor_tensor(out=yacc[:], in0=u_t2[:],
                                            scalar=cols[:, 96 + dt2:97 + dt2],
                                            in1=yacc[:], op0=ALU.mult, op1=ALU.add)
                    ve.tensor_mul(out=yacc[:], in0=yacc[:], in1=sil2[:]).inc("ve")
                    gp.wait("ve")
                    gp.dma_start(out=scr["y3"][dt2 * 128:(dt2 + 1) * 128, ts0:ts0 + L],
                                 in_=yacc[:]).inc("dst", 16)
                    ve.wait("dst")
                    if "B" in PAR:
                        ctx.fence()
                ctx.fence()

            with ExitStack() as pc:
                def sbc(n, sh, d):
                    return pc.enter_context(nc.sbuf_tensor(f"{n}_s{s}", sh, d))
                def pbc(n, sh, d):
                    return pc.enter_context(nc.psum_tensor(f"{n}_s{s}", sh, d))
                wout_sb = [sbc(f"mb_wo{i}", [128, 1024], BF16) for i in range(16)]
                wst2 = sbc("mb_wst2", [128, 1024], F32)
                y3c = [sbc(f"mb_y3c{i}", [128, 512], BF16) for i in range(16)]
                xt = sbc("mb_xt", [128, 512], F32)
                yt = sbc("mb_yt", [128, 512], F32)
                p_y = pbc("mb_p_y", [128, 512], F32)
                for i in range(16):
                    sy.wait("ve")
                    sy.dma_start(out=wst2[:], in_=w["wout"][i * 128:(i + 1) * 128, :]).inc("dma", 16)
                    ve.wait("dma")
                    ve.wait("te")
                    ve.tensor_copy(out=wout_sb[i][:], in_=wst2[:]).inc("ve")
                for tc in range(4):
                    tg = slice(ts0 + tc * 512, ts0 + (tc + 1) * 512)
                    te.wait("te")
                    sy.wait("te")
                    for i in range(16):
                        sy.dma_start(out=y3c[i][:], in_=scr["y3"][i * 128:(i + 1) * 128,
                                                                  tg]).inc("dma", 16)
                    for jc in range(8):
                        jsl = slice(jc * 128, (jc + 1) * 128)
                        sy.wait("ve")
                        sy.dma_start(out=xt[:], in_=x_hbm[jsl, tg]).inc("dma", 16)
                        te.wait("dma")
                        te.wait("ve")
                        for i in range(16):
                            mi = te.matmul(p_y[:], wout_sb[i][:, jsl], y3c[i][:],
                                           start=(i == 0), stop=(i == 15))
                        mi.inc("te")
                        ve.wait("te")
                        ve.wait("dma")
                        ve.tensor_add(out=yt[:], in0=xt[:], in1=p_y[:]).inc("ve")
                        sy.wait("ve")
                        sy.dma_start(out=y_hbm[jsl, tg], in_=yt[:]).inc("dst", 16)
                        ve.wait("dst")
                        if "C" in PAR:
                            ctx.fence()
                ctx.fence()


def build_mamba(debug=False):
    nc = bass.Bass("TRN2", target_bir_lowering=False)
    ctx = Ctx(nc)
    ctx.consts = declare_consts(nc)
    x_hbm = nc.dram_tensor("x", [D, T], F32, kind="ExternalInput").ap()
    w = {}
    w["win"] = nc.dram_tensor("win", [1024, 4096], F32, kind="ExternalInput").ap()
    w["convw"] = nc.dram_tensor("convw", [128, 64], F32, kind="ExternalInput").ap()
    w["convb"] = nc.dram_tensor("convb", [128, 16], F32, kind="ExternalInput").ap()
    w["wxdt"] = nc.dram_tensor("wxdt", [2048, 64], F32, kind="ExternalInput").ap()
    w["wxB"] = nc.dram_tensor("wxB", [2048, 16], F32, kind="ExternalInput").ap()
    w["wxC"] = nc.dram_tensor("wxC", [2048, 16], F32, kind="ExternalInput").ap()
    w["wdt"] = nc.dram_tensor("wdt", [64, 2048], F32, kind="ExternalInput").ap()
    w["bdt"] = nc.dram_tensor("bdt", [128, 16], F32, kind="ExternalInput").ap()
    w["acols"] = nc.dram_tensor("acols", [128, 256], F32, kind="ExternalInput").ap()
    w["dcol"] = nc.dram_tensor("dcol", [128, 16], F32, kind="ExternalInput").ap()
    w["wout"] = nc.dram_tensor("wout", [2048, 1024], F32, kind="ExternalInput").ap()
    w["sel16"] = nc.dram_tensor("sel16", [16, 2048], BF16, kind="ExternalInput").ap()
    y_hbm = nc.dram_tensor("y", [D, T], F32, kind="ExternalOutput").ap()
    scr = {}
    kind = "ExternalOutput" if debug else "Internal"
    scr["u"] = nc.dram_tensor("scr_u", [2048, T], BF16, kind=kind).ap()
    scr["sil"] = nc.dram_tensor("scr_sil", [2048, T], BF16, kind=kind).ap()
    scr["y3"] = nc.dram_tensor("scr_y3", [2048, T], BF16, kind=kind).ap()
    scr["dtb"] = nc.dram_tensor("scr_dtb", [64, T], BF16, kind=kind).ap()
    scr["bc"] = nc.dram_tensor("scr_bc", [32, T], BF16, kind=kind).ap()

    from contextlib import ExitStack
    with ExitStack() as es:
        def sb(name, shape, dtype):
            return es.enter_context(nc.sbuf_tensor(name, shape, dtype))
        ct = load_consts(ctx, sb)
        emit_mamba(ctx, ct, x_hbm, w, y_hbm, scr)
    ctx.sy.wait("dma")
    ctx.sy.wait("dst")
    ctx.emit()
    return nc


def mamba_host_inputs(inputs):
    m = {}
    m["win"] = np.ascontiguousarray(np.asarray(inputs["L0_attn_b_W_in"], np.float32))
    cw = np.asarray(inputs["L0_attn_b_conv_w"], np.float32)
    m["convw"] = np.ascontiguousarray(cw.reshape(16, 128, 4).transpose(1, 0, 2).reshape(128, 64))
    m["convb"] = np.ascontiguousarray(np.asarray(inputs["L0_attn_b_conv_b"], np.float32).reshape(16, 128).T)
    wx = np.asarray(inputs["L0_attn_b_W_x"], np.float32)
    m["wxdt"] = np.ascontiguousarray(wx[:, 0:64])
    m["wxB"] = np.ascontiguousarray(wx[:, 64:80])
    m["wxC"] = np.ascontiguousarray(wx[:, 80:96])
    m["wdt"] = np.ascontiguousarray(np.asarray(inputs["L0_attn_b_W_dt"], np.float32))
    m["bdt"] = np.ascontiguousarray(np.asarray(inputs["L0_attn_b_b_dt"], np.float32).reshape(16, 128).T)
    m["acols"] = np.ascontiguousarray(np.asarray(inputs["L0_attn_b_A_log"], np.float32)
                                      .reshape(16, 128, 16).transpose(1, 0, 2).reshape(128, 256))
    m["dcol"] = np.ascontiguousarray(np.asarray(inputs["L0_attn_b_D"], np.float32).reshape(16, 128).T)
    m["wout"] = np.ascontiguousarray(np.asarray(inputs["L0_attn_b_W_out"], np.float32))
    sel16 = np.zeros((16, 2048), BF16NP)
    for n in range(16):
        sel16[n, n * 128:(n + 1) * 128] = 1.0
    m["sel16"] = sel16
    return m



def emit_l1a(ctx, ct, x_hbm, w, y_hbm):
    from contextlib import ExitStack
    nc = ctx.nc
    sy, te, ve, sc, gp = ctx.sy, ctx.te, ctx.ve, ctx.sc, ctx.gp
    WS = (4.0, 16.0, 64.0, 256.0)
    ESC = 2.0 / float(np.sqrt(2048.0))

    with ExitStack() as es:
        def sb(name, shape, dtype):
            return es.enter_context(nc.sbuf_tensor(name, shape, dtype))

        xhat = [sb(f"a1_xhat{i}", [128, T], BF16) for i in range(8)]
        with ExitStack() as ln_es:
            emit_ln(ctx, ct, x_hbm, xhat,
                    lambda n, s, d: ln_es.enter_context(nc.sbuf_tensor(n, s, d)),
                    lambda n, s, d: ln_es.enter_context(nc.psum_tensor(n, s, d)))

        r3 = sb("a1_r3", [128, 4096], F32)
        sy.dma_start(out=r3[:], in_=w["r3"]).inc("dma", 16)
        bvrow = sb("a1_bvrow", [1, 3072], BF16)
        with nc.sbuf_tensor("a1_bstage", [1, 3072], F32) as bstage:
            sy.dma_start(out=bstage[:], in_=w["qkvb_row"]).inc("dma", 16)
            ve.wait("dma")
            ve.tensor_copy(out=bvrow[:], in_=bstage[:]).inc("ve")
            ve.wait("ve")
        qkvb = sb("a1_qkvb", [128, 24], F32)
        sy.dma_start(out=qkvb[:], in_=w["qkvb_cols"]).inc("dma", 16)
        dx0 = [sb(f"a1_dx0_{s}", [1, L], F32) for s in range(SEQ)]

        def pb(name, shape, dtype):
            return es.enter_context(nc.psum_tensor(name, shape, dtype))
        p_qv = pb("a1_p_qv", [128, 512], F32)
        p_s = pb("a1_p_s", [128, 512], F32)
        p_oa = pb("a1_p_oa", [128, 1024], F32)
        p_ob = pb("a1_p_ob", [128, 1024], F32)
        p_nd = pb("a1_p_nd", [1, 1024], F32)

        with ExitStack() as dxs:
            def sbd(n, sh, d):
                return dxs.enter_context(nc.sbuf_tensor(n, sh, d))
            offw_sb = sbd("a1_offw", [128, 2048], BF16)
            wst = sbd("a1_wst", [128, 2048], F32)
            lin2_sb = sbd("a1_lin2", [128, 2], BF16)
            rt = sbd("a1_rt", [128, 512], BF16)
            z1 = sbd("a1_z1", [128, L], BF16)
            z1b = sbd("a1_z1b", [128, L], BF16)
            lb = sbd("a1_lb", [1, 1], F32)
            sy.dma_start(out=lb[:], in_=w["lin2_b"]).inc("dma", 16)
            src = bass.AP(w["offwT"].tensor, 0, [[256, 128], [128 * 256, 8], [1, 256]])
            sy.dma_start(out=wst[:], in_=src).inc("dma", 16)
            ve.wait("dma")
            ve.tensor_copy(out=offw_sb[:], in_=wst[:]).inc("ve")
            sy.wait("ve")
            sy.dma_start(out=wst[0:128, 0:2], in_=w["lin2wT"]).inc("dma", 16)
            ve.wait("dma")
            ve.tensor_copy(out=lin2_sb[:], in_=wst[0:128, 0:2]).inc("ve")
            te.wait("ve")
            for s in range(SEQ):
                ts0 = s * L
                for mh in range(2):
                    dst = z1 if mh == 0 else z1b
                    for tc in range(4):
                        tg = slice(ts0 + tc * 512, ts0 + (tc + 1) * 512)
                        tl = slice(tc * 512, (tc + 1) * 512)
                        for kt in range(8):
                            sc.wait("te")
                            sc.activation(out=rt[:], in_=xhat[kt][:, tg],
                                          func=AF.Relu).inc("se")
                            te.wait("se")
                            te.matmul(p_qv[:], offw_sb[:, kt * 256 + mh * 128:
                                                       kt * 256 + (mh + 1) * 128],
                                      rt[:], start=(kt == 0),
                                      stop=(kt == 7)).inc("te")
                        sc.wait("te")
                        sc.copy(out=dst[:, tl], in_=p_qv[:]).inc("se")
                te.wait("se")
                for tc in range(4):
                    tl = slice(tc * 512, (tc + 1) * 512)
                    te.wait("se")
                    te.matmul(p_nd[:, 0:512], lin2_sb[:, 0:1], z1[:, tl],
                              start=True, stop=False)
                    te.matmul(p_nd[:, 0:512], lin2_sb[:, 1:2], z1b[:, tl],
                              start=False, stop=True).inc("te")
                    sc.wait("te")
                    sc.activation(out=dx0[s][0:1, tl], in_=p_nd[0:1, 0:512],
                                  func=AF.Sigmoid, bias=lb[:]).inc("se")
                ve.wait("se")
                ve.tensor_scalar_mul(out=dx0[s][:], in0=dx0[s][:],
                                     scalar1=float(L)).inc("ve")
            ctx.fence()

        wstage = sb("a1_wstage", [128, 2048], F32)
        wq_sb = sb("a1_wq", [128, 2048], BF16)
        wk_sb = sb("a1_wk", [128, 2048], BF16)
        wv_sb = sb("a1_wv", [128, 2048], BF16)
        qop = [sb(f"a1_qop{i}", [128, L], BF16) for i in range(2)]
        kop = [sb(f"a1_kop{i}", [128, L], BF16) for i in range(2)]
        vop_all = sb("a1_vop", [128, 16 * 256], BF16)
        negc = sb("a1_negc", [1, L], BF16)
        sqn = sb("a1_sqn", [128, L], BF16)
        sqn2 = sb("a1_sqn2", [128, L], BF16)
        n_k = sb("a1_nk", [1, L], F32)
        b2 = sb("a1_b2", [1, 1], F32)
        dxrep = sb("a1_dxrep", [128, L], F32)
        attnT = sb("a1_attnT", [128, 1024], BF16)
        mskt = sb("a1_mskt", [128, 1024], BF16)
        drep = sb("a1_drep", [128, 1024], F32)
        xt2 = sb("a1_xt2", [128, 1024], F32)
        yt2 = sb("a1_yt2", [128, 1024], F32)
        vop = [vop_all[:, tt * 256:(tt + 1) * 256] for tt in range(16)]

        for g in range(4):
            for a, wdst in ((0, wq_sb), (1, wk_sb), (2, wv_sb)):
                coff = a * 1024 + g * 256
                sy.wait("ve")
                src = bass.AP(w["qkvwT"].tensor, coff,
                              [[3072, 128], [128 * 3072, 8], [1, 256]])
                sy.dma_start(out=wstage[:], in_=src).inc("dma", 16)
                ve.wait("dma")
                ve.wait("te")
                ve.tensor_copy(out=wdst[:], in_=wstage[:]).inc("ve")
            wrdy = ctx.cnt["ve"]
            for s in range(SEQ):
                ts0 = s * L
                te.wait("ve", wrdy)
                for mh in range(2):
                    bq = qkvb[:, g * 2 + mh: g * 2 + mh + 1]
                    bk = qkvb[:, 8 + g * 2 + mh: 8 + g * 2 + mh + 1]
                    for tc in range(4):
                        tg = slice(ts0 + tc * 512, ts0 + (tc + 1) * 512)
                        tl = slice(tc * 512, (tc + 1) * 512)
                        te.wait("se")
                        for kt in range(8):
                            mi = te.matmul(p_qv[:], wq_sb[:, kt * 256 + mh * 128:
                                                          kt * 256 + (mh + 1) * 128],
                                           xhat[kt][:, tg], start=(kt == 0), stop=(kt == 7))
                        mi.inc("te")
                        ve.wait("te")
                        ve.tensor_scalar_add(out=qop[mh][:, tl], in0=p_qv[:],
                                             scalar1=bq).inc("ve")
                        te.wait("ve")
                        for kt in range(8):
                            mi = te.matmul(p_qv[:], wk_sb[:, kt * 256 + mh * 128:
                                                          kt * 256 + (mh + 1) * 128],
                                           xhat[kt][:, tg], start=(kt == 0), stop=(kt == 7))
                        mi.inc("te")
                        ve.wait("te")
                        ve.tensor_scalar_add(out=kop[mh][:, tl], in0=p_qv[:],
                                             scalar1=bk).inc("ve")
                        te.wait("ve")
                for tt in range(16):
                    tsl = slice(ts0 + tt * 128, ts0 + (tt + 1) * 128)
                    for mh in range(2):
                        for kt in range(8):
                            te.matmul(p_qv[:, 0:128], xhat[kt][:, tsl],
                                      wv_sb[:, kt * 256 + mh * 128: kt * 256 + (mh + 1) * 128],
                                      start=(kt == 0), stop=False)
                        mi = te.matmul(p_qv[:, 0:128], ct["onesrow_b"][0:1, 0:128],
                                       bvrow[0:1, 2048 + g * 256 + mh * 128:
                                             2048 + g * 256 + (mh + 1) * 128],
                                       start=False, stop=True)
                        mi.inc("te")
                        sc.wait("te")
                        sc.copy(out=vop[tt][:, mh * 128:(mh + 1) * 128],
                                in_=p_qv[:, 0:128]).inc("se")
                        te.wait("se")
                sc.wait("ve")
                sc.activation(out=sqn[:], in_=kop[0][:], func=AF.Square)
                sc.activation(out=sqn2[:], in_=kop[1][:], func=AF.Square).inc("se")
                te.wait("se")
                for qc in range(4):
                    qsl = slice(qc * 512, (qc + 1) * 512)
                    te.wait("ve")
                    te.matmul(p_nd[:, 0:512], ct["ones_b"][0:128, :], sqn[:, qsl],
                              start=True, stop=False)
                    te.matmul(p_nd[:, 0:512], ct["ones_b"][0:128, :], sqn2[:, qsl],
                              start=False, stop=True).inc("te")
                    ve.wait("te")
                    ve.tensor_copy(out=n_k[0:1, qsl], in_=p_nd[0:1, 0:512]).inc("ve")
                ctx.vsep()
                ve.reduce_max(out=b2[:], in_=n_k[:], axis=AX.X).inc("ve")
                sc.wait("te")
                sc.wait("ve")
                sc.activation(out=sqn[:], in_=qop[0][:], func=AF.Square)
                sc.activation(out=sqn2[:], in_=qop[1][:], func=AF.Square).inc("se")
                te.wait("se")
                te.wait("ve")
                for qc in range(4):
                    qsl = slice(qc * 512, (qc + 1) * 512)
                    te.wait("se")
                    te.matmul(p_nd[:, 0:512], ct["ones_b"][0:128, :], sqn[:, qsl],
                              start=True, stop=False)
                    te.matmul(p_nd[:, 0:512], ct["ones_b"][0:128, :], sqn2[:, qsl],
                              start=False, stop=True).inc("te")
                    sc.wait("te")
                    sc.wait("ve")
                    sc.activation(out=n_k[0:1, qsl], in_=p_nd[0:1, 0:512],
                                  func=AF.Sqrt, scale=b2[:]).inc("se")
                sc.mul(out=negc[:], in_=n_k[:], mul=-1.0).inc("se")
                ve.wait("se")
                ve.tensor_scalar_add(out=n_k[:], in0=dx0[s][:],
                                     scalar1=WS[g]).inc("ve")
                te.wait("ve")
                for qc in range(4):
                    qsl = slice(qc * 512, (qc + 1) * 512)
                    te.wait("se")
                    te.matmul(p_qv[:], ct["onesrow_f"][0:1, :], n_k[0:1, qsl],
                              start=True, stop=True).inc("te")
                    sc.wait("te")
                    sc.copy(out=dxrep[:, qsl], in_=p_qv[:]).inc("se")
                if getattr(ctx, "dbg", None) is not None and g == 0 and s == 0:
                    sy.wait("se")
                    sy.dma_start(out=ctx.dbg["dxrep"], in_=dxrep[:]).inc("dma", 16)
                    sy.dma_start(out=ctx.dbg["negc"], in_=negc[:]).inc("dma", 16)
                    sy.dma_start(out=ctx.dbg["dx0"], in_=dx0[0][:]).inc("dma", 16)
                for qh in range(2):
                    qof = qh * 1024
                    for kt in range(16):
                        te.wait("se")
                        te.wait("ve")
                        for qq in range(2):
                            qsl = slice(qof + qq * 512, qof + (qq + 1) * 512)
                            te.matmul(p_s[:], kop[0][:, kt * 128:(kt + 1) * 128],
                                      qop[0][:, qsl], start=True, stop=False)
                            te.matmul(p_s[:], kop[1][:, kt * 128:(kt + 1) * 128],
                                      qop[1][:, qsl], start=False, stop=False)
                            mi = te.matmul(p_s[:], ct["onesrow_b"][0:1, 0:128],
                                           negc[0:1, qsl], start=False, stop=True)
                            mi.inc("te")
                            sc.wait("te")
                            sc.activation(out=attnT[:, qq * 512:(qq + 1) * 512],
                                          in_=p_s[:], func=AF.Exp,
                                          scale=ESC).inc("se")
                            te.wait("se")
                        ve.wait("se")
                        c0 = 2048 - kt * 128 + qof
                        ve.tensor_tensor(out=mskt[:], in0=r3[:, c0:c0 + 1024],
                                         in1=dxrep[:, qof:qof + 1024], op=ALU.is_le)
                        if getattr(ctx, "dbg", None) is not None and g == 0 and s == 0 and qh == 0 and kt == 0:
                            ve.tensor_copy(out=ctx.dbg_msk_sb[:], in_=mskt[:])
                            ve.tensor_copy(out=ctx.dbg_att_sb[:], in_=attnT[:]).inc("ve")
                            sy.wait("ve")
                            sy.dma_start(out=ctx.dbg["msk"], in_=ctx.dbg_msk_sb[:]).inc("dma", 16)
                            sy.dma_start(out=ctx.dbg["att"], in_=ctx.dbg_att_sb[:]).inc("dma", 16)
                        ve.tensor_mul(out=attnT[:], in0=attnT[:], in1=mskt[:]).inc("ve")
                        te.wait("ve")
                        for qq in range(2):
                            asl = slice(qq * 512, (qq + 1) * 512)
                            psl = slice(qq * 512, (qq + 1) * 512)
                            te.matmul(p_oa[:, psl], vop[kt][:, 0:128], attnT[:, asl],
                                      start=(kt == 0), stop=(kt == 15))
                            te.matmul(p_ob[:, psl], vop[kt][:, 128:256], attnT[:, asl],
                                      start=(kt == 0), stop=(kt == 15))
                            mi = te.matmul(p_nd[:, psl], ct["ones_b"][0:128, :], attnT[:, asl],
                                           start=(kt == 0), stop=(kt == 15))
                        mi.inc("te")
                    ve.wait("te")
                    if getattr(ctx, "dbg", None) is not None and g == 0 and s == 0 and qh == 0:
                        ve.tensor_copy(out=ctx.dbg_msk_sb[0:1, :], in_=p_nd[:]).inc("ve")
                        sy.wait("ve")
                        sy.dma_start(out=ctx.dbg["den"], in_=ctx.dbg_msk_sb[0:1, :]).inc("dma", 16)
                    ve.reciprocal(out=n_k[0:1, 0:1024], in_=p_nd[:]).inc("ve")
                    te.wait("ve")
                    for qq in range(2):
                        qsl = slice(qq * 512, (qq + 1) * 512)
                        te.wait("se")
                        te.matmul(p_qv[:], ct["onesrow_f"][0:1, :], n_k[0:1, qsl],
                                  start=True, stop=True).inc("te")
                        sc.wait("te")
                        sc.copy(out=drep[:, qsl], in_=p_qv[:]).inc("se")
                    for half, p_o in ((0, p_oa), (1, p_ob)):
                        jrow = (g * 2 + half) * 128
                        tg = slice(ts0 + qof, ts0 + qof + 1024)
                        sy.wait("ve")
                        sy.dma_start(out=xt2[:], in_=x_hbm[jrow:jrow + 128, tg]).inc("dma", 16)
                        ve.wait("se")
                        ve.tensor_mul(out=yt2[:], in0=p_o[:], in1=drep[:])
                        ve.wait("dma")
                        ve.tensor_add(out=yt2[:], in0=yt2[:], in1=xt2[:]).inc("ve")
                        sy.wait("ve")
                        sy.dma_start(out=y_hbm[jrow:jrow + 128, tg], in_=yt2[:]).inc("dst", 16)
                        ve.wait("dst")
                    te.wait("ve")
        ctx.fence()


def build_l1a(debug=False):
    nc = bass.Bass("TRN2", target_bir_lowering=False)
    ctx = Ctx(nc)
    ctx.consts = declare_consts(nc)
    if debug:
        ctx.dbg = {
            "msk": nc.dram_tensor("dbg_msk", [128, 1024], F32, kind="ExternalOutput").ap(),
            "att": nc.dram_tensor("dbg_att", [128, 1024], F32, kind="ExternalOutput").ap(),
            "dxrep": nc.dram_tensor("dbg_dxrep", [128, L], F32, kind="ExternalOutput").ap(),
            "negc": nc.dram_tensor("dbg_negc", [1, L], BF16, kind="ExternalOutput").ap(),
            "dx0": nc.dram_tensor("dbg_dx0", [1, L], F32, kind="ExternalOutput").ap(),
            "den": nc.dram_tensor("dbg_den", [1, 1024], F32, kind="ExternalOutput").ap(),
        }
    else:
        ctx.dbg = None
    x_hbm = nc.dram_tensor("x", [D, T], F32, kind="ExternalInput").ap()
    w = {}
    w["qkvwT"] = nc.dram_tensor("qkvwT", [1024, 3072], F32, kind="ExternalInput").ap()
    w["qkvb_row"] = nc.dram_tensor("qkvb_row", [1, 3072], F32, kind="ExternalInput").ap()
    w["qkvb_cols"] = nc.dram_tensor("qkvb_cols", [128, 24], F32, kind="ExternalInput").ap()
    w["offwT"] = nc.dram_tensor("offwT", [1024, 256], F32, kind="ExternalInput").ap()
    w["lin2wT"] = nc.dram_tensor("lin2wT", [128, 2], F32, kind="ExternalInput").ap()
    w["lin2_b"] = nc.dram_tensor("lin2_b", [1, 1], F32, kind="ExternalInput").ap()
    w["r3"] = nc.dram_tensor("r3", [128, 4096], F32, kind="ExternalInput").ap()
    y_hbm = nc.dram_tensor("y", [D, T], F32, kind="ExternalOutput").ap()

    from contextlib import ExitStack
    with ExitStack() as es:
        def sb(name, shape, dtype):
            return es.enter_context(nc.sbuf_tensor(name, shape, dtype))
        ct = load_consts(ctx, sb)
        if ctx.dbg is not None:
            ctx.dbg_msk_sb = sb("dbg_msk_sb", [128, 1024], F32)
            ctx.dbg_att_sb = sb("dbg_att_sb", [128, 1024], F32)
        emit_l1a(ctx, ct, x_hbm, w, y_hbm)
    ctx.sy.wait("dma")
    ctx.sy.wait("dst")
    ctx.emit()
    return nc


def l1a_host_inputs(inputs):
    m = {}
    m["qkvwT"] = np.ascontiguousarray(np.asarray(inputs["L1_attn_a_qkv_w"], np.float32).T)
    qb = np.asarray(inputs["L1_attn_a_qkv_b"], np.float32)
    m["qkvb_row"] = np.ascontiguousarray(qb.reshape(1, 3072))
    m["qkvb_cols"] = np.ascontiguousarray(qb.reshape(24, 128).T)
    m["offwT"] = np.ascontiguousarray(np.asarray(inputs["L1_attn_a_off_w"], np.float32).T)
    m["lin2wT"] = np.ascontiguousarray(np.asarray(inputs["L1_attn_a_lin2_w"], np.float32).reshape(2, 128).T)
    m["lin2_b"] = np.ascontiguousarray(np.asarray(inputs["L1_attn_a_lin2_b"], np.float32).reshape(1, 1))
    p = np.arange(128, dtype=np.float64)[:, None]
    c = np.arange(4096, dtype=np.float64)[None, :]
    v = p - c + 2048.0
    r3 = np.where(v < 0, 2e18, v).astype(np.float32)
    m["r3"] = r3
    return m



def emit_moe(ctx, ct, x_hbm, w, y_hbm, scr, tag="", skip_wconv=False):
    from contextlib import ExitStack
    nc = ctx.nc
    sy, te, ve, sc, gp = ctx.sy, ctx.te, ctx.ve, ctx.sc, ctx.gp

    with ExitStack() as es:
        def sb(name, shape, dtype):
            return es.enter_context(nc.sbuf_tensor(name + tag, shape, dtype))

        maskT = sb("mo_maskT", [8, T], F32)
        wg_sb = sb("mo_wg", [128, 64], BF16)
        bgrep = sb("mo_bgrep", [128, 8], F32)
        b1c = sb("mo_b1c", [128, 256], F32)
        b2e = sb("mo_b2e", [1, 1024], BF16)
        identf = sb("mo_identf", [128, 128], F32)
        onesrow512 = sb("mo_ones512", [1, 512], BF16)

        with ExitStack() as p0:
            def sb0(n, sh, d):
                return p0.enter_context(nc.sbuf_tensor(n + tag, sh, d))
            def pb0(n, sh, d):
                return p0.enter_context(nc.psum_tensor(n + tag, sh, d))
            xhat = [sb0(f"mo_xhat{i}", [128, T], BF16) for i in range(8)]
            emit_ln(ctx, ct, x_hbm, xhat, sb0, pb0)

            wst = sb0("mo_wst", [128, 4096], F32)
            wstb = sb0("mo_wstb", [128, 4096], BF16)
            lg = sb0("mo_lg", [128, 8], F32)
            Et = sb0("mo_Et", [128, 8], F32)
            mrow = sb0("mo_m", [128, 1], F32)
            negm = sb0("mo_negm", [128, 1], F32)
            srow = sb0("mo_s", [128, 1], F32)
            p_g = pb0("mo_p_g", [128, 8], F32)
            if getattr(ctx, "dbg_lg", None) is not None:
                ctx.dbg_lg_sb = sb0("dbg_lg_sb", [128, 8], F32)
                ctx.dbg_m_sb = sb0("dbg_m_sb", [128, 1], F32)
            p_tr = pb0("mo_p_tr", [128, 128], F32)

            sy.dma_start(out=identf[:], in_=w["identf"]).inc("dma", 16)
            sy.dma_start(out=onesrow512[:], in_=w["ones512b"]).inc("dma", 16)
            sy.dma_start(out=b1c[:], in_=w["b1c"]).inc("dma", 16)
            for kt in range(8):
                gp.dma_start(out=wg_sb[:, kt * 8:(kt + 1) * 8],
                             in_=w["Wg"][kt * 128:(kt + 1) * 128, :]).inc("dma", 16)
            bgrow = sb0("mo_bgrow", [1, 8], F32)
            sy.dma_start(out=bgrow[:], in_=w["bg"]).inc("dma", 16)
            te.wait("dma")
            te.matmul(p_g[:], ct["onesrow_f"][0:1, :], bgrow[:],
                      start=True, stop=True).inc("te")
            sc.wait("te")
            sc.copy(out=bgrep[:], in_=p_g[:]).inc("se")
            te.wait("se")

            for i in range(8):
                sy.wait("ve")
                sy.dma_start(out=scr["xh"][i * 128:(i + 1) * 128, :],
                             in_=xhat[i][:]).inc("dst", 16)
            for ts in range(T // 128):
                tsl = slice(ts * 128, (ts + 1) * 128)
                te.wait("ve")
                te.wait("se")
                for kt in range(8):
                    mi = te.matmul(p_g[:], xhat[kt][:, tsl], wg_sb[:, kt * 8:(kt + 1) * 8],
                                   start=(kt == 0), stop=(kt == 7))
                mi.inc("te")
                ve.wait("te")
                ve.wait("se")
                ve.tensor_add(out=lg[:], in0=p_g[:], in1=bgrep[:])
                ctx.vsep()
                ve.reduce_max(out=mrow[:], in_=lg[:], axis=AX.X)
                ctx.vsep()
                ve.tensor_scalar_mul(out=negm[:], in0=mrow[:], scalar1=-1.0).inc("ve")
                if getattr(ctx, "dbg_lg", None) is not None and ts == 0:
                    ve.tensor_copy(out=ctx.dbg_lg_sb[:], in_=lg[:])
                    ve.tensor_copy(out=ctx.dbg_m_sb[:], in_=mrow[:]).inc("ve")
                sc.wait("ve")
                sc.activation(out=Et[:], in_=lg[:], func=AF.Exp, bias=negm[:],
                              accum_out=srow[:]).inc("se")
                ve.wait("se")
                ve.reciprocal(out=srow[:], in_=srow[:])
                ctx.vsep()
                ve.tensor_scalar(out=lg[:], in0=lg[:], scalar1=mrow[:],
                                 scalar2=srow[:], op0=ALU.is_ge,
                                 op1=ALU.mult).inc("ve")
                te.wait("ve")
                te.transpose(p_tr[0:8, :], lg[:, 0:8] if False else lg[:],
                             identf[:]).inc("te")
                sc.wait("te")
                sc.copy(out=maskT[:, tsl], in_=p_tr[0:8, 0:128]).inc("se")
                te.wait("se")
                ctx.fence()
            if getattr(ctx, "dbg_maskT", None) is not None:
                sy.wait("se")
                sy.wait("ve")
                sy.dma_start(out=ctx.dbg_maskT, in_=maskT[:]).inc("dst", 16)
                sy.dma_start(out=ctx.dbg_lg, in_=ctx.dbg_lg_sb[:]).inc("dst", 16)
                sy.dma_start(out=ctx.dbg_m, in_=ctx.dbg_m_sb[:]).inc("dst", 16)
            ctx.fence()

        if not skip_wconv:
            with ExitStack() as p1:
                def sb1(n, sh, d):
                    return p1.enter_context(nc.sbuf_tensor(n + tag, sh, d))
                wst = sb1("mo_wst1", [128, 4096], F32)
                wstb = sb1("mo_wstb1", [128, 4096], BF16)
                for e in range(8):
                    for i in range(8):
                        sy.wait("ve")
                        sy.dma_start(out=wst[:], in_=w["W1"][e, i * 128:(i + 1) * 128, :]).inc("dma", 16)
                        ve.wait("dma")
                        ve.tensor_copy(out=wstb[:], in_=wst[:]).inc("ve")
                        sy.wait("ve")
                        sy.dma_start(out=scr["w1c"][(e * 8 + i) * 128:(e * 8 + i + 1) * 128, :],
                                     in_=wstb[:]).inc("dst", 16)
                    for i in range(8):
                        sy.wait("ve")
                        sy.dma_start(out=wst[:], in_=bass.AP(w["W2"].tensor,
                                     (e * 4096 + i * 512) * 1024,
                                     [[1024, 128], [128 * 1024, 4], [1, 1024]])).inc("dma", 16)
                        ve.wait("dma")
                        ve.tensor_copy(out=wstb[:], in_=wst[:]).inc("ve")
                        sy.wait("ve")
                        sy.dma_start(out=scr["w2c"][(e * 8 + i) * 128:(e * 8 + i + 1) * 128, :],
                                     in_=wstb[:]).inc("dst", 16)
                ctx.fence()

        sy.dma_start(out=y_hbm, in_=x_hbm).inc("dst", 16)
        ctx.fence()

        with ExitStack() as p3:
            def sb3(n, sh, d):
                return p3.enter_context(nc.sbuf_tensor(n + tag, sh, d))
            def pb3(n, sh, d):
                return p3.enter_context(nc.psum_tensor(n + tag, sh, d))
            w1_sb = [sb3(f"mo_w1_{i}", [128, 4096], BF16) for i in range(8)]
            w2_sb = [sb3(f"mo_w2_{i}", [128, 1024], BF16) for i in range(32)]
            xb = [sb3(f"mo_xb{i}", [128, 512], BF16) for i in range(8)]
            h_act = [sb3(f"mo_h{i}", [128, 512], BF16) for i in range(32)]
            mwrep = sb3("mo_mwrep", [128, 512], F32)
            yt = sb3("mo_yt", [128, 512], F32)
            ot = sb3("mo_ot", [128, 512], F32)
            p_h = [pb3(f"mo_p_h{i}", [128, 512], F32) for i in range(2)]
            p_o = [pb3(f"mo_p_o{i}", [128, 512], F32) for i in range(2)]
            p_mw = pb3("mo_p_mw", [128, 512], F32)

            for e in range(8):
                gp.wait("te")
                gp.dma_start(out=b2e[:],
                             in_=w["b2row"][0:1, e * 1024:(e + 1) * 1024]).inc("dma", 16)
                sy.wait("te")
                for i in range(8):
                    sy.dma_start(out=w1_sb[i][:],
                                 in_=scr["w1c"][(e * 8 + i) * 128:(e * 8 + i + 1) * 128,
                                                :]).inc("dma", 16)
                for i in range(32):
                    sy.dma_start(out=w2_sb[i][:],
                                 in_=scr["w2c"][(e * 8 + i // 4) * 128 + (i % 4) * 0:
                                                (e * 8 + i // 4) * 128 + 128,
                                                (i % 4) * 1024:(i % 4) * 1024 + 1024]
                                 ).inc("dma", 16)
                hsilu = [0, 0]
                for tblk in range(8):
                    tbl = slice(tblk * 512, (tblk + 1) * 512)
                    sy.wait("te")
                    for kt in range(8):
                        sy.dma_start(out=xb[kt][:],
                                     in_=scr["xh"][kt * 128:(kt + 1) * 128, tbl]).inc("dma", 16)
                    te.wait("dma")
                    te.wait("se")
                    te.wait("ve")
                    te.matmul(p_mw[:], w["sel8_sb"][:, e * 128:(e + 1) * 128],
                              maskT[:, tbl], start=True, stop=True).inc("te")
                    sc.wait("te")
                    sc.copy(out=mwrep[:], in_=p_mw[:]).inc("se")
                    te.wait("dma")
                    for hg in range(32):
                        pb = p_h[hg % 2]
                        if hg >= 2:
                            te.wait("se", hsilu[hg % 2])
                        for kt in range(8):
                            mi = te.matmul(pb[:], w1_sb[kt][:, hg * 128:(hg + 1) * 128],
                                           xb[kt][:], start=(kt == 0), stop=(kt == 7))
                        mi.inc("te")
                        sc.wait("te")
                        hsilu[hg % 2] = sc.activation(
                            out=h_act[hg][:], in_=pb[:], func=AF.Silu,
                            bias=b1c[:, e * 32 + hg: e * 32 + hg + 1]).inc("se")
                    te.wait("se")
                    for jt in range(8):
                        po = p_o[jt % 2]
                        te.wait("ve")
                        for i in range(32):
                            te.matmul(po[:], w2_sb[i][:, jt * 128:(jt + 1) * 128],
                                      h_act[i][:], start=(i == 0), stop=False)
                        mi = te.matmul(po[:], b2e[0:1, jt * 128:(jt + 1) * 128],
                                       onesrow512[:], start=False, stop=True)
                        mi.inc("te")
                        sy.wait("ve")
                        sy.dma_start(out=yt[:] if False else yt[:],
                                     in_=y_hbm[jt * 128:(jt + 1) * 128, tbl]).inc("dma", 16)
                        ve.wait("te")
                        ve.wait("se")
                        ve.tensor_mul(out=ot[:], in0=po[:], in1=mwrep[:]).inc("ve")
                        ve.wait("dma")
                        ve.tensor_add(out=yt[:], in0=yt[:], in1=ot[:]).inc("ve")
                        sy.wait("ve")
                        sy.dma_start(out=y_hbm[jt * 128:(jt + 1) * 128, tbl],
                                     in_=yt[:]).inc("dst", 16)
                        ve.wait("dst")
            ctx.fence()


def build_moe(debug=False):
    nc = bass.Bass("TRN2", target_bir_lowering=False)
    ctx = Ctx(nc)
    ctx.consts = declare_consts(nc)
    x_hbm = nc.dram_tensor("x", [D, T], F32, kind="ExternalInput").ap()
    w = {}
    w["Wg"] = nc.dram_tensor("Wg", [1024, 8], F32, kind="ExternalInput").ap()
    w["bg"] = nc.dram_tensor("bg", [1, 8], F32, kind="ExternalInput").ap()
    w["W1"] = nc.dram_tensor("W1", [8, 1024, 4096], F32, kind="ExternalInput").ap()
    w["b1c"] = nc.dram_tensor("b1c", [128, 256], F32, kind="ExternalInput").ap()
    w["W2"] = nc.dram_tensor("W2", [8, 4096, 1024], F32, kind="ExternalInput").ap()
    w["b2row"] = nc.dram_tensor("b2row", [1, 8192], F32, kind="ExternalInput").ap()
    w["identf"] = nc.dram_tensor("identf", [128, 128], F32, kind="ExternalInput").ap()
    w["ones512b"] = nc.dram_tensor("ones512b", [1, 512], BF16, kind="ExternalInput").ap()
    w["sel8"] = nc.dram_tensor("sel8", [8, 1024], F32, kind="ExternalInput").ap()
    y_hbm = nc.dram_tensor("y", [D, T], F32, kind="ExternalOutput").ap()
    scr = {}
    kind = "ExternalOutput" if debug else "Internal"
    scr["xh"] = nc.dram_tensor("scr_xh", [1024, T], BF16, kind=kind).ap()
    scr["w1c"] = nc.dram_tensor("scr_w1c", [8192, 4096], BF16, kind=kind).ap()
    scr["w2c"] = nc.dram_tensor("scr_w2c", [8192, 4096], BF16, kind=kind).ap()
    if debug:
        ctx.dbg_maskT = nc.dram_tensor("dbg_maskT", [8, T], F32, kind="ExternalOutput").ap()
        ctx.dbg_lg = nc.dram_tensor("dbg_lg", [128, 8], F32, kind="ExternalOutput").ap()
        ctx.dbg_m = nc.dram_tensor("dbg_m", [128, 1], F32, kind="ExternalOutput").ap()

    from contextlib import ExitStack
    with ExitStack() as es:
        def sb(name, shape, dtype):
            return es.enter_context(nc.sbuf_tensor(name, shape, dtype))
        ct = load_consts(ctx, sb)
        sel8_sb = sb("mo_sel8", [8, 1024], F32)
        ctx.sy.dma_start(out=sel8_sb[:], in_=w["sel8"]).inc("dma", 16)
        w["sel8_sb"] = sel8_sb
        emit_moe(ctx, ct, x_hbm, w, y_hbm, scr)
    ctx.sy.wait("dma")
    ctx.sy.wait("dst")
    ctx.emit()
    return nc


def moe_host_inputs(inputs):
    m = {}
    m["Wg"] = np.ascontiguousarray(np.asarray(inputs["L1_ffn_Wg"], np.float32))
    m["bg"] = np.ascontiguousarray(np.asarray(inputs["L1_ffn_bg"], np.float32).reshape(1, 8))
    m["W1"] = np.ascontiguousarray(np.asarray(inputs["L1_ffn_W1"], np.float32))
    m["b1c"] = np.ascontiguousarray(np.asarray(inputs["L1_ffn_b1"], np.float32)
                                    .reshape(8, 32, 128).transpose(2, 0, 1).reshape(128, 256))
    m["W2"] = np.ascontiguousarray(np.asarray(inputs["L1_ffn_W2"], np.float32))
    m["b2row"] = np.ascontiguousarray(np.asarray(inputs["L1_ffn_b2"], np.float32).reshape(1, 8192))
    m["identf"] = np.eye(128, dtype=np.float32)
    m["ones512b"] = np.ones((1, 512), BF16NP)
    sel8 = np.zeros((8, 1024), np.float32)
    for e in range(8):
        sel8[e, e * 128:(e + 1) * 128] = 1.0
    m["sel8"] = sel8
    return m



def emit_l1b(ctx, ct, x_hbm, w, y_hbm, scr):
    from contextlib import ExitStack
    nc = ctx.nc
    sy, te, ve, sc, gp = ctx.sy, ctx.te, ctx.ve, ctx.sc, ctx.gp
    QSC = 1.0 / 32.0

    with ExitStack() as es:
        def sb(name, shape, dtype):
            return es.enter_context(nc.sbuf_tensor(name, shape, dtype))

        xhat = [sb(f"b1_xhat{i}", [128, T], BF16) for i in range(8)]
        with ExitStack() as ln_es:
            emit_ln(ctx, ct, x_hbm, xhat,
                    lambda n, s, d: ln_es.enter_context(nc.sbuf_tensor(n, s, d)),
                    lambda n, s, d: ln_es.enter_context(nc.psum_tensor(n, s, d)))

        with ExitStack() as ap:
            def sba(n, sh, d):
                return ap.enter_context(nc.sbuf_tensor(n, sh, d))
            def pba(n, sh, d):
                return ap.enter_context(nc.psum_tensor(n, sh, d))
            wstage = sba("b1_wstage", [128, 512], F32)
            wq_sb = sba("b1_wq", [128, 512], BF16)
            wk_sb = sba("b1_wk", [128, 512], BF16)
            wv_sb = sba("b1_wv", [128, 512], BF16)
            qop = sba("b1_qop", [65, L], BF16)
            kop = sba("b1_kop", [65, L], BF16)
            vop_all = sba("b1_vop", [128, 16 * 65], BF16)
            sqn = sba("b1_sqn", [64, L], BF16)
            n_k = sba("b1_nk", [1, L], F32)
            b2 = sba("b1_b2", [1, 1], F32)
            c_row = sba("b1_crow", [1, L], F32)
            dinv = sba("b1_dinv", [1, 1024], F32)
            attnT = [sba(f"b1_attnT{i}", [128, 512], BF16) for i in range(4)]
            o_sb = sba("b1_osb", [65, 1024], F32)
            drep = sba("b1_drep", [64, 1024], F32)
            gstage = sba("b1_gstage", [128, 128], F32)
            gbf = sba("b1_gbf", [128, 128], BF16)
            adg = sba("b1_adg", [128, 128], BF16)
            tblk = sba("b1_tblk", [128, 31 * 128], BF16)
            hot = sba("b1_hot", [64, 1024], F32)
            hot_bf = sba("b1_hotbf", [64, 1024], BF16)

            p_qv = pba("b1_p_qv", [128, 512], F32)
            p_q2 = pba("b1_p_q2", [128, 512], F32)
            p_rot = [p_qv, p_q2]
            p_s = [pba("b1_p_sc0", [128, 512], F32), pba("b1_p_sc1", [128, 512], F32)]
            p_o = pba("b1_p_o", [65, 1024], F32)
            p_b = pba("b1_p_b", [64, 1024], F32)
            p_dr = p_qv
            p_n = p_qv

            vop = [vop_all[:, tt * 65:(tt + 1) * 65] for tt in range(16)]
            gp.memset(kop[64:65, :], 1.0)
            for tt in range(16):
                g = gp.memset(vop_all[:, tt * 65 + 64: tt * 65 + 65], 1.0)
            g.inc("gp")
            te.wait("gp")
            sc.wait("gp")
            sy.dma_start(out=adg[:], in_=w["antidiag"]).inc("dma", 16)

            for h in range(16):
                for dlt in range(31):
                    c0 = 128 * (dlt - 15) + 1920
                    sy.wait("ve")
                    src = bass.AP(w["tableT"].tensor, h * 4095 + c0, [[1, 128], [1, 128]])
                    sy.dma_start(out=gstage[:], in_=src).inc("dma", 16)
                    ve.wait("dma")
                    ve.wait("te")
                    ve.tensor_copy(out=gbf[:], in_=gstage[:]).inc("ve")
                    te.wait("ve")
                    te.wait("se")
                    te.matmul(p_dr[:, 0:128], adg[:], gbf[:], start=True,
                              stop=True).inc("te")
                    sc.wait("te")
                    sc.copy(out=tblk[:, dlt * 128:(dlt + 1) * 128],
                            in_=p_dr[:, 0:128]).inc("se")
                for wdst, whbm in ((wq_sb, w["wqT"]), (wk_sb, w["wkT"]), (wv_sb, w["wvT"])):
                    sy.wait("te")
                    sy.wait("ve")
                    src = bass.AP(whbm.tensor, h * 64,
                                  [[1024, 128], [128 * 1024, 8], [1, 64]])
                    sy.dma_start(out=wstage[:], in_=src).inc("dma", 16)
                    ve.wait("dma")
                    ve.wait("te")
                    ve.tensor_copy(out=wdst[:], in_=wstage[:]).inc("ve")
                wrdy = ctx.cnt["ve"]
                for s in range(SEQ):
                    ts0 = s * L
                    te.wait("ve", wrdy)
                    rot_free = [0, 0]
                    ri = 0
                    jobs = []
                    for qc in range(4):
                        jobs.append(("q", qc))
                        jobs.append(("k", qc))
                    for tt in range(16):
                        jobs.append(("v", tt))
                    for kind, idx in jobs:
                        pbuf = p_rot[ri]
                        te.wait("se", rot_free[ri])
                        if kind in ("q", "k"):
                            qsl = slice(ts0 + idx * 512, ts0 + (idx + 1) * 512)
                            lsl = slice(idx * 512, (idx + 1) * 512)
                            wsb = wq_sb if kind == "q" else wk_sb
                            for kt in range(8):
                                mi = te.matmul(pbuf[0:64, :], wsb[:, kt * 64:(kt + 1) * 64],
                                               xhat[kt][:, qsl], start=(kt == 0), stop=(kt == 7))
                            mi.inc("te")
                            sc.wait("te")
                            dst = qop if kind == "q" else kop
                            rot_free[ri] = sc.activation(
                                out=dst[0:64, lsl], in_=pbuf[0:64, :], func=AF.Copy,
                                scale=(QSC if kind == "q" else 1.0)).inc("se")
                        else:
                            tsl = slice(ts0 + idx * 128, ts0 + (idx + 1) * 128)
                            for kt in range(8):
                                mi = te.matmul(pbuf[:, 0:64], xhat[kt][:, tsl],
                                               wv_sb[:, kt * 64:(kt + 1) * 64],
                                               start=(kt == 0), stop=(kt == 7))
                            mi.inc("te")
                            sc.wait("te")
                            rot_free[ri] = sc.copy(out=vop[idx][:, 0:64],
                                                   in_=pbuf[:, 0:64]).inc("se")
                        ri = (ri + 1) % 2
                    sc.activation(out=sqn[:], in_=kop[0:64, :], func=AF.Square).inc("se")
                    te.wait("se")
                    for qc in range(4):
                        te.wait("ve")
                        te.matmul(p_n[0:1, 0:512], ct["ones_b"][0:64, :],
                                  sqn[:, qc * 512:(qc + 1) * 512],
                                  start=True, stop=True).inc("te")
                        ve.wait("te")
                        ve.tensor_copy(out=n_k[0:1, qc * 512:(qc + 1) * 512],
                                       in_=p_n[0:1, 0:512]).inc("ve")
                    ctx.vsep()
                    ve.reduce_max(out=b2[:], in_=n_k[:], axis=AX.X).inc("ve")
                    sc.wait("te")
                    sc.activation(out=sqn[:], in_=qop[0:64, :], func=AF.Square).inc("se")
                    te.wait("se")
                    te.wait("ve")
                    for qc in range(4):
                        te.wait("se")
                        te.matmul(p_n[0:1, 0:512], ct["ones_b"][0:64, :],
                                  sqn[:, qc * 512:(qc + 1) * 512],
                                  start=True, stop=True).inc("te")
                        sc.wait("te")
                        sc.wait("ve")
                        sc.activation(out=c_row[0:1, qc * 512:(qc + 1) * 512],
                                      in_=p_n[0:1, 0:512], func=AF.Sqrt,
                                      scale=b2[:]).inc("se")
                    sc.mul(out=qop[64:65, :], in_=c_row[:], mul=-1.0).inc("se")
                    p_sx = [p_s[0], p_s[1], p_q2, p_qv]
                    exp_cnt = [0, 0, 0, 0]
                    pv_cnt = [0, 0, 0, 0]
                    cbase = ctx.cnt.get("se", 0)
                    for qh in range(2):
                        for kt in range(16):
                            for qq in range(2):
                                buf = (kt * 2 + qq) % 4
                                te.wait("se", cbase)
                                te.wait("se", exp_cnt[buf])
                                te.matmul(p_sx[buf][:],
                                          kop[:, kt * 128:(kt + 1) * 128],
                                          qop[:, qh * 1024 + qq * 512:
                                              qh * 1024 + (qq + 1) * 512],
                                          start=True, stop=True).inc("te")
                                tmm = ctx.cnt["te"]
                                sc.wait("te", pv_cnt[buf])
                                sc.wait("te", tmm)
                                exp_cnt[buf] = sc.activation(out=attnT[buf][:],
                                                             in_=p_sx[buf][:],
                                                             func=AF.Exp).inc("se")
                                te.wait("se", exp_cnt[buf])
                                mi = te.matmul(p_o[:, qq * 512:(qq + 1) * 512],
                                               vop[kt][:, 0:65],
                                               attnT[buf][:],
                                               start=(kt == 0), stop=(kt == 15))
                                pv_cnt[buf] = mi.inc("te")
                            for qb in range(8):
                                dlt = qh * 8 + qb - kt
                                mi = te.matmul(p_b[:, qb * 128:(qb + 1) * 128],
                                               vop[kt][:, 0:64],
                                               tblk[:, (dlt + 15) * 128:(dlt + 16) * 128],
                                               start=(kt == 0), stop=(kt == 15))
                        mi.inc("te")
                        ve.wait("te")
                        ve.reciprocal(out=dinv[:], in_=p_o[64:65, :]).inc("ve")
                        te.wait("ve")
                        te.matmul(p_dr[:], ct["onesrow_f"][0:1, :], dinv[0:1, 0:512],
                                  start=True, stop=True).inc("te")
                        sc.wait("te")
                        sc.copy(out=drep[:, 0:512], in_=p_dr[0:64, :]).inc("se")
                        te.wait("se")
                        te.matmul(p_dr[:], ct["onesrow_f"][0:1, :],
                                  dinv[0:1, 512:1024], start=True, stop=True).inc("te")
                        sc.wait("te")
                        sc.copy(out=drep[:, 512:1024], in_=p_dr[0:64, :])
                        sc.copy(out=o_sb[:], in_=p_o[:]).inc("se")
                        ve.wait("se")
                        ve.tensor_mul(out=hot[:], in0=o_sb[0:64, :], in1=drep[:])
                        ve.tensor_add(out=hot_bf[:], in0=hot[:], in1=p_b[:]).inc("ve")
                        sy.wait("ve")
                        sy.dma_start(out=scr["ho"][h * 64:(h + 1) * 64,
                                                   ts0 + qh * 1024: ts0 + (qh + 1) * 1024],
                                     in_=hot_bf[:]).inc("dst", 16)
                        ve.wait("dst")
                        te.wait("ve")
            ctx.fence()

        with ExitStack() as lp:
            def sbl(n, sh, d):
                return lp.enter_context(nc.sbuf_tensor(n, sh, d))
            def pbl(n, sh, d):
                return lp.enter_context(nc.psum_tensor(n, sh, d))
            hot2 = [sbl(f"b1_ho{i}", [128, 512], BF16) for i in range(8)]
            sq8 = sbl("b1_sq8", [128, 4096], BF16)
            s_mean = sbl("b1_mean", [1, 512], F32)
            s_var = sbl("b1_var", [1, 512], F32)
            s_rstd = sbl("b1_rstd", [1, 512], F32)
            s_mr = sbl("b1_mr", [1, 512], F32)
            gcol = sbl("b1_gcol", [128, 8], F32)
            bcol = sbl("b1_bcol", [128, 8], F32)
            xt = sbl("b1_xt", [128, 512], F32)
            yt = sbl("b1_yt", [128, 512], F32)
            p_s1 = pbl("b1_p_s1", [1, 512], F32)
            p_s2 = pbl("b1_p_s2", [1, 512], F32)
            p_mr = pbl("b1_p_mr", [128, 512], F32)
            p_rs = pbl("b1_p_rs", [128, 512], F32)
            sy.dma_start(out=gcol[:], in_=w["gcol"]).inc("dma", 16)
            sy.dma_start(out=bcol[:], in_=w["bcol"]).inc("dma", 16)
            for tc in range(8):
                sl = slice(tc * 512, (tc + 1) * 512)
                sy.wait("ve")
                for ft in range(8):
                    sy.dma_start(out=hot2[ft][:],
                                 in_=scr["ho"][ft * 128:(ft + 1) * 128, sl]).inc("dma", 16)
                te.wait("dma")
                te.wait("ve")
                for ft in range(8):
                    te.matmul(p_s1[:], ct["ones_b"][:], hot2[ft][:],
                              start=(ft == 0), stop=(ft == 7))
                sc.wait("dma")
                sc.wait("te")
                for ft in range(8):
                    si = sc.activation(out=sq8[:, ft * 512:(ft + 1) * 512],
                                       in_=hot2[ft][:], func=AF.Square)
                si.inc("se")
                te.wait("se")
                for ft in range(8):
                    mi = te.matmul(p_s2[:], ct["ones_b"][:],
                                   sq8[:, ft * 512:(ft + 1) * 512],
                                   start=(ft == 0), stop=(ft == 7))
                mi.inc("te")
                ve.wait("te")
                ve.tensor_scalar_mul(out=s_mean[:], in0=p_s1[:], scalar1=1.0 / D)
                ve.tensor_mul(out=s_var[:], in0=s_mean[:], in1=s_mean[:])
                ve.scalar_tensor_tensor(out=s_var[:], in0=p_s2[:], scalar=1.0 / D,
                                        in1=s_var[:], op0=ALU.mult,
                                        op1=ALU.subtract).inc("ve")
                sc.wait("ve")
                sc.activation(out=s_var[:], in_=s_var[:], func=AF.Ln, bias=ct["eps"])
                sc.activation(out=s_rstd[:], in_=s_var[:], func=AF.Exp,
                              scale=-0.5).inc("se")
                ve.wait("se")
                ve.tensor_mul(out=s_mr[:], in0=s_mean[:], in1=s_rstd[:]).inc("ve")
                te.wait("ve")
                te.matmul(p_mr[:], ct["onesrow_f"][0:1, :], s_mr[:], start=True, stop=True)
                te.matmul(p_rs[:], ct["onesrow_f"][0:1, :], s_rstd[:], start=True,
                          stop=True).inc("te")
                ve.wait("te")
                for ft in range(8):
                    sy.wait("ve")
                    sy.dma_start(out=xt[:], in_=x_hbm[ft * 128:(ft + 1) * 128, sl]).inc("dma", 16)
                    ve.wait("dma")
                    ve.tensor_mul(out=yt[:], in0=hot2[ft][:], in1=p_rs[:])
                    ve.tensor_sub(out=yt[:], in0=yt[:], in1=p_mr[:])
                    ve.tensor_scalar(out=yt[:], in0=yt[:], scalar1=gcol[:, ft:ft + 1],
                                     scalar2=None, op0=ALU.mult)
                    ve.tensor_scalar(out=yt[:], in0=yt[:], scalar1=bcol[:, ft:ft + 1],
                                     scalar2=None, op0=ALU.add)
                    ve.tensor_add(out=yt[:], in0=yt[:], in1=xt[:]).inc("ve")
                    sy.wait("ve")
                    sy.dma_start(out=y_hbm[ft * 128:(ft + 1) * 128, sl],
                                 in_=yt[:]).inc("dst", 16)
                    ve.wait("dst")
            ctx.fence()


def build_l1b():
    nc = bass.Bass("TRN2", target_bir_lowering=False)
    ctx = Ctx(nc)
    ctx.consts = declare_consts(nc)
    x_hbm = nc.dram_tensor("x", [D, T], F32, kind="ExternalInput").ap()
    w = {}
    w["wqT"] = nc.dram_tensor("wqT", [1024, 1024], F32, kind="ExternalInput").ap()
    w["wkT"] = nc.dram_tensor("wkT", [1024, 1024], F32, kind="ExternalInput").ap()
    w["wvT"] = nc.dram_tensor("wvT", [1024, 1024], F32, kind="ExternalInput").ap()
    w["tableT"] = nc.dram_tensor("tableT", [16, 4095], F32, kind="ExternalInput").ap()
    w["antidiag"] = nc.dram_tensor("antidiag", [128, 128], BF16, kind="ExternalInput").ap()
    w["gcol"] = nc.dram_tensor("gcol", [128, 8], F32, kind="ExternalInput").ap()
    w["bcol"] = nc.dram_tensor("bcol", [128, 8], F32, kind="ExternalInput").ap()
    y_hbm = nc.dram_tensor("y", [D, T], F32, kind="ExternalOutput").ap()
    scr = {"ho": nc.dram_tensor("scr_ho", [1024, T], BF16).ap()}

    from contextlib import ExitStack
    with ExitStack() as es:
        def sb(name, shape, dtype):
            return es.enter_context(nc.sbuf_tensor(name, shape, dtype))
        ct = load_consts(ctx, sb)
        emit_l1b(ctx, ct, x_hbm, w, y_hbm, scr)
    ctx.sy.wait("dma")
    ctx.sy.wait("dst")
    ctx.emit()
    return nc


def l1b_host_inputs(inputs):
    m = {}
    m["wqT"] = np.ascontiguousarray(np.asarray(inputs["L1_attn_b_Wq"], np.float32).T)
    m["wkT"] = np.ascontiguousarray(np.asarray(inputs["L1_attn_b_Wk"], np.float32).T)
    m["wvT"] = np.ascontiguousarray(np.asarray(inputs["L1_attn_b_Wv"], np.float32).T)
    tab = np.asarray(inputs["L1_attn_b_bias_table"], np.float32)
    idx = np.minimum(np.arange(4095), tab.shape[0] - 1)
    m["tableT"] = np.ascontiguousarray(tab[idx].T)
    m["antidiag"] = np.ascontiguousarray(np.eye(128, dtype=BF16NP)[::-1])
    m["gcol"] = np.ascontiguousarray(np.asarray(inputs["L1_attn_b_ln_gamma"], np.float32).reshape(8, 128).T)
    m["bcol"] = np.ascontiguousarray(np.asarray(inputs["L1_attn_b_ln_beta"], np.float32).reshape(8, 128).T)
    return m


def build_l0a(debug=False):
    nc = bass.Bass("TRN2", target_bir_lowering=False)
    ctx = Ctx(nc)
    ctx.dbg = debug
    ctx.consts = declare_consts(nc)
    if debug:
        ctx.dbg_xhat = nc.dram_tensor("dbg_xhat", [1024, T], BF16, kind="ExternalOutput").ap()
        ctx.dbg_qop = nc.dram_tensor("dbg_qop", [65, L], BF16, kind="ExternalOutput").ap()
        ctx.dbg_kop = nc.dram_tensor("dbg_kop", [65, L], BF16, kind="ExternalOutput").ap()
        ctx.dbg_att = nc.dram_tensor("dbg_att", [128, 1024], BF16, kind="ExternalOutput").ap()
        ctx.dbg_den = nc.dram_tensor("dbg_den", [1, 1024], F32, kind="ExternalOutput").ap()
        ctx.dbg_nk = nc.dram_tensor("dbg_nk", [1, L], F32, kind="ExternalOutput").ap()
        ctx.dbg_b2 = nc.dram_tensor("dbg_b2", [1, 1], F32, kind="ExternalOutput").ap()
        ctx.dbg_sqn = nc.dram_tensor("dbg_sqn", [64, L], BF16, kind="ExternalOutput").ap()
    x_hbm = nc.dram_tensor("x", [D, T], F32, kind="ExternalInput").ap()
    wqkv = nc.dram_tensor("wqkv", [16, 3, 1024, 64], F32, kind="ExternalInput").ap()
    wo = nc.dram_tensor("wo", [1024, 1024], F32, kind="ExternalInput").ap()
    y_hbm = nc.dram_tensor("y", [D, T], F32, kind="ExternalOutput").ap()

    from contextlib import ExitStack
    with ExitStack() as es:
        def sb(name, shape, dtype):
            return es.enter_context(nc.sbuf_tensor(name, shape, dtype))
        ct = load_consts(ctx, sb)
        emit_l0a(ctx, ct, x_hbm, wqkv, wo, y_hbm)
    ctx.sy.wait("dma")
    ctx.sy.wait("dst")
    ctx.emit()
    return nc



def build_fused():
    nc = bass.Bass("TRN2", target_bir_lowering=False)
    ctx = Ctx(nc)
    ctx.consts = declare_consts(nc)
    ctx.dbg = None

    x_hbm = nc.dram_tensor("x", [D, T], F32, kind="ExternalInput").ap()
    y_hbm = nc.dram_tensor("y", [D, T], F32, kind="ExternalOutput").ap()
    xi = [nc.dram_tensor(f"xi{i}", [D, T], F32).ap() for i in range(7)]

    wqkv = nc.dram_tensor("wqkv", [16, 3, 1024, 64], F32, kind="ExternalInput").ap()
    wo = nc.dram_tensor("wo", [1024, 1024], F32, kind="ExternalInput").ap()
    wgT = nc.dram_tensor("wgT", [1024, 3072], I32, kind="ExternalInput").ap()
    wuT = nc.dram_tensor("wuT", [1024, 3072], I32, kind="ExternalInput").ap()
    wdT = nc.dram_tensor("wdT", [3072, 1024], I32, kind="ExternalInput").ap()
    ytmp = nc.dram_tensor("ytmp", [D, T], F32).ap()
    wm = {}
    wm["win"] = nc.dram_tensor("win", [1024, 4096], F32, kind="ExternalInput").ap()
    wm["convw"] = nc.dram_tensor("convw", [128, 64], F32, kind="ExternalInput").ap()
    wm["convb"] = nc.dram_tensor("convb", [128, 16], F32, kind="ExternalInput").ap()
    wm["wxdt"] = nc.dram_tensor("wxdt", [2048, 64], F32, kind="ExternalInput").ap()
    wm["wxB"] = nc.dram_tensor("wxB", [2048, 16], F32, kind="ExternalInput").ap()
    wm["wxC"] = nc.dram_tensor("wxC", [2048, 16], F32, kind="ExternalInput").ap()
    wm["wdt"] = nc.dram_tensor("wdt", [64, 2048], F32, kind="ExternalInput").ap()
    wm["bdt"] = nc.dram_tensor("bdt", [128, 16], F32, kind="ExternalInput").ap()
    wm["acols"] = nc.dram_tensor("acols", [128, 256], F32, kind="ExternalInput").ap()
    wm["dcol"] = nc.dram_tensor("dcol", [128, 16], F32, kind="ExternalInput").ap()
    wm["wout"] = nc.dram_tensor("wout", [2048, 1024], F32, kind="ExternalInput").ap()
    wm["sel16"] = nc.dram_tensor("sel16", [16, 2048], BF16, kind="ExternalInput").ap()
    scr_m = {}
    scr_m["u"] = nc.dram_tensor("scr_u", [2048, T], BF16).ap()
    scr_m["sil"] = nc.dram_tensor("scr_sil", [2048, T], BF16).ap()
    scr_m["y3"] = nc.dram_tensor("scr_y3", [2048, T], BF16).ap()
    scr_m["dtb"] = nc.dram_tensor("scr_dtb", [64, T], BF16).ap()
    scr_m["bc"] = nc.dram_tensor("scr_bc", [32, T], BF16).ap()
    wa = {}
    wa["qkvwT"] = nc.dram_tensor("qkvwT", [1024, 3072], F32, kind="ExternalInput").ap()
    wa["qkvb_row"] = nc.dram_tensor("qkvb_row", [1, 3072], F32, kind="ExternalInput").ap()
    wa["qkvb_cols"] = nc.dram_tensor("qkvb_cols", [128, 24], F32, kind="ExternalInput").ap()
    wa["offwT"] = nc.dram_tensor("offwT", [1024, 256], F32, kind="ExternalInput").ap()
    wa["lin2wT"] = nc.dram_tensor("lin2wT", [128, 2], F32, kind="ExternalInput").ap()
    wa["lin2_b"] = nc.dram_tensor("lin2_b", [1, 1], F32, kind="ExternalInput").ap()
    wa["r3"] = nc.dram_tensor("r3", [128, 4096], F32, kind="ExternalInput").ap()
    we = {}
    we["Wg"] = nc.dram_tensor("Wg", [1024, 8], F32, kind="ExternalInput").ap()
    we["bg"] = nc.dram_tensor("bg", [1, 8], F32, kind="ExternalInput").ap()
    we["W1"] = nc.dram_tensor("W1", [8, 1024, 4096], F32, kind="ExternalInput").ap()
    we["b1c"] = nc.dram_tensor("b1c", [128, 256], F32, kind="ExternalInput").ap()
    we["W2"] = nc.dram_tensor("W2", [8, 4096, 1024], F32, kind="ExternalInput").ap()
    we["b2row"] = nc.dram_tensor("b2row", [1, 8192], F32, kind="ExternalInput").ap()
    we["identf"] = nc.dram_tensor("identf", [128, 128], F32, kind="ExternalInput").ap()
    we["ones512b"] = nc.dram_tensor("ones512b", [1, 512], BF16, kind="ExternalInput").ap()
    we["sel8"] = nc.dram_tensor("sel8", [8, 1024], F32, kind="ExternalInput").ap()
    scr_e = {}
    scr_e["xh"] = nc.dram_tensor("scr_xh", [1024, T], BF16).ap()
    scr_e["w1c"] = nc.dram_tensor("scr_w1c", [8192, 4096], BF16).ap()
    scr_e["w2c"] = nc.dram_tensor("scr_w2c", [8192, 4096], BF16).ap()
    wb = {}
    wb["wqT"] = nc.dram_tensor("wqT", [1024, 1024], F32, kind="ExternalInput").ap()
    wb["wkT"] = nc.dram_tensor("wkT", [1024, 1024], F32, kind="ExternalInput").ap()
    wb["wvT"] = nc.dram_tensor("wvT", [1024, 1024], F32, kind="ExternalInput").ap()
    wb["tableT"] = nc.dram_tensor("tableT", [16, 4095], F32, kind="ExternalInput").ap()
    wb["antidiag"] = nc.dram_tensor("antidiag", [128, 128], BF16, kind="ExternalInput").ap()
    wb["gcol"] = nc.dram_tensor("gcol", [128, 8], F32, kind="ExternalInput").ap()
    wb["bcol"] = nc.dram_tensor("bcol", [128, 8], F32, kind="ExternalInput").ap()
    scr_b = {"ho": nc.dram_tensor("scr_ho", [1024, T], BF16).ap()}

    from contextlib import ExitStack
    with ExitStack() as es:
        def sb(name, shape, dtype):
            return es.enter_context(nc.sbuf_tensor(name, shape, dtype))
        ct = load_consts(ctx, sb)
        sel8_sb = sb("mo_sel8", [8, 1024], F32)
        ctx.sy.dma_start(out=sel8_sb[:], in_=we["sel8"]).inc("dma", 16)
        we["sel8_sb"] = sel8_sb

        emit_l0a(ctx, ct, x_hbm, wqkv, wo, xi[0])
        ctx.fence()
        emit_ffn0(ctx, ct, xi[0], wgT, wuT, wdT, xi[1], ytmp, tag="A")
        ctx.fence()
        emit_mamba(ctx, ct, xi[1], wm, xi[2], scr_m)
        ctx.fence()
        emit_ffn0(ctx, ct, xi[2], wgT, wuT, wdT, xi[3], ytmp, tag="B")
        ctx.fence()
        emit_l1a(ctx, ct, xi[3], wa, xi[4])
        ctx.fence()
        emit_moe(ctx, ct, xi[4], we, xi[5], scr_e, tag="A")
        ctx.fence()
        emit_l1b(ctx, ct, xi[5], wb, xi[6], scr_b)
        ctx.fence()
        emit_moe(ctx, ct, xi[6], we, y_hbm, scr_e, tag="B", skip_wconv=True)
    ctx.sy.wait("dma")
    ctx.sy.wait("dst")
    ctx.emit()
    return nc


def fused_host_inputs(inputs):
    m = {}
    m.update(host_consts())
    m["wqkv"] = np.ascontiguousarray(np.asarray(inputs["L0_attn_a_w_qkv"], np.float32))
    m["wo"] = np.ascontiguousarray(np.asarray(inputs["L0_attn_a_w_o"], np.float32))
    m["wgT"] = np.ascontiguousarray(np.asarray(inputs["L0_ffn_w_gate"], np.int32).T)
    m["wuT"] = np.ascontiguousarray(np.asarray(inputs["L0_ffn_w_up"], np.int32).T)
    m["wdT"] = np.ascontiguousarray(np.asarray(inputs["L0_ffn_w_down"], np.int32).T)
    m.update(mamba_host_inputs(inputs))
    m.update(l1a_host_inputs(inputs))
    m.update(moe_host_inputs(inputs))
    m.update(l1b_host_inputs(inputs))
    return m


def _to_feat_tok(x):
    b, t, f = x.shape
    return np.ascontiguousarray(x.transpose(2, 0, 1).reshape(f, b * t))


def _from_feat_tok(y):
    return np.ascontiguousarray(np.asarray(y).reshape(D, SEQ, L).transpose(1, 2, 0))


import os
_DBGFIN = os.environ.get("KDBG", "") == "1"


def _ckfin(tag, xcur):
    if _DBGFIN:
        import sys as _s
        for c, a in enumerate(xcur):
            f = np.isfinite(a).mean()
            if f < 1.0:
                print(f"[KDBG] {tag} core{c} finite={f}", file=_s.stderr)
        print(f"[KDBG] {tag} checked", file=_s.stderr)


def kernel_multi(**inputs):
    consts = host_consts()
    x = np.asarray(inputs["L0_attn_a_inputs"], np.float32)

    nc = build_l0a()
    wqkv_h = np.ascontiguousarray(np.asarray(inputs["L0_attn_a_w_qkv"], np.float32))
    wo_h = np.ascontiguousarray(np.asarray(inputs["L0_attn_a_w_o"], np.float32))
    in_maps = []
    for c in range(NCORES):
        m = dict(consts)
        m["x"] = _to_feat_tok(x[2 * c:2 * c + 2])
        m["wqkv"] = wqkv_h
        m["wo"] = wo_h
        in_maps.append(m)
    res = run_bass_kernel_spmd(nc, in_maps, core_ids=list(range(NCORES)))
    xcur = [np.asarray(r["y"]) for r in res.results]
    _ckfin("l0a", xcur)

    nc = build_ffn0()
    wgT_h = np.ascontiguousarray(np.asarray(inputs["L0_ffn_w_gate"], np.int32).T)
    wuT_h = np.ascontiguousarray(np.asarray(inputs["L0_ffn_w_up"], np.int32).T)
    wdT_h = np.ascontiguousarray(np.asarray(inputs["L0_ffn_w_down"], np.int32).T)
    in_maps = []
    for c in range(NCORES):
        m = dict(consts)
        m["x"] = xcur[c]
        m["wgT"] = wgT_h
        m["wuT"] = wuT_h
        m["wdT"] = wdT_h
        in_maps.append(m)
    res = run_bass_kernel_spmd(nc, in_maps, core_ids=list(range(NCORES)))
    xcur = [np.asarray(r["y"]) for r in res.results]
    _ckfin("ffn0a", xcur)

    nc = build_mamba()
    mh = mamba_host_inputs(inputs)
    in_maps = []
    for c in range(NCORES):
        m = dict(consts)
        m.update(mh)
        m["x"] = xcur[c]
        in_maps.append(m)
    res = run_bass_kernel_spmd(nc, in_maps, core_ids=list(range(NCORES)))
    xcur = [np.asarray(r["y"]) for r in res.results]
    _ckfin("mamba", xcur)

    nc = build_ffn0()
    in_maps = []
    for c in range(NCORES):
        m = dict(consts)
        m["x"] = xcur[c]
        m["wgT"] = wgT_h
        m["wuT"] = wuT_h
        m["wdT"] = wdT_h
        in_maps.append(m)
    res = run_bass_kernel_spmd(nc, in_maps, core_ids=list(range(NCORES)))
    xcur = [np.asarray(r["y"]) for r in res.results]
    _ckfin("ffn0b", xcur)

    nc = build_l1a()
    ah = l1a_host_inputs(inputs)
    in_maps = []
    for c in range(NCORES):
        m = dict(consts)
        m.update(ah)
        m["x"] = xcur[c]
        in_maps.append(m)
    res = run_bass_kernel_spmd(nc, in_maps, core_ids=list(range(NCORES)))
    xcur = [np.asarray(r["y"]) for r in res.results]
    _ckfin("l1a", xcur)

    nc = build_moe()
    eh = moe_host_inputs(inputs)
    in_maps = []
    for c in range(NCORES):
        m = dict(consts)
        m.update(eh)
        m["x"] = xcur[c]
        in_maps.append(m)
    res = run_bass_kernel_spmd(nc, in_maps, core_ids=list(range(NCORES)))
    xcur = [np.asarray(r["y"]) for r in res.results]
    _ckfin("moe1", xcur)

    nc = build_l1b()
    bh = l1b_host_inputs(inputs)
    in_maps = []
    for c in range(NCORES):
        m = dict(consts)
        m.update(bh)
        m["x"] = xcur[c]
        in_maps.append(m)
    res = run_bass_kernel_spmd(nc, in_maps, core_ids=list(range(NCORES)))
    xcur = [np.asarray(r["y"]) for r in res.results]
    _ckfin("l1b", xcur)

    nc = build_moe()
    in_maps = []
    for c in range(NCORES):
        m = dict(consts)
        m.update(eh)
        m["x"] = xcur[c]
        in_maps.append(m)
    res = run_bass_kernel_spmd(nc, in_maps, core_ids=list(range(NCORES)))
    xcur = [np.asarray(r["y"]) for r in res.results]
    _ckfin("moe2", xcur)

    x = np.concatenate([_from_feat_tok(y) for y in xcur], axis=0)
    return x


def kernel(**inputs):
    x = np.asarray(inputs["L0_attn_a_inputs"], np.float32)
    nc = build_fused()
    shared = fused_host_inputs(inputs)
    in_maps = []
    for c in range(NCORES):
        m = dict(shared)
        m["x"] = _to_feat_tok(x[2 * c:2 * c + 2])
        in_maps.append(m)
    res = run_bass_kernel_spmd(nc, in_maps, core_ids=list(range(NCORES)))
    return np.concatenate([_from_feat_tok(np.asarray(r["y"])) for r in res.results], axis=0)


# version 09 (final)
# speedup vs baseline: 1.0360x; 1.0072x over previous
import numpy as np
import ml_dtypes
import concourse.bass as bass
import concourse.mybir as mybir
from concourse.bass_utils import run_bass_kernel_spmd

F32 = mybir.dt.float32
BF16 = mybir.dt.bfloat16
I32 = mybir.dt.int32
AF = mybir.ActivationFunctionType
ALU = mybir.AluOpType
AX = mybir.AxisListType
BF16NP = ml_dtypes.bfloat16

NCORES = 8
BATCH = 16
L = 2048
D = 1024
SEQ = 2
T = SEQ * L

ENGINES = ("sync", "tensor", "vector", "scalar", "gpsimd")


class RecInst:
    __slots__ = ("ctx", "item")

    def __init__(self, ctx, item):
        self.ctx = ctx
        self.item = item

    def inc(self, name, n=1):
        self.ctx.cnt[name] = self.ctx.cnt.get(name, 0) + n
        self.item[3] = (name, n)
        return self.ctx.cnt[name]


class Rec:
    __slots__ = ("ctx", "eng")

    def __init__(self, ctx, eng):
        object.__setattr__(self, "ctx", ctx)
        object.__setattr__(self, "eng", eng)

    def wait(self, name, val=None):
        ctx = self.ctx
        if val is None:
            val = ctx.cnt.get(name, 0)
        if val <= 0:
            return
        if name in ("dma", "dst") and self.eng != "sync":
            tok = "ldk" if name == "dma" else "stk"
            if ctx.flushed.get(name, -1) < val:
                ctx.plan["sync"].append(["wait_ge", (name, ctx.cnt.get(name, 0)), {}, None])
                ctx.plan["sync"].append(["sem_inc", (tok, 1), {}, None])
                ctx.cnt[tok] = ctx.cnt.get(tok, 0) + 1
                ctx.flushed[name] = ctx.cnt.get(name, 0)
                ctx.lastwait[("sync", name)] = ctx.cnt.get(name, 0)
            self.wait(tok, ctx.cnt[tok])
            return
        key = (self.eng, name)
        if ctx.lastwait.get(key, -1) >= val:
            return
        ctx.lastwait[key] = val
        ctx.plan[self.eng].append(["wait_ge", (name, val), {}, None])

    def __getattr__(self, meth):
        ctx = self.ctx
        plan = ctx.plan[self.eng]

        def f(*a, **k):
            item = [meth, a, k, None]
            plan.append(item)
            return RecInst(ctx, item)

        return f


class Ctx:
    def __init__(self, nc):
        self.nc = nc
        self.plan = {e: [] for e in ENGINES}
        self.cnt = {}
        self.lastwait = {}
        self.flushed = {}
        self.sems = {}
        self._sem_guards = []
        self._vsep_tile = nc.alloc_sbuf_tensor("ctx_vsep", [1, 16], F32)
        self.sy = Rec(self, "sync")
        self.te = Rec(self, "tensor")
        self.ve = Rec(self, "vector")
        self.sc = Rec(self, "scalar")
        self.gp = Rec(self, "gpsimd")

    def vsep(self):
        t = self._vsep_tile
        self.ve.tensor_copy(out=t[0:1, 0:8], in_=t[0:1, 8:16])
        self.ve.tensor_copy(out=t[0:1, 0:8], in_=t[0:1, 8:16])

    def fence(self):
        snap = {k: self.cnt.get(k, 0) for k in ("te", "ve", "se", "gp", "dma", "dst")}
        for rec, own in ((self.te, "te"), (self.ve, "ve"), (self.sc, "se"),
                         (self.gp, "gp"), (self.sy, None)):
            for k, v in snap.items():
                if k == own:
                    continue
                rec.wait(k, v)

    def _sem(self, name):
        if name not in self.sems:
            g = self.nc.semaphore(name)
            self.sems[name] = g.__enter__()
            self._sem_guards.append(g)
        return self.sems[name]

    def emit(self):
        for e in ENGINES:
            for meth, a, k, incr in self.plan[e]:
                if meth in ("wait_ge", "sem_inc"):
                    self._sem(a[0])
                if incr is not None:
                    self._sem(incr[0])
        nc = self.nc
        with nc.Block() as block:
            for e in ENGINES:
                if not self.plan[e]:
                    continue

                def runner(eng, e=e):
                    for meth, a, k, incr in self.plan[e]:
                        if meth == "wait_ge":
                            eng.wait_ge(self.sems[a[0]], a[1])
                            continue
                        if meth == "sem_inc":
                            eng.sem_inc(self.sems[a[0]], a[1])
                            continue
                        inst = getattr(eng, meth)(*a, **k)
                        if incr is not None:
                            inst.then_inc(self.sems[incr[0]], incr[1])

                getattr(block, e)(runner)
        for g in self._sem_guards:
            g.__exit__(None, None, None)


def host_consts():
    return {
        "c_ones_f": np.ones((128, 1), np.float32),
        "c_ones_b": np.ones((128, 1), BF16NP),
        "c_onesrow_f": np.ones((1, 128), np.float32),
        "c_onesrow_b": np.ones((1, 128), BF16NP),
        "c_scal": np.array([[1e-5, 1.0, 0.0, 0.0, 0.0, 0.0, 0.0, 0.0]], np.float32),
    }


def declare_consts(nc):
    c = {}
    c["ones_f"] = nc.dram_tensor("c_ones_f", [128, 1], F32, kind="ExternalInput").ap()
    c["ones_b"] = nc.dram_tensor("c_ones_b", [128, 1], BF16, kind="ExternalInput").ap()
    c["onesrow_f"] = nc.dram_tensor("c_onesrow_f", [1, 128], F32, kind="ExternalInput").ap()
    c["onesrow_b"] = nc.dram_tensor("c_onesrow_b", [1, 128], BF16, kind="ExternalInput").ap()
    c["scal"] = nc.dram_tensor("c_scal", [1, 8], F32, kind="ExternalInput").ap()
    return c


def load_consts(ctx, sb):
    sy = ctx.sy
    t = {}
    t["ones_f"] = sb("ct_ones_f", [128, 1], F32)
    t["ones_b"] = sb("ct_ones_b", [128, 1], BF16)
    t["onesrow_f"] = sb("ct_onesrow_f", [1, 128], F32)
    t["onesrow_b"] = sb("ct_onesrow_b", [1, 128], BF16)
    t["scal"] = sb("ct_scal", [1, 8], F32)
    sy.dma_start(out=t["ones_f"][:], in_=ctx.consts["ones_f"]).inc("dma", 16)
    sy.dma_start(out=t["ones_b"][:], in_=ctx.consts["ones_b"]).inc("dma", 16)
    sy.dma_start(out=t["onesrow_f"][:], in_=ctx.consts["onesrow_f"]).inc("dma", 16)
    sy.dma_start(out=t["onesrow_b"][:], in_=ctx.consts["onesrow_b"]).inc("dma", 16)
    sy.dma_start(out=t["scal"][:], in_=ctx.consts["scal"]).inc("dma", 16)
    t["eps"] = t["scal"][0:1, 0:1]
    t["one"] = t["scal"][0:1, 1:2]
    return t


def emit_ln(ctx, ct, x_hbm, xhat, sb, pb, ntok=T):
    nc = ctx.nc
    sy, te, ve, sc = ctx.sy, ctx.te, ctx.ve, ctx.sc
    ctx.ln_id = getattr(ctx, "ln_id", 0) + 1
    P = f"ln{ctx.ln_id}_"
    xt = [sb(f"{P}xt{i}", [128, 512], F32) for i in range(8)]
    sq = sb(P + "sq", [128, 4096], F32)
    s_mean = sb(P + "mean", [1, 512], F32)
    s_var = sb(P + "var", [1, 512], F32)
    s_rstd = sb(P + "rstd", [1, 512], F32)
    s_mr = sb(P + "mr", [1, 512], F32)
    p_s1 = pb(P + "p_s1", [1, 512], F32)
    p_s2 = pb(P + "p_s2", [1, 512], F32)
    p_mr = pb(P + "p_mr", [128, 512], F32)
    p_rs = pb(P + "p_rs", [128, 512], F32)

    for tc in range(ntok // 512):
        sl = slice(tc * 512, (tc + 1) * 512)
        sy.wait("ve")
        for ft in range(8):
            sy.dma_start(out=xt[ft][:],
                         in_=x_hbm[ft * 128:(ft + 1) * 128, sl]).inc("dma", 16)
        te.wait("dma")
        te.wait("ve")
        for ft in range(8):
            te.matmul(p_s1[:], ct["ones_f"][:], xt[ft][:],
                      start=(ft == 0), stop=(ft == 7))
        sc.wait("dma")
        sc.wait("te")
        for ft in range(8):
            si = sc.activation(out=sq[:, ft * 512:(ft + 1) * 512], in_=xt[ft][:],
                               func=AF.Square)
        si.inc("se")
        te.wait("se")
        for ft in range(8):
            mi = te.matmul(p_s2[:], ct["ones_f"][:], sq[:, ft * 512:(ft + 1) * 512],
                           start=(ft == 0), stop=(ft == 7))
        mi.inc("te")
        ve.wait("te")
        ve.tensor_scalar_mul(out=s_mean[:], in0=p_s1[:], scalar1=1.0 / D)
        ve.tensor_mul(out=s_var[:], in0=s_mean[:], in1=s_mean[:])
        ve.scalar_tensor_tensor(out=s_var[:], in0=p_s2[:], scalar=1.0 / D,
                                in1=s_var[:], op0=ALU.mult,
                                op1=ALU.subtract).inc("ve")
        sc.wait("ve")
        sc.activation(out=s_var[:], in_=s_var[:], func=AF.Ln, bias=ct["eps"])
        sc.activation(out=s_rstd[:], in_=s_var[:], func=AF.Exp,
                      scale=-0.5).inc("se")
        ve.wait("se")
        ve.tensor_mul(out=s_mr[:], in0=s_mean[:], in1=s_rstd[:]).inc("ve")
        te.wait("ve")
        te.matmul(p_mr[:], ct["onesrow_f"][:], s_mr[:], start=True, stop=True)
        te.matmul(p_rs[:], ct["onesrow_f"][:], s_rstd[:], start=True,
                  stop=True).inc("te")
        ve.wait("te")
        for ft in range(8):
            ve.tensor_mul(out=xt[ft][:], in0=xt[ft][:], in1=p_rs[:])
            vi = ve.tensor_sub(out=xhat[ft][:, sl], in0=xt[ft][:], in1=p_mr[:])
        vi.inc("ve")
    ctx.fence()


def emit_l0a(ctx, ct, x_hbm, wqkv, wo, y_hbm):
    from contextlib import ExitStack
    nc = ctx.nc
    sy, te, ve, sc, gp = ctx.sy, ctx.te, ctx.ve, ctx.sc, ctx.gp

    with ExitStack() as es:
        def sb(name, shape, dtype):
            return es.enter_context(nc.sbuf_tensor(name, shape, dtype))

        xhat = [sb(f"l0a_xhat{i}", [128, T], BF16) for i in range(8)]
        with ExitStack() as ln_es:
            def sbl(name, shape, dtype):
                return ln_es.enter_context(nc.sbuf_tensor(name, shape, dtype))
            def pbl(name, shape, dtype):
                return ln_es.enter_context(nc.psum_tensor(name, shape, dtype))
            emit_ln(ctx, ct, x_hbm, xhat, sbl, pbl)

        hp = [sb(f"l0a_hp{i}", [128, L], BF16) for i in range(8)]
        wstage = [sb("l0a_wstage0", [128, 1024], F32), sb("l0a_wstage1", [128, 512], F32)]
        wq_sb = [sb(f"l0a_wq{i}", [128, 512], BF16) for i in range(2)]
        wk_sb = [sb(f"l0a_wk{i}", [128, 512], BF16) for i in range(2)]
        wv_sb = [sb(f"l0a_wv{i}", [128, 512], BF16) for i in range(2)]
        qop = [sb(f"l0a_qop{i}", [65, L], BF16) for i in range(2)]
        kop = [sb(f"l0a_kop{i}", [65, L], BF16) for i in range(2)]
        vop_all = [sb(f"l0a_vop{i}", [128, 16 * 65], BF16) for i in range(2)]
        sqn = [sb(f"l0a_sqn{i}", [64, L], BF16) for i in range(2)]
        n_k = [sb(f"l0a_nk{i}", [1, L], F32) for i in range(2)]
        b2 = [sb(f"l0a_b2{i}", [1, 1], F32) for i in range(2)]
        c_row = n_k
        dinv = sb("l0a_dinv", [1, 1024], F32)
        attnT = [sb(f"l0a_attnT{i}", [128, 1024], BF16) for i in range(2)]
        drep = sb("l0a_drep", [64, 1024], F32)
        wo_sb = [sb(f"l0a_wo{i}", [128, 1024], BF16) for i in range(8)]
        xt_res = sb("l0a_xt", [128, 512], F32)
        yt_res = sb("l0a_yt", [128, 512], F32)

        def pb(name, shape, dtype):
            return es.enter_context(nc.psum_tensor(name, shape, dtype))
        p_rot = [pb(f"l0a_p_r{i}", [128, 512], F32) for i in range(2)]
        p_s = [pb("l0a_p_s0", [128, 1024], F32), pb("l0a_p_s1", [128, 1024], F32)]
        p_o = pb("l0a_p_o", [65, 1024], F32)
        p_dr = p_rot[0]
        p_n = p_dr

        vop = [[va[:, tt * 65:(tt + 1) * 65] for tt in range(16)] for va in vop_all]

        for par in range(2):
            gp.memset(kop[par][64:65, :], 1.0)
            for tt in range(16):
                g = gp.memset(vop_all[par][:, tt * 65 + 64: tt * 65 + 65], 1.0)
        g.inc("gp")
        te.wait("gp")
        sc.wait("gp")

        for i in range(8):
            sy.wait("ve")
            sy.dma_start(out=wstage[0][:], in_=wo[i * 128:(i + 1) * 128, :]).inc("dma", 16)
            ve.wait("dma")
            ve.tensor_copy(out=wo_sb[i][:], in_=wstage[0][:]).inc("ve")

        st = [dict(wrdy=0, pdone=0) for _ in range(2)]
        rot_free = [0, 0]
        rot_i = [0]

        def emit_wload(h, s):
            par = h % 2
            for wdst, a in ((wq_sb[par], 0), (wk_sb[par], 1), (wv_sb[par], 2)):
                sy.wait("te", st[par]["pdone"])
                sy.wait("ve")
                src = bass.AP(wqkv.tensor, (h * 3 + a) * 1024 * 64,
                              [[64, 128], [128 * 64, 8], [1, 64]])
                sy.dma_start(out=wstage[1][:], in_=src).inc("dma", 16)
                ve.wait("dma")
                ve.wait("te", st[par]["pdone"])
                ve.tensor_copy(out=wdst[:], in_=wstage[1][:]).inc("ve")
            st[par]["wrdy"] = ctx.cnt["ve"]

        def emit_proj(h, s):
            par = h % 2
            ts0 = s * L
            te.wait("ve", st[par]["wrdy"])
            jobs = ([("q", qc) for qc in range(4)] + [("k", qc) for qc in range(4)]
                    + [("v", tt) for tt in range(16)])
            for kind, idx in jobs:
                ri = rot_i[0]
                rot_i[0] = (ri + 1) % 2
                pbuf = p_rot[ri]
                te.wait("se", rot_free[ri])
                if kind in ("q", "k"):
                    qsl = slice(ts0 + idx * 512, ts0 + (idx + 1) * 512)
                    lsl = slice(idx * 512, (idx + 1) * 512)
                    wsb = wq_sb[par] if kind == "q" else wk_sb[par]
                    for kt in range(8):
                        mi = te.matmul(pbuf[0:64, :], wsb[:, kt * 64:(kt + 1) * 64],
                                       xhat[kt][:, qsl], start=(kt == 0), stop=(kt == 7))
                    mi.inc("te")
                    sc.wait("te")
                    dsttile = qop[par] if kind == "q" else kop[par]
                    rot_free[ri] = sc.activation(
                        out=dsttile[0:64, lsl], in_=pbuf[0:64, :], func=AF.Copy,
                        scale=(0.125 if kind == "q" else 1.0)).inc("se")
                else:
                    tsl = slice(ts0 + idx * 128, ts0 + (idx + 1) * 128)
                    for kt in range(8):
                        mi = te.matmul(pbuf[:, 0:64], xhat[kt][:, tsl],
                                       wv_sb[par][:, kt * 64:(kt + 1) * 64],
                                       start=(kt == 0), stop=(kt == 7))
                    mi.inc("te")
                    sc.wait("te")
                    rot_free[ri] = sc.copy(out=vop[par][idx][:, 0:64],
                                           in_=pbuf[:, 0:64]).inc("se")
            st[par]["pdone"] = ctx.cnt["te"]

        def emit_cbound(h, s):
            par = h % 2
            sc.activation(out=sqn[par][:], in_=kop[par][0:64, :],
                          func=AF.Square).inc("se")
            te.wait("se")
            for qc in range(4):
                te.wait("ve")
                te.matmul(p_n[0:1, 0:512], ct["ones_b"][0:64, :],
                          sqn[par][:, qc * 512:(qc + 1) * 512],
                          start=True, stop=True).inc("te")
                ve.wait("te")
                ve.tensor_copy(out=n_k[par][0:1, qc * 512:(qc + 1) * 512],
                               in_=p_n[0:1, 0:512]).inc("ve")
            ctx.vsep()
            ve.reduce_max(out=b2[par][:], in_=n_k[par][:], axis=AX.X).inc("ve")
            sc.wait("te")
            sc.activation(out=sqn[par][:], in_=qop[par][0:64, :],
                          func=AF.Square).inc("se")
            te.wait("se")
            te.wait("ve")
            for qc in range(4):
                te.wait("se")
                te.matmul(p_n[0:1, 0:512], ct["ones_b"][0:64, :],
                          sqn[par][:, qc * 512:(qc + 1) * 512],
                          start=True, stop=True).inc("te")
                sc.wait("te")
                sc.wait("ve")
                sc.activation(out=c_row[par][0:1, qc * 512:(qc + 1) * 512],
                              in_=p_n[0:1, 0:512], func=AF.Sqrt,
                              scale=b2[par][:]).inc("se")
            sc.mul(out=qop[par][64:65, :], in_=c_row[par][:], mul=-1.0).inc("se")
            rot_free[0] = ctx.cnt["se"]

        def emit_attn(h, s):
            par = h % 2
            exp_cnt = [0, 0]
            pv_cnt = [0, 0]
            cbase = ctx.cnt.get("se", 0)
            for qh in range(2):
                for kt in range(16):
                    buf = kt % 2
                    te.wait("se", cbase)
                    te.wait("se", exp_cnt[buf])
                    for qq in range(2):
                        mi = te.matmul(p_s[buf][:, qq * 512:(qq + 1) * 512],
                                       kop[par][:, kt * 128:(kt + 1) * 128],
                                       qop[par][:, qh * 1024 + qq * 512:
                                           qh * 1024 + (qq + 1) * 512],
                                       start=True, stop=True)
                    mi.inc("te")
                    tmm = ctx.cnt["te"]
                    sc.wait("te", pv_cnt[buf])
                    sc.wait("te", tmm)
                    exp_cnt[buf] = sc.activation(out=attnT[buf][:],
                                                 in_=p_s[buf][:],
                                                 func=AF.Exp).inc("se")
                    te.wait("se", exp_cnt[buf])
                    for qq in range(2):
                        mi = te.matmul(p_o[:, qq * 512:(qq + 1) * 512],
                                       vop[par][kt][:, 0:65],
                                       attnT[buf][:, qq * 512:(qq + 1) * 512],
                                       start=(kt == 0), stop=(kt == 15))
                    pv_cnt[buf] = mi.inc("te")
                ve.wait("te")
                ve.reciprocal(out=dinv[:], in_=p_o[64:65, :]).inc("ve")
                te.wait("ve")
                te.wait("se")
                te.matmul(p_dr[:], ct["onesrow_f"][0:1, :], dinv[0:1, 0:512],
                          start=True, stop=True).inc("te")
                sc.wait("te")
                sc.copy(out=drep[:, 0:512], in_=p_dr[0:64, :]).inc("se")
                te.wait("se")
                te.matmul(p_dr[:], ct["onesrow_f"][0:1, :],
                          dinv[0:1, 512:1024], start=True, stop=True).inc("te")
                sc.wait("te")
                sc.copy(out=drep[:, 512:1024], in_=p_dr[0:64, :]).inc("se")
                ve.wait("se")
                dsttile = hp[h // 2][(h % 2) * 64:(h % 2) * 64 + 64,
                                     qh * 1024:(qh + 1) * 1024]
                ve.tensor_mul(out=dsttile, in0=p_o[0:64, :], in1=drep[:]).inc("ve")
                te.wait("ve")
                rot_free[0] = ctx.cnt["se"]

        for s in range(SEQ):
            for h in range(16):
                emit_wload(h, s)
                emit_proj(h, s)
                if h >= 1:
                    emit_attn(h - 1, s)
                emit_cbound(h, s)
            emit_attn(15, s)

            ts0 = s * L
            hpr = ctx.cnt["ve"]
            wo_free = [0, 0]
            xb_free = [0, 0, 0]
            yt_store = [0, 0, 0]
            xt3 = [xt_res, yt_res, sb(f"l0a_xt3_{s}", [128, 512], F32)]
            yt3 = [sb(f"l0a_yt3_{i}_{s}", [128, 512], F32) for i in range(3)]
            u = 0
            for tc in range(4):
                tsl = slice(ts0 + tc * 512, ts0 + (tc + 1) * 512)
                tloc = slice(tc * 512, (tc + 1) * 512)
                for jc in range(8):
                    ri = u % 3
                    pi = u % 2
                    u += 1
                    sy.wait("ve", xb_free[ri])
                    sy.dma_start(out=xt3[ri][:],
                                 in_=x_hbm[jc * 128:(jc + 1) * 128, tsl]).inc("dma", 16)
                    te.wait("ve", max(hpr, wo_free[pi]))
                    for p in range(8):
                        mi = te.matmul(p_rot[pi][:], wo_sb[p][:, jc * 128:(jc + 1) * 128],
                                       hp[p][:, tloc], start=(p == 0), stop=(p == 7))
                    mi.inc("te")
                    ve.wait("te")
                    ve.wait("dma")
                    ve.wait("dst", yt_store[ri])
                    ve.tensor_add(out=yt3[ri][:], in0=xt3[ri][:], in1=p_rot[pi][:]).inc("ve")
                    wo_free[pi] = ctx.cnt["ve"]
                    xb_free[ri] = ctx.cnt["ve"]
                    sy.wait("ve")
                    sy.dma_start(out=y_hbm[jc * 128:(jc + 1) * 128, tsl],
                                 in_=yt3[ri][:]).inc("dst", 16)
                    yt_store[ri] = ctx.cnt["dst"]
            te.wait("ve")


def emit_ffn0(ctx, ct, x_hbm, wgT, wuT, wdT, y_hbm, ytmp, tag=""):
    from contextlib import ExitStack
    nc = ctx.nc
    sy, te, ve, sc = ctx.sy, ctx.te, ctx.ve, ctx.sc

    with ExitStack() as es:
        def sb(name, shape, dtype):
            return es.enter_context(nc.sbuf_tensor(name + tag, shape, dtype))
        def pb(name, shape, dtype):
            return es.enter_context(nc.psum_tensor(name + tag, shape, dtype))

        xhat = [sb(f"f0_xhat{i}", [128, T], BF16) for i in range(8)]
        with ExitStack() as ln_es:
            emit_ln(ctx, ct, x_hbm, xhat,
                    lambda n, s, d: ln_es.enter_context(nc.sbuf_tensor(n + tag, s, d)),
                    lambda n, s, d: ln_es.enter_context(nc.psum_tensor(n + tag, s, d)))

        wg_sb = [sb(f"f0_wg{i}", [128, 1536], BF16) for i in range(8)]
        wu_sb = [sb(f"f0_wu{i}", [128, 1536], BF16) for i in range(8)]
        wd_sb = [sb(f"f0_wd{i}", [128, 1024], BF16) for i in range(12)]
        wint = sb("f0_wint", [128, 1536], I32)
        g_sb = [sb("f0_g0", [128, 512], F32), sb("f0_g1", [128, 512], F32)]
        h_act = [sb(f"f0_h{i}", [128, 512], BF16) for i in range(12)]
        xt = sb("f0_xt", [128, 512], F32)
        yt = sb("f0_yt", [128, 512], F32)
        p_g = [pb("f0_p_g0", [128, 512], F32), pb("f0_p_g1", [128, 512], F32)]
        p_u = [pb("f0_p_u0", [128, 512], F32), pb("f0_p_u1", [128, 512], F32)]
        p_y = [pb("f0_p_y0", [128, 512], F32), pb("f0_p_y1", [128, 512], F32)]

        for half in range(2):
            hof = half * 1536
            for grp, wsb_list, whbm, rows, cols in (
                (0, wg_sb, wgT, 1024, 1536),
                (1, wu_sb, wuT, 1024, 1536),
                (2, wd_sb, wdT, 3072, 1024),
            ):
                nt = len(wsb_list)
                for i in range(nt):
                    sy.wait("ve")
                    if grp < 2:
                        src = whbm[i * 128:(i + 1) * 128, hof:hof + 1536]
                        w = 1536
                    else:
                        src = whbm[hof + i * 128: hof + (i + 1) * 128, :]
                        w = 1024
                    sy.dma_start(out=wint[:, 0:w], in_=src).inc("dma", 16)
                    ve.wait("dma")
                    ve.wait("te")
                    ve.tensor_scalar_mul(out=wsb_list[i][:, 0:w], in0=wint[:, 0:w],
                                         scalar1=1.0 / 127.0).inc("ve")
            wrdy = ctx.cnt["ve"]
            te.wait("ve", wrdy)
            silu_free = [0, 0]
            gsb_free = [0, 0]
            mult_free = [0, 0]
            py_free = [0, 0]
            for tc in range(8):
                tsl = slice(tc * 512, (tc + 1) * 512)
                for ht in range(12):
                    ri = ht % 2
                    te.wait("se", silu_free[ri])
                    for kt in range(8):
                        mi = te.matmul(p_g[ri][:], wg_sb[kt][:, ht * 128:(ht + 1) * 128],
                                       xhat[kt][:, tsl], start=(kt == 0), stop=(kt == 7))
                    mi.inc("te")
                    sc.wait("te")
                    sc.wait("ve", gsb_free[ri])
                    silu_free[ri] = sc.activation(out=g_sb[ri][:], in_=p_g[ri][:],
                                                  func=AF.Silu).inc("se")
                    te.wait("ve", mult_free[ri])
                    for kt in range(8):
                        mi = te.matmul(p_u[ri][:], wu_sb[kt][:, ht * 128:(ht + 1) * 128],
                                       xhat[kt][:, tsl], start=(kt == 0), stop=(kt == 7))
                    mi.inc("te")
                    ve.wait("te")
                    ve.wait("se", silu_free[ri])
                    mult_free[ri] = ve.tensor_mul(out=h_act[ht][:], in0=g_sb[ri][:],
                                                  in1=p_u[ri][:]).inc("ve")
                    gsb_free[ri] = mult_free[ri]
                hrdy = ctx.cnt["ve"]
                for jc in range(8):
                    ri = jc % 2
                    jsl = slice(jc * 128, (jc + 1) * 128)
                    sy.wait("ve")
                    if half == 0:
                        sy.dma_start(out=xt[:], in_=x_hbm[jsl, tsl]).inc("dma", 16)
                    else:
                        sy.dma_start(out=xt[:], in_=ytmp[jsl, tsl]).inc("dma", 16)
                    te.wait("ve", max(hrdy, py_free[ri]))
                    for ht in range(12):
                        mi = te.matmul(p_y[ri][:], wd_sb[ht][:, jsl], h_act[ht][:],
                                       start=(ht == 0), stop=(ht == 11))
                    mi.inc("te")
                    ve.wait("te")
                    ve.wait("dma")
                    ve.tensor_add(out=yt[:], in0=xt[:], in1=p_y[ri][:]).inc("ve")
                    py_free[ri] = ctx.cnt["ve"]
                    sy.wait("ve")
                    if half == 0:
                        sy.dma_start(out=ytmp[jsl, tsl], in_=yt[:]).inc("dst", 16)
                    else:
                        sy.dma_start(out=y_hbm[jsl, tsl], in_=yt[:]).inc("dst", 16)
                    ve.wait("dst")
        ctx.fence()


def build_ffn0():
    nc = bass.Bass("TRN2", target_bir_lowering=False)
    ctx = Ctx(nc)
    ctx.consts = declare_consts(nc)
    x_hbm = nc.dram_tensor("x", [D, T], F32, kind="ExternalInput").ap()
    wgT = nc.dram_tensor("wgT", [1024, 3072], I32, kind="ExternalInput").ap()
    wuT = nc.dram_tensor("wuT", [1024, 3072], I32, kind="ExternalInput").ap()
    wdT = nc.dram_tensor("wdT", [3072, 1024], I32, kind="ExternalInput").ap()
    y_hbm = nc.dram_tensor("y", [D, T], F32, kind="ExternalOutput").ap()
    ytmp = nc.dram_tensor("ytmp", [D, T], F32).ap()

    from contextlib import ExitStack
    with ExitStack() as es:
        def sb(name, shape, dtype):
            return es.enter_context(nc.sbuf_tensor(name, shape, dtype))
        ct = load_consts(ctx, sb)
        emit_ffn0(ctx, ct, x_hbm, wgT, wuT, wdT, y_hbm, ytmp)
    ctx.sy.wait("dma")
    ctx.sy.wait("dst")
    ctx.emit()
    return nc



def emit_mamba(ctx, ct, x_hbm, w, y_hbm, scr):
    from contextlib import ExitStack
    import os as _os
    PAR = set((_os.environ.get("KPAR") or "").split(","))
    nc = ctx.nc
    sy, te, ve, sc, gp = ctx.sy, ctx.te, ctx.ve, ctx.sc, ctx.gp

    with ExitStack() as es:
        def sb(name, shape, dtype):
            return es.enter_context(nc.sbuf_tensor(name, shape, dtype))
        def pb(name, shape, dtype):
            return es.enter_context(nc.psum_tensor(name, shape, dtype))

        cols = sb("mb_cols", [128, 160], F32)
        sy.dma_start(out=cols[:, 0:64], in_=w["convw"]).inc("dma", 16)
        sy.dma_start(out=cols[:, 64:80], in_=w["convb"]).inc("dma", 16)
        sy.dma_start(out=cols[:, 80:96], in_=w["bdt"]).inc("dma", 16)
        sy.dma_start(out=cols[:, 96:112], in_=w["dcol"]).inc("dma", 16)
        a_sb = sb("mb_asb", [128, 256], F32)
        sy.dma_start(out=a_sb[:], in_=w["acols"]).inc("dma", 16)
        sel16 = sb("mb_sel16", [16, 2048], BF16)
        sy.dma_start(out=sel16[:], in_=w["sel16"]).inc("dma", 16)
        sc.wait("dma")
        sc.activation(out=a_sb[:], in_=a_sb[:], func=AF.Exp).inc("se")
        ve.wait("se")
        ve.tensor_scalar_mul(out=a_sb[:], in0=a_sb[:], scalar1=-1.0).inc("ve")
        wxdt = [sb(f"mb_wxdt{i}", [128, 64], BF16) for i in range(16)]
        wxB = [sb(f"mb_wxB{i}", [128, 16], BF16) for i in range(16)]
        wxC = [sb(f"mb_wxC{i}", [128, 16], BF16) for i in range(16)]
        wdt_sb = sb("mb_wdt", [64, 2048], BF16)
        wsts_es = ExitStack()
        wst_s = wsts_es.enter_context(nc.sbuf_tensor("mb_wsts", [128, 1024], F32))
        for i in range(16):
            sy.wait("ve")
            sy.dma_start(out=wst_s[:, 0:64], in_=w["wxdt"][i * 128:(i + 1) * 128, :]).inc("dma", 16)
            sy.dma_start(out=wst_s[:, 64:80], in_=w["wxB"][i * 128:(i + 1) * 128, :]).inc("dma", 16)
            sy.dma_start(out=wst_s[:, 80:96], in_=w["wxC"][i * 128:(i + 1) * 128, :]).inc("dma", 16)
            ve.wait("dma")
            ve.tensor_copy(out=wxdt[i][:], in_=wst_s[:, 0:64])
            ve.tensor_copy(out=wxB[i][:], in_=wst_s[:, 64:80])
            ve.tensor_copy(out=wxC[i][:], in_=wst_s[:, 80:96]).inc("ve")
        for hc in range(2):
            sy.wait("ve")
            sy.dma_start(out=wst_s[0:64, :],
                         in_=w["wdt"][:, hc * 1024:(hc + 1) * 1024]).inc("dma", 16)
            ve.wait("dma")
            ve.tensor_copy(out=wdt_sb[:, hc * 1024:(hc + 1) * 1024],
                           in_=wst_s[0:64, :]).inc("ve")
        ctx.te.wait("ve")
        wsts_es.close()

        with ExitStack() as axh:
          xhat = [axh.enter_context(nc.sbuf_tensor(f"mb_xhat{i}", [128, T], BF16))
                  for i in range(8)]
          with ExitStack() as ln_es:
            emit_ln(ctx, ct, x_hbm, xhat,
                    lambda n, s, d: ln_es.enter_context(nc.sbuf_tensor(n, s, d)),
                    lambda n, s, d: ln_es.enter_context(nc.psum_tensor(n, s, d)))

          for s in range(SEQ):
            ts0 = s * L
            if True:
              with ExitStack() as pa:
                  def sba(n, sh, d):
                      return pa.enter_context(nc.sbuf_tensor(f"{n}_s{s}", sh, d))
                  def pba(n, sh, d):
                      return pa.enter_context(nc.psum_tensor(f"{n}_s{s}", sh, d))
                  win_sb = [sba(f"mb_win{i}", [128, 4096], BF16) for i in range(8)]
                  wst = sba("mb_wst", [128, 1024], F32)
                  xs_t = sba("mb_xs", [128, L], F32)
                  acc = sba("mb_acc", [128, L], F32)
                  u_t = sba("mb_u", [128, L], BF16)
                  sil_t = sba("mb_sil", [128, 512], BF16)
                  dt_acc = sba("mb_dtacc", [64, L], F32)
                  b_acc = sba("mb_bacc", [16, L], F32)
                  c_acc = sba("mb_cacc", [16, L], F32)
                  dt_bf = sba("mb_dtbf", [64, L], BF16)
                  bb_bf = sba("mb_bbbf", [16, L], BF16)
                  cc_bf = sba("mb_ccbf", [16, L], BF16)
                  p_xz = pba("mb_p_xz", [128, 512], F32)
                  p_d = pba("mb_p_d", [64, 512], F32)
                  p_b = pba("mb_p_b", [16, 512], F32)
                  p_c = pba("mb_p_c", [16, 512], F32)

                  for i in range(8):
                      for hc in range(4):
                          sy.wait("ve")
                          sy.dma_start(out=wst[:],
                                       in_=w["win"][i * 128:(i + 1) * 128,
                                                    hc * 1024:(hc + 1) * 1024]).inc("dma", 16)
                          ve.wait("dma")
                          ve.wait("te")
                          ve.tensor_copy(out=win_sb[i][:, hc * 1024:(hc + 1) * 1024],
                                         in_=wst[:]).inc("ve")
                  gp.wait("ve")
                  gp.wait("se")
                  gp.wait("te")
                  gp.wait("dma")
                  gp.memset(dt_acc[:], 0.0)
                  gp.memset(b_acc[:], 0.0)
                  gp.memset(c_acc[:], 0.0).inc("gp")
                  ve.wait("gp")
                  wrdy = ctx.cnt["ve"]
                  te.wait("ve", wrdy)

                  for mt in range(32):
                      is_xs = mt < 16
                      for tc in range(4):
                          tl = slice(tc * 512, (tc + 1) * 512)
                          tg = slice(ts0 + tc * 512, ts0 + (tc + 1) * 512)
                          te.wait("se")
                          for kt in range(8):
                              mi = te.matmul(p_xz[:], win_sb[kt][:, mt * 128:(mt + 1) * 128],
                                             xhat[kt][:, tg], start=(kt == 0), stop=(kt == 7))
                          mi.inc("te")
                          sc.wait("te")
                          if is_xs:
                              sc.wait("ve")
                              sc.copy(out=xs_t[:, tl], in_=p_xz[:]).inc("se")
                          else:
                              sc.activation(out=sil_t[:], in_=p_xz[:],
                                            func=AF.Silu).inc("se")
                              sy.wait("se")
                              sy.dma_start(out=scr["sil"][(mt - 16) * 128:(mt - 15) * 128,
                                                          ts0 + tc * 512: ts0 + (tc + 1) * 512],
                                           in_=sil_t[:]).inc("dst", 16)
                              sc.wait("dst")
                      if is_xs:
                          dt2 = mt
                          ve.wait("se")
                          ve.tensor_scalar_mul(out=acc[:], in0=xs_t[:],
                                               scalar1=cols[:, dt2 * 4 + 3: dt2 * 4 + 4])
                          for k in (1, 2, 3):
                              ve.scalar_tensor_tensor(out=acc[:, k:L], in0=xs_t[:, 0:L - k],
                                                      scalar=cols[:, dt2 * 4 + 3 - k: dt2 * 4 + 4 - k],
                                                      in1=acc[:, k:L], op0=ALU.mult,
                                                      op1=ALU.add)
                          ve.nop()
                          ctx.plan["vector"].pop()
                          ctx.plan["vector"][-1][3] = None
                          vi = RecInst(ctx, ctx.plan["vector"][-1])
                          vi.inc("ve")
                          sc.wait("ve")
                          sc.activation(out=u_t[:], in_=acc[:], func=AF.Silu,
                                        bias=cols[:, 64 + dt2: 65 + dt2]).inc("se")
                          sy.wait("se")
                          sy.dma_start(out=scr["u"][dt2 * 128:(dt2 + 1) * 128,
                                                    ts0:ts0 + L], in_=u_t[:]).inc("dst", 16)
                          te.wait("se")
                          for tc in range(4):
                              tl = slice(tc * 512, (tc + 1) * 512)
                              te.wait("ve")
                              te.matmul(p_d[:], wxdt[dt2][:], u_t[:, tl], start=True, stop=True)
                              te.matmul(p_b[:], wxB[dt2][:], u_t[:, tl], start=True, stop=True)
                              te.matmul(p_c[:], wxC[dt2][:], u_t[:, tl], start=True,
                                        stop=True).inc("te")
                              ve.wait("te")
                              ve.tensor_add(out=dt_acc[:, tl], in0=dt_acc[:, tl], in1=p_d[:])
                              ve.tensor_add(out=b_acc[:, tl], in0=b_acc[:, tl], in1=p_b[:])
                              ve.tensor_add(out=c_acc[:, tl], in0=c_acc[:, tl],
                                            in1=p_c[:]).inc("ve")
                          sy.wait("te")
                      if "A" in PAR:
                          ctx.fence()
                  ve.tensor_copy(out=dt_bf[:], in_=dt_acc[:])
                  ve.tensor_copy(out=bb_bf[:], in_=b_acc[:])
                  ve.tensor_copy(out=cc_bf[:], in_=c_acc[:]).inc("ve")
                  sy.wait("ve")
                  sy.dma_start(out=scr["dtb"][0:64, ts0:ts0 + L], in_=dt_bf[:]).inc("dst", 16)
                  sy.dma_start(out=scr["bc"][0:16, ts0:ts0 + L], in_=bb_bf[:]).inc("dst", 16)
                  sy.dma_start(out=scr["bc"][16:32, ts0:ts0 + L], in_=cc_bf[:]).inc("dst", 16)
                  ctx.fence()

        for s in range(SEQ):
            ts0 = s * L
            with ExitStack() as pbs:
                def sbb(n, sh, d):
                    return pbs.enter_context(nc.sbuf_tensor(f"{n}_s{s}", sh, d))
                def pbb(n, sh, d):
                    return pbs.enter_context(nc.psum_tensor(f"{n}_s{s}", sh, d))
                bb_sb = sbb("mb_bb", [16, L], BF16)
                cc_sb = sbb("mb_cc", [16, L], BF16)
                dt_bf2 = sbb("mb_dtbf2", [64, L], BF16)
                brep = [sbb(f"mb_brep{i}", [128, L], BF16) for i in range(16)]
                crep = [sbb(f"mb_crep{i}", [128, L], BF16) for i in range(16)]
                delta = sbb("mb_delta", [128, L], F32)
                sabs = sbb("mb_sabs", [128, 512], F32)
                du = sbb("mb_du", [128, L], BF16)
                dA = [sbb("mb_dA0", [128, L], BF16), sbb("mb_dA1", [128, L], BF16)]
                Xs = sbb("mb_X", [128, L], BF16)
                yp = [sbb("mb_yp0", [128, L], BF16), sbb("mb_yp1", [128, L], BF16)]
                yacc = sbb("mb_yacc", [128, L], F32)
                u_t2 = sbb("mb_u2", [128, L], BF16)
                sil2 = sbb("mb_sil2", [128, L], BF16)
                p_rep = pbb("mb_p_rep", [128, 512], F32)
                p_de = pbb("mb_p_de", [128, 512], F32)

                sy.dma_start(out=bb_sb[:], in_=scr["bc"][0:16, ts0:ts0 + L]).inc("dma", 16)
                sy.dma_start(out=cc_sb[:], in_=scr["bc"][16:32, ts0:ts0 + L]).inc("dma", 16)
                sy.dma_start(out=dt_bf2[:], in_=scr["dtb"][0:64, ts0:ts0 + L]).inc("dma", 16)
                te.wait("dma")
                for n in range(16):
                    if "R" in PAR:
                        ctx.fence()
                    for tc in range(4):
                        tl = slice(tc * 512, (tc + 1) * 512)
                        te.wait("se")
                        te.matmul(p_rep[:], sel16[:, n * 128:(n + 1) * 128],
                                  bb_sb[:, tl], start=True, stop=True).inc("te")
                        sc.wait("te")
                        sc.copy(out=brep[n][:, tl], in_=p_rep[:]).inc("se")
                        te.wait("se")
                        te.matmul(p_rep[:], sel16[:, n * 128:(n + 1) * 128],
                                  cc_sb[:, tl], start=True, stop=True).inc("te")
                        sc.wait("te")
                        sc.copy(out=crep[n][:, tl], in_=p_rep[:]).inc("se")

                for dt2 in range(16):
                    bcol = cols[:, 80 + dt2:81 + dt2]
                    sy.wait("ve")
                    sy.dma_start(out=u_t2[:], in_=scr["u"][dt2 * 128:(dt2 + 1) * 128,
                                                           ts0:ts0 + L]).inc("dma", 16)
                    sy.dma_start(out=sil2[:], in_=scr["sil"][dt2 * 128:(dt2 + 1) * 128,
                                                             ts0:ts0 + L]).inc("dma", 16)
                    for tc in range(4):
                        tl = slice(tc * 512, (tc + 1) * 512)
                        te.wait("ve")
                        te.matmul(p_de[:], wdt_sb[:, dt2 * 128:(dt2 + 1) * 128],
                                  dt_bf2[:, tl], start=True, stop=True).inc("te")
                        sc.wait("te")
                        sc.activation(out=sabs[:], in_=p_de[:], func=AF.Abs, bias=bcol)
                        sc.activation(out=sabs[:], in_=sabs[:], func=AF.Exp, scale=-1.0)
                        sc.activation(out=sabs[:], in_=sabs[:], func=AF.Ln, bias=ct["ones_f"][:])
                        sc.activation(out=delta[:, tl], in_=p_de[:], func=AF.Relu,
                                      bias=bcol).inc("se")
                        ve.wait("se")
                        ve.tensor_add(out=delta[:, tl], in0=delta[:, tl],
                                      in1=sabs[:]).inc("ve")
                    ve.wait("dma")
                    ve.tensor_mul(out=du[:], in0=delta[:], in1=u_t2[:]).inc("ve")
                    scan_done = [0, 0]
                    exp_done = [0, 0]
                    gp_done = [0, 0]
                    for n in range(16):
                        buf = n % 2
                        acol = a_sb[:, dt2 * 16 + n: dt2 * 16 + n + 1]
                        sc.wait("ve", scan_done[buf])
                        if n == 0:
                            sc.wait("ve")
                        exp_done[buf] = sc.activation(out=dA[buf][:], in_=delta[:],
                                                      func=AF.Exp, scale=acol).inc("se")
                        ve.wait("se", exp_done[buf])
                        ve.wait("gp", gp_done[buf])
                        ve.tensor_mul(out=Xs[:], in0=du[:], in1=brep[n][:])
                        ve.tensor_tensor_scan(out=Xs[:], data0=dA[buf][:], data1=Xs[:],
                                              initial=0.0, op0=ALU.mult, op1=ALU.add)
                        scan_done[buf] = ve.tensor_mul(out=yp[buf][:], in0=Xs[:],
                                                       in1=crep[n][:]).inc("ve")
                        gp.wait("ve", scan_done[buf])
                        if n == 0:
                            gp.wait("dst")
                            gp_done[buf] = gp.tensor_copy(out=yacc[:],
                                                          in_=yp[buf][:]).inc("gp")
                        else:
                            gp_done[buf] = gp.tensor_add(out=yacc[:], in0=yacc[:],
                                                         in1=yp[buf][:]).inc("gp")
                    ve.wait("gp", gp_done[15 % 2])
                    ve.wait("gp", gp_done[14 % 2])
                    ve.scalar_tensor_tensor(out=yacc[:], in0=u_t2[:],
                                            scalar=cols[:, 96 + dt2:97 + dt2],
                                            in1=yacc[:], op0=ALU.mult, op1=ALU.add)
                    ve.tensor_mul(out=yacc[:], in0=yacc[:], in1=sil2[:]).inc("ve")
                    gp.wait("ve")
                    gp.dma_start(out=scr["y3"][dt2 * 128:(dt2 + 1) * 128, ts0:ts0 + L],
                                 in_=yacc[:]).inc("dst", 16)
                    ve.wait("dst")
                    if "B" in PAR:
                        ctx.fence()
                ctx.fence()

            with ExitStack() as pc:
                def sbc(n, sh, d):
                    return pc.enter_context(nc.sbuf_tensor(f"{n}_s{s}", sh, d))
                def pbc(n, sh, d):
                    return pc.enter_context(nc.psum_tensor(f"{n}_s{s}", sh, d))
                wout_sb = [sbc(f"mb_wo{i}", [128, 1024], BF16) for i in range(16)]
                wst2 = sbc("mb_wst2", [128, 1024], F32)
                y3c = [sbc(f"mb_y3c{i}", [128, 512], BF16) for i in range(16)]
                xt = sbc("mb_xt", [128, 512], F32)
                yt = sbc("mb_yt", [128, 512], F32)
                p_y = pbc("mb_p_y", [128, 512], F32)
                for i in range(16):
                    sy.wait("ve")
                    sy.dma_start(out=wst2[:], in_=w["wout"][i * 128:(i + 1) * 128, :]).inc("dma", 16)
                    ve.wait("dma")
                    ve.wait("te")
                    ve.tensor_copy(out=wout_sb[i][:], in_=wst2[:]).inc("ve")
                for tc in range(4):
                    tg = slice(ts0 + tc * 512, ts0 + (tc + 1) * 512)
                    te.wait("te")
                    sy.wait("te")
                    for i in range(16):
                        sy.dma_start(out=y3c[i][:], in_=scr["y3"][i * 128:(i + 1) * 128,
                                                                  tg]).inc("dma", 16)
                    for jc in range(8):
                        jsl = slice(jc * 128, (jc + 1) * 128)
                        sy.wait("ve")
                        sy.dma_start(out=xt[:], in_=x_hbm[jsl, tg]).inc("dma", 16)
                        te.wait("dma")
                        te.wait("ve")
                        for i in range(16):
                            mi = te.matmul(p_y[:], wout_sb[i][:, jsl], y3c[i][:],
                                           start=(i == 0), stop=(i == 15))
                        mi.inc("te")
                        ve.wait("te")
                        ve.wait("dma")
                        ve.tensor_add(out=yt[:], in0=xt[:], in1=p_y[:]).inc("ve")
                        sy.wait("ve")
                        sy.dma_start(out=y_hbm[jsl, tg], in_=yt[:]).inc("dst", 16)
                        ve.wait("dst")
                        if "C" in PAR:
                            ctx.fence()
                ctx.fence()


def build_mamba(debug=False):
    nc = bass.Bass("TRN2", target_bir_lowering=False)
    ctx = Ctx(nc)
    ctx.consts = declare_consts(nc)
    x_hbm = nc.dram_tensor("x", [D, T], F32, kind="ExternalInput").ap()
    w = {}
    w["win"] = nc.dram_tensor("win", [1024, 4096], F32, kind="ExternalInput").ap()
    w["convw"] = nc.dram_tensor("convw", [128, 64], F32, kind="ExternalInput").ap()
    w["convb"] = nc.dram_tensor("convb", [128, 16], F32, kind="ExternalInput").ap()
    w["wxdt"] = nc.dram_tensor("wxdt", [2048, 64], F32, kind="ExternalInput").ap()
    w["wxB"] = nc.dram_tensor("wxB", [2048, 16], F32, kind="ExternalInput").ap()
    w["wxC"] = nc.dram_tensor("wxC", [2048, 16], F32, kind="ExternalInput").ap()
    w["wdt"] = nc.dram_tensor("wdt", [64, 2048], F32, kind="ExternalInput").ap()
    w["bdt"] = nc.dram_tensor("bdt", [128, 16], F32, kind="ExternalInput").ap()
    w["acols"] = nc.dram_tensor("acols", [128, 256], F32, kind="ExternalInput").ap()
    w["dcol"] = nc.dram_tensor("dcol", [128, 16], F32, kind="ExternalInput").ap()
    w["wout"] = nc.dram_tensor("wout", [2048, 1024], F32, kind="ExternalInput").ap()
    w["sel16"] = nc.dram_tensor("sel16", [16, 2048], BF16, kind="ExternalInput").ap()
    y_hbm = nc.dram_tensor("y", [D, T], F32, kind="ExternalOutput").ap()
    scr = {}
    kind = "ExternalOutput" if debug else "Internal"
    scr["u"] = nc.dram_tensor("scr_u", [2048, T], BF16, kind=kind).ap()
    scr["sil"] = nc.dram_tensor("scr_sil", [2048, T], BF16, kind=kind).ap()
    scr["y3"] = nc.dram_tensor("scr_y3", [2048, T], BF16, kind=kind).ap()
    scr["dtb"] = nc.dram_tensor("scr_dtb", [64, T], BF16, kind=kind).ap()
    scr["bc"] = nc.dram_tensor("scr_bc", [32, T], BF16, kind=kind).ap()

    from contextlib import ExitStack
    with ExitStack() as es:
        def sb(name, shape, dtype):
            return es.enter_context(nc.sbuf_tensor(name, shape, dtype))
        ct = load_consts(ctx, sb)
        emit_mamba(ctx, ct, x_hbm, w, y_hbm, scr)
    ctx.sy.wait("dma")
    ctx.sy.wait("dst")
    ctx.emit()
    return nc


def mamba_host_inputs(inputs):
    m = {}
    m["win"] = np.ascontiguousarray(np.asarray(inputs["L0_attn_b_W_in"], np.float32))
    cw = np.asarray(inputs["L0_attn_b_conv_w"], np.float32)
    m["convw"] = np.ascontiguousarray(cw.reshape(16, 128, 4).transpose(1, 0, 2).reshape(128, 64))
    m["convb"] = np.ascontiguousarray(np.asarray(inputs["L0_attn_b_conv_b"], np.float32).reshape(16, 128).T)
    wx = np.asarray(inputs["L0_attn_b_W_x"], np.float32)
    m["wxdt"] = np.ascontiguousarray(wx[:, 0:64])
    m["wxB"] = np.ascontiguousarray(wx[:, 64:80])
    m["wxC"] = np.ascontiguousarray(wx[:, 80:96])
    m["wdt"] = np.ascontiguousarray(np.asarray(inputs["L0_attn_b_W_dt"], np.float32))
    m["bdt"] = np.ascontiguousarray(np.asarray(inputs["L0_attn_b_b_dt"], np.float32).reshape(16, 128).T)
    m["acols"] = np.ascontiguousarray(np.asarray(inputs["L0_attn_b_A_log"], np.float32)
                                      .reshape(16, 128, 16).transpose(1, 0, 2).reshape(128, 256))
    m["dcol"] = np.ascontiguousarray(np.asarray(inputs["L0_attn_b_D"], np.float32).reshape(16, 128).T)
    m["wout"] = np.ascontiguousarray(np.asarray(inputs["L0_attn_b_W_out"], np.float32))
    sel16 = np.zeros((16, 2048), BF16NP)
    for n in range(16):
        sel16[n, n * 128:(n + 1) * 128] = 1.0
    m["sel16"] = sel16
    return m



def emit_l1a(ctx, ct, x_hbm, w, y_hbm):
    from contextlib import ExitStack
    nc = ctx.nc
    sy, te, ve, sc, gp = ctx.sy, ctx.te, ctx.ve, ctx.sc, ctx.gp
    WS = (4.0, 16.0, 64.0, 256.0)
    ESC = 2.0 / float(np.sqrt(2048.0))

    with ExitStack() as es:
        def sb(name, shape, dtype):
            return es.enter_context(nc.sbuf_tensor(name, shape, dtype))

        xhat = [sb(f"a1_xhat{i}", [128, T], BF16) for i in range(8)]
        with ExitStack() as ln_es:
            emit_ln(ctx, ct, x_hbm, xhat,
                    lambda n, s, d: ln_es.enter_context(nc.sbuf_tensor(n, s, d)),
                    lambda n, s, d: ln_es.enter_context(nc.psum_tensor(n, s, d)))

        r3 = sb("a1_r3", [128, 4096], F32)
        sy.dma_start(out=r3[:], in_=w["r3"]).inc("dma", 16)
        bvrow = sb("a1_bvrow", [1, 3072], BF16)
        with nc.sbuf_tensor("a1_bstage", [1, 3072], F32) as bstage:
            sy.dma_start(out=bstage[:], in_=w["qkvb_row"]).inc("dma", 16)
            ve.wait("dma")
            ve.tensor_copy(out=bvrow[:], in_=bstage[:]).inc("ve")
            ve.wait("ve")
        qkvb = sb("a1_qkvb", [128, 24], F32)
        sy.dma_start(out=qkvb[:], in_=w["qkvb_cols"]).inc("dma", 16)
        dx0 = [sb(f"a1_dx0_{s}", [1, L], F32) for s in range(SEQ)]

        def pb(name, shape, dtype):
            return es.enter_context(nc.psum_tensor(name, shape, dtype))
        p_qv = pb("a1_p_qv", [128, 512], F32)
        p_s = pb("a1_p_s", [128, 512], F32)
        p_oa = pb("a1_p_oa", [128, 1024], F32)
        p_ob = pb("a1_p_ob", [128, 1024], F32)
        p_nd = pb("a1_p_nd", [1, 1024], F32)

        with ExitStack() as dxs:
            def sbd(n, sh, d):
                return dxs.enter_context(nc.sbuf_tensor(n, sh, d))
            offw_sb = sbd("a1_offw", [128, 2048], BF16)
            wst = sbd("a1_wst", [128, 2048], F32)
            lin2_sb = sbd("a1_lin2", [128, 2], BF16)
            rt = sbd("a1_rt", [128, 512], BF16)
            z1 = sbd("a1_z1", [128, L], BF16)
            z1b = sbd("a1_z1b", [128, L], BF16)
            lb = sbd("a1_lb", [1, 1], F32)
            sy.dma_start(out=lb[:], in_=w["lin2_b"]).inc("dma", 16)
            src = bass.AP(w["offwT"].tensor, 0, [[256, 128], [128 * 256, 8], [1, 256]])
            sy.dma_start(out=wst[:], in_=src).inc("dma", 16)
            ve.wait("dma")
            ve.tensor_copy(out=offw_sb[:], in_=wst[:]).inc("ve")
            sy.wait("ve")
            sy.dma_start(out=wst[0:128, 0:2], in_=w["lin2wT"]).inc("dma", 16)
            ve.wait("dma")
            ve.tensor_copy(out=lin2_sb[:], in_=wst[0:128, 0:2]).inc("ve")
            te.wait("ve")
            for s in range(SEQ):
                ts0 = s * L
                for mh in range(2):
                    dst = z1 if mh == 0 else z1b
                    for tc in range(4):
                        tg = slice(ts0 + tc * 512, ts0 + (tc + 1) * 512)
                        tl = slice(tc * 512, (tc + 1) * 512)
                        for kt in range(8):
                            sc.wait("te")
                            sc.activation(out=rt[:], in_=xhat[kt][:, tg],
                                          func=AF.Relu).inc("se")
                            te.wait("se")
                            te.matmul(p_qv[:], offw_sb[:, kt * 256 + mh * 128:
                                                       kt * 256 + (mh + 1) * 128],
                                      rt[:], start=(kt == 0),
                                      stop=(kt == 7)).inc("te")
                        sc.wait("te")
                        sc.copy(out=dst[:, tl], in_=p_qv[:]).inc("se")
                te.wait("se")
                for tc in range(4):
                    tl = slice(tc * 512, (tc + 1) * 512)
                    te.wait("se")
                    te.matmul(p_nd[:, 0:512], lin2_sb[:, 0:1], z1[:, tl],
                              start=True, stop=False)
                    te.matmul(p_nd[:, 0:512], lin2_sb[:, 1:2], z1b[:, tl],
                              start=False, stop=True).inc("te")
                    sc.wait("te")
                    sc.activation(out=dx0[s][0:1, tl], in_=p_nd[0:1, 0:512],
                                  func=AF.Sigmoid, bias=lb[:]).inc("se")
                ve.wait("se")
                ve.tensor_scalar_mul(out=dx0[s][:], in0=dx0[s][:],
                                     scalar1=float(L)).inc("ve")
            ctx.fence()

        wstage = sb("a1_wstage", [128, 2048], F32)
        wq_sb = sb("a1_wq", [128, 2048], BF16)
        wk_sb = sb("a1_wk", [128, 2048], BF16)
        wv_sb = sb("a1_wv", [128, 2048], BF16)
        qop = [sb(f"a1_qop{i}", [128, L], BF16) for i in range(2)]
        kop = [sb(f"a1_kop{i}", [128, L], BF16) for i in range(2)]
        vop_all = sb("a1_vop", [128, 16 * 256], BF16)
        negc = sb("a1_negc", [1, L], BF16)
        sqn = sb("a1_sqn", [128, L], BF16)
        sqn2 = sb("a1_sqn2", [128, L], BF16)
        n_k = sb("a1_nk", [1, L], F32)
        b2 = sb("a1_b2", [1, 1], F32)
        dxrep = sb("a1_dxrep", [128, L], F32)
        attnT = sb("a1_attnT", [128, 1024], BF16)
        mskt = sb("a1_mskt", [128, 1024], BF16)
        drep = sb("a1_drep", [128, 1024], F32)
        xt2 = sb("a1_xt2", [128, 1024], F32)
        yt2 = sb("a1_yt2", [128, 1024], F32)
        vop = [vop_all[:, tt * 256:(tt + 1) * 256] for tt in range(16)]

        for g in range(4):
            for a, wdst in ((0, wq_sb), (1, wk_sb), (2, wv_sb)):
                coff = a * 1024 + g * 256
                sy.wait("ve")
                src = bass.AP(w["qkvwT"].tensor, coff,
                              [[3072, 128], [128 * 3072, 8], [1, 256]])
                sy.dma_start(out=wstage[:], in_=src).inc("dma", 16)
                ve.wait("dma")
                ve.wait("te")
                ve.tensor_copy(out=wdst[:], in_=wstage[:]).inc("ve")
            wrdy = ctx.cnt["ve"]
            for s in range(SEQ):
                ts0 = s * L
                te.wait("ve", wrdy)
                for mh in range(2):
                    bq = qkvb[:, g * 2 + mh: g * 2 + mh + 1]
                    bk = qkvb[:, 8 + g * 2 + mh: 8 + g * 2 + mh + 1]
                    for tc in range(4):
                        tg = slice(ts0 + tc * 512, ts0 + (tc + 1) * 512)
                        tl = slice(tc * 512, (tc + 1) * 512)
                        te.wait("se")
                        for kt in range(8):
                            mi = te.matmul(p_qv[:], wq_sb[:, kt * 256 + mh * 128:
                                                          kt * 256 + (mh + 1) * 128],
                                           xhat[kt][:, tg], start=(kt == 0), stop=(kt == 7))
                        mi.inc("te")
                        ve.wait("te")
                        ve.tensor_scalar_add(out=qop[mh][:, tl], in0=p_qv[:],
                                             scalar1=bq).inc("ve")
                        te.wait("ve")
                        for kt in range(8):
                            mi = te.matmul(p_qv[:], wk_sb[:, kt * 256 + mh * 128:
                                                          kt * 256 + (mh + 1) * 128],
                                           xhat[kt][:, tg], start=(kt == 0), stop=(kt == 7))
                        mi.inc("te")
                        ve.wait("te")
                        ve.tensor_scalar_add(out=kop[mh][:, tl], in0=p_qv[:],
                                             scalar1=bk).inc("ve")
                        te.wait("ve")
                for tt in range(16):
                    tsl = slice(ts0 + tt * 128, ts0 + (tt + 1) * 128)
                    for mh in range(2):
                        for kt in range(8):
                            te.matmul(p_qv[:, 0:128], xhat[kt][:, tsl],
                                      wv_sb[:, kt * 256 + mh * 128: kt * 256 + (mh + 1) * 128],
                                      start=(kt == 0), stop=False)
                        mi = te.matmul(p_qv[:, 0:128], ct["onesrow_b"][0:1, 0:128],
                                       bvrow[0:1, 2048 + g * 256 + mh * 128:
                                             2048 + g * 256 + (mh + 1) * 128],
                                       start=False, stop=True)
                        mi.inc("te")
                        sc.wait("te")
                        sc.copy(out=vop[tt][:, mh * 128:(mh + 1) * 128],
                                in_=p_qv[:, 0:128]).inc("se")
                        te.wait("se")
                sc.wait("ve")
                sc.activation(out=sqn[:], in_=kop[0][:], func=AF.Square)
                sc.activation(out=sqn2[:], in_=kop[1][:], func=AF.Square).inc("se")
                te.wait("se")
                for qc in range(4):
                    qsl = slice(qc * 512, (qc + 1) * 512)
                    te.wait("ve")
                    te.matmul(p_nd[:, 0:512], ct["ones_b"][0:128, :], sqn[:, qsl],
                              start=True, stop=False)
                    te.matmul(p_nd[:, 0:512], ct["ones_b"][0:128, :], sqn2[:, qsl],
                              start=False, stop=True).inc("te")
                    ve.wait("te")
                    ve.tensor_copy(out=n_k[0:1, qsl], in_=p_nd[0:1, 0:512]).inc("ve")
                ctx.vsep()
                ve.reduce_max(out=b2[:], in_=n_k[:], axis=AX.X).inc("ve")
                sc.wait("te")
                sc.wait("ve")
                sc.activation(out=sqn[:], in_=qop[0][:], func=AF.Square)
                sc.activation(out=sqn2[:], in_=qop[1][:], func=AF.Square).inc("se")
                te.wait("se")
                te.wait("ve")
                for qc in range(4):
                    qsl = slice(qc * 512, (qc + 1) * 512)
                    te.wait("se")
                    te.matmul(p_nd[:, 0:512], ct["ones_b"][0:128, :], sqn[:, qsl],
                              start=True, stop=False)
                    te.matmul(p_nd[:, 0:512], ct["ones_b"][0:128, :], sqn2[:, qsl],
                              start=False, stop=True).inc("te")
                    sc.wait("te")
                    sc.wait("ve")
                    sc.activation(out=n_k[0:1, qsl], in_=p_nd[0:1, 0:512],
                                  func=AF.Sqrt, scale=b2[:]).inc("se")
                sc.mul(out=negc[:], in_=n_k[:], mul=-1.0).inc("se")
                ve.wait("se")
                ve.tensor_scalar_add(out=n_k[:], in0=dx0[s][:],
                                     scalar1=WS[g]).inc("ve")
                te.wait("ve")
                for qc in range(4):
                    qsl = slice(qc * 512, (qc + 1) * 512)
                    te.wait("se")
                    te.matmul(p_qv[:], ct["onesrow_f"][0:1, :], n_k[0:1, qsl],
                              start=True, stop=True).inc("te")
                    sc.wait("te")
                    sc.copy(out=dxrep[:, qsl], in_=p_qv[:]).inc("se")
                if getattr(ctx, "dbg", None) is not None and g == 0 and s == 0:
                    sy.wait("se")
                    sy.dma_start(out=ctx.dbg["dxrep"], in_=dxrep[:]).inc("dma", 16)
                    sy.dma_start(out=ctx.dbg["negc"], in_=negc[:]).inc("dma", 16)
                    sy.dma_start(out=ctx.dbg["dx0"], in_=dx0[0][:]).inc("dma", 16)
                for qh in range(2):
                    qof = qh * 1024
                    for kt in range(16):
                        te.wait("se")
                        te.wait("ve")
                        for qq in range(2):
                            qsl = slice(qof + qq * 512, qof + (qq + 1) * 512)
                            te.matmul(p_s[:], kop[0][:, kt * 128:(kt + 1) * 128],
                                      qop[0][:, qsl], start=True, stop=False)
                            te.matmul(p_s[:], kop[1][:, kt * 128:(kt + 1) * 128],
                                      qop[1][:, qsl], start=False, stop=False)
                            mi = te.matmul(p_s[:], ct["onesrow_b"][0:1, 0:128],
                                           negc[0:1, qsl], start=False, stop=True)
                            mi.inc("te")
                            sc.wait("te")
                            sc.activation(out=attnT[:, qq * 512:(qq + 1) * 512],
                                          in_=p_s[:], func=AF.Exp,
                                          scale=ESC).inc("se")
                            te.wait("se")
                        ve.wait("se")
                        c0 = 2048 - kt * 128 + qof
                        ve.tensor_tensor(out=mskt[:], in0=r3[:, c0:c0 + 1024],
                                         in1=dxrep[:, qof:qof + 1024], op=ALU.is_le)
                        if getattr(ctx, "dbg", None) is not None and g == 0 and s == 0 and qh == 0 and kt == 0:
                            ve.tensor_copy(out=ctx.dbg_msk_sb[:], in_=mskt[:])
                            ve.tensor_copy(out=ctx.dbg_att_sb[:], in_=attnT[:]).inc("ve")
                            sy.wait("ve")
                            sy.dma_start(out=ctx.dbg["msk"], in_=ctx.dbg_msk_sb[:]).inc("dma", 16)
                            sy.dma_start(out=ctx.dbg["att"], in_=ctx.dbg_att_sb[:]).inc("dma", 16)
                        ve.tensor_mul(out=attnT[:], in0=attnT[:], in1=mskt[:]).inc("ve")
                        te.wait("ve")
                        for qq in range(2):
                            asl = slice(qq * 512, (qq + 1) * 512)
                            psl = slice(qq * 512, (qq + 1) * 512)
                            te.matmul(p_oa[:, psl], vop[kt][:, 0:128], attnT[:, asl],
                                      start=(kt == 0), stop=(kt == 15))
                            te.matmul(p_ob[:, psl], vop[kt][:, 128:256], attnT[:, asl],
                                      start=(kt == 0), stop=(kt == 15))
                            mi = te.matmul(p_nd[:, psl], ct["ones_b"][0:128, :], attnT[:, asl],
                                           start=(kt == 0), stop=(kt == 15))
                        mi.inc("te")
                    ve.wait("te")
                    if getattr(ctx, "dbg", None) is not None and g == 0 and s == 0 and qh == 0:
                        ve.tensor_copy(out=ctx.dbg_msk_sb[0:1, :], in_=p_nd[:]).inc("ve")
                        sy.wait("ve")
                        sy.dma_start(out=ctx.dbg["den"], in_=ctx.dbg_msk_sb[0:1, :]).inc("dma", 16)
                    ve.reciprocal(out=n_k[0:1, 0:1024], in_=p_nd[:]).inc("ve")
                    te.wait("ve")
                    for qq in range(2):
                        qsl = slice(qq * 512, (qq + 1) * 512)
                        te.wait("se")
                        te.matmul(p_qv[:], ct["onesrow_f"][0:1, :], n_k[0:1, qsl],
                                  start=True, stop=True).inc("te")
                        sc.wait("te")
                        sc.copy(out=drep[:, qsl], in_=p_qv[:]).inc("se")
                    for half, p_o in ((0, p_oa), (1, p_ob)):
                        jrow = (g * 2 + half) * 128
                        tg = slice(ts0 + qof, ts0 + qof + 1024)
                        sy.wait("ve")
                        sy.dma_start(out=xt2[:], in_=x_hbm[jrow:jrow + 128, tg]).inc("dma", 16)
                        ve.wait("se")
                        ve.tensor_mul(out=yt2[:], in0=p_o[:], in1=drep[:])
                        ve.wait("dma")
                        ve.tensor_add(out=yt2[:], in0=yt2[:], in1=xt2[:]).inc("ve")
                        sy.wait("ve")
                        sy.dma_start(out=y_hbm[jrow:jrow + 128, tg], in_=yt2[:]).inc("dst", 16)
                        ve.wait("dst")
                    te.wait("ve")
        ctx.fence()


def build_l1a(debug=False):
    nc = bass.Bass("TRN2", target_bir_lowering=False)
    ctx = Ctx(nc)
    ctx.consts = declare_consts(nc)
    if debug:
        ctx.dbg = {
            "msk": nc.dram_tensor("dbg_msk", [128, 1024], F32, kind="ExternalOutput").ap(),
            "att": nc.dram_tensor("dbg_att", [128, 1024], F32, kind="ExternalOutput").ap(),
            "dxrep": nc.dram_tensor("dbg_dxrep", [128, L], F32, kind="ExternalOutput").ap(),
            "negc": nc.dram_tensor("dbg_negc", [1, L], BF16, kind="ExternalOutput").ap(),
            "dx0": nc.dram_tensor("dbg_dx0", [1, L], F32, kind="ExternalOutput").ap(),
            "den": nc.dram_tensor("dbg_den", [1, 1024], F32, kind="ExternalOutput").ap(),
        }
    else:
        ctx.dbg = None
    x_hbm = nc.dram_tensor("x", [D, T], F32, kind="ExternalInput").ap()
    w = {}
    w["qkvwT"] = nc.dram_tensor("qkvwT", [1024, 3072], F32, kind="ExternalInput").ap()
    w["qkvb_row"] = nc.dram_tensor("qkvb_row", [1, 3072], F32, kind="ExternalInput").ap()
    w["qkvb_cols"] = nc.dram_tensor("qkvb_cols", [128, 24], F32, kind="ExternalInput").ap()
    w["offwT"] = nc.dram_tensor("offwT", [1024, 256], F32, kind="ExternalInput").ap()
    w["lin2wT"] = nc.dram_tensor("lin2wT", [128, 2], F32, kind="ExternalInput").ap()
    w["lin2_b"] = nc.dram_tensor("lin2_b", [1, 1], F32, kind="ExternalInput").ap()
    w["r3"] = nc.dram_tensor("r3", [128, 4096], F32, kind="ExternalInput").ap()
    y_hbm = nc.dram_tensor("y", [D, T], F32, kind="ExternalOutput").ap()

    from contextlib import ExitStack
    with ExitStack() as es:
        def sb(name, shape, dtype):
            return es.enter_context(nc.sbuf_tensor(name, shape, dtype))
        ct = load_consts(ctx, sb)
        if ctx.dbg is not None:
            ctx.dbg_msk_sb = sb("dbg_msk_sb", [128, 1024], F32)
            ctx.dbg_att_sb = sb("dbg_att_sb", [128, 1024], F32)
        emit_l1a(ctx, ct, x_hbm, w, y_hbm)
    ctx.sy.wait("dma")
    ctx.sy.wait("dst")
    ctx.emit()
    return nc


def l1a_host_inputs(inputs):
    m = {}
    m["qkvwT"] = np.ascontiguousarray(np.asarray(inputs["L1_attn_a_qkv_w"], np.float32).T)
    qb = np.asarray(inputs["L1_attn_a_qkv_b"], np.float32)
    m["qkvb_row"] = np.ascontiguousarray(qb.reshape(1, 3072))
    m["qkvb_cols"] = np.ascontiguousarray(qb.reshape(24, 128).T)
    m["offwT"] = np.ascontiguousarray(np.asarray(inputs["L1_attn_a_off_w"], np.float32).T)
    m["lin2wT"] = np.ascontiguousarray(np.asarray(inputs["L1_attn_a_lin2_w"], np.float32).reshape(2, 128).T)
    m["lin2_b"] = np.ascontiguousarray(np.asarray(inputs["L1_attn_a_lin2_b"], np.float32).reshape(1, 1))
    p = np.arange(128, dtype=np.float64)[:, None]
    c = np.arange(4096, dtype=np.float64)[None, :]
    v = p - c + 2048.0
    r3 = np.where(v < 0, 2e18, v).astype(np.float32)
    m["r3"] = r3
    return m



def emit_moe(ctx, ct, x_hbm, w, y_hbm, scr, tag="", skip_wconv=False):
    from contextlib import ExitStack
    nc = ctx.nc
    sy, te, ve, sc, gp = ctx.sy, ctx.te, ctx.ve, ctx.sc, ctx.gp

    with ExitStack() as es:
        def sb(name, shape, dtype):
            return es.enter_context(nc.sbuf_tensor(name + tag, shape, dtype))

        maskT = sb("mo_maskT", [8, T], F32)
        wg_sb = sb("mo_wg", [128, 64], BF16)
        bgrep = sb("mo_bgrep", [128, 8], F32)
        b1c = sb("mo_b1c", [128, 256], F32)
        b2e = sb("mo_b2e", [1, 1024], BF16)
        identf = sb("mo_identf", [128, 128], F32)
        onesrow512 = sb("mo_ones512", [1, 512], BF16)

        with ExitStack() as p0:
            def sb0(n, sh, d):
                return p0.enter_context(nc.sbuf_tensor(n + tag, sh, d))
            def pb0(n, sh, d):
                return p0.enter_context(nc.psum_tensor(n + tag, sh, d))
            xhat = [sb0(f"mo_xhat{i}", [128, T], BF16) for i in range(8)]
            emit_ln(ctx, ct, x_hbm, xhat, sb0, pb0)

            wst = sb0("mo_wst", [128, 4096], F32)
            wstb = sb0("mo_wstb", [128, 4096], BF16)
            lg = sb0("mo_lg", [128, 8], F32)
            Et = sb0("mo_Et", [128, 8], F32)
            mrow = sb0("mo_m", [128, 1], F32)
            negm = sb0("mo_negm", [128, 1], F32)
            srow = sb0("mo_s", [128, 1], F32)
            p_g = pb0("mo_p_g", [128, 8], F32)
            if getattr(ctx, "dbg_lg", None) is not None:
                ctx.dbg_lg_sb = sb0("dbg_lg_sb", [128, 8], F32)
                ctx.dbg_m_sb = sb0("dbg_m_sb", [128, 1], F32)
            p_tr = pb0("mo_p_tr", [128, 128], F32)

            sy.dma_start(out=identf[:], in_=w["identf"]).inc("dma", 16)
            sy.dma_start(out=onesrow512[:], in_=w["ones512b"]).inc("dma", 16)
            sy.dma_start(out=b1c[:], in_=w["b1c"]).inc("dma", 16)
            for kt in range(8):
                gp.dma_start(out=wg_sb[:, kt * 8:(kt + 1) * 8],
                             in_=w["Wg"][kt * 128:(kt + 1) * 128, :]).inc("dma", 16)
            bgrow = sb0("mo_bgrow", [1, 8], F32)
            sy.dma_start(out=bgrow[:], in_=w["bg"]).inc("dma", 16)
            te.wait("dma")
            te.matmul(p_g[:], ct["onesrow_f"][0:1, :], bgrow[:],
                      start=True, stop=True).inc("te")
            sc.wait("te")
            sc.copy(out=bgrep[:], in_=p_g[:]).inc("se")
            te.wait("se")

            for i in range(8):
                sy.wait("ve")
                sy.dma_start(out=scr["xh"][i * 128:(i + 1) * 128, :],
                             in_=xhat[i][:]).inc("dst", 16)
            for ts in range(T // 128):
                tsl = slice(ts * 128, (ts + 1) * 128)
                te.wait("ve")
                te.wait("se")
                for kt in range(8):
                    mi = te.matmul(p_g[:], xhat[kt][:, tsl], wg_sb[:, kt * 8:(kt + 1) * 8],
                                   start=(kt == 0), stop=(kt == 7))
                mi.inc("te")
                ve.wait("te")
                ve.wait("se")
                ve.tensor_add(out=lg[:], in0=p_g[:], in1=bgrep[:])
                ctx.vsep()
                ve.reduce_max(out=mrow[:], in_=lg[:], axis=AX.X)
                ctx.vsep()
                ve.tensor_scalar_mul(out=negm[:], in0=mrow[:], scalar1=-1.0).inc("ve")
                if getattr(ctx, "dbg_lg", None) is not None and ts == 0:
                    ve.tensor_copy(out=ctx.dbg_lg_sb[:], in_=lg[:])
                    ve.tensor_copy(out=ctx.dbg_m_sb[:], in_=mrow[:]).inc("ve")
                sc.wait("ve")
                sc.activation(out=Et[:], in_=lg[:], func=AF.Exp, bias=negm[:],
                              accum_out=srow[:]).inc("se")
                ve.wait("se")
                ve.reciprocal(out=srow[:], in_=srow[:])
                ctx.vsep()
                ve.tensor_scalar(out=lg[:], in0=lg[:], scalar1=mrow[:],
                                 scalar2=srow[:], op0=ALU.is_ge,
                                 op1=ALU.mult).inc("ve")
                te.wait("ve")
                te.transpose(p_tr[0:8, :], lg[:, 0:8] if False else lg[:],
                             identf[:]).inc("te")
                sc.wait("te")
                sc.copy(out=maskT[:, tsl], in_=p_tr[0:8, 0:128]).inc("se")
                te.wait("se")
                ctx.fence()
            if getattr(ctx, "dbg_maskT", None) is not None:
                sy.wait("se")
                sy.wait("ve")
                sy.dma_start(out=ctx.dbg_maskT, in_=maskT[:]).inc("dst", 16)
                sy.dma_start(out=ctx.dbg_lg, in_=ctx.dbg_lg_sb[:]).inc("dst", 16)
                sy.dma_start(out=ctx.dbg_m, in_=ctx.dbg_m_sb[:]).inc("dst", 16)
            ctx.fence()

        if not skip_wconv:
            with ExitStack() as p1:
                def sb1(n, sh, d):
                    return p1.enter_context(nc.sbuf_tensor(n + tag, sh, d))
                wst = sb1("mo_wst1", [128, 4096], F32)
                wstb = sb1("mo_wstb1", [128, 4096], BF16)
                for e in range(8):
                    for i in range(8):
                        sy.wait("ve")
                        sy.dma_start(out=wst[:], in_=w["W1"][e, i * 128:(i + 1) * 128, :]).inc("dma", 16)
                        ve.wait("dma")
                        ve.tensor_copy(out=wstb[:], in_=wst[:]).inc("ve")
                        sy.wait("ve")
                        sy.dma_start(out=scr["w1c"][(e * 8 + i) * 128:(e * 8 + i + 1) * 128, :],
                                     in_=wstb[:]).inc("dst", 16)
                    for i in range(8):
                        sy.wait("ve")
                        sy.dma_start(out=wst[:], in_=bass.AP(w["W2"].tensor,
                                     (e * 4096 + i * 512) * 1024,
                                     [[1024, 128], [128 * 1024, 4], [1, 1024]])).inc("dma", 16)
                        ve.wait("dma")
                        ve.tensor_copy(out=wstb[:], in_=wst[:]).inc("ve")
                        sy.wait("ve")
                        sy.dma_start(out=scr["w2c"][(e * 8 + i) * 128:(e * 8 + i + 1) * 128, :],
                                     in_=wstb[:]).inc("dst", 16)
                ctx.fence()

        sy.dma_start(out=y_hbm, in_=x_hbm).inc("dst", 16)
        ctx.fence()

        with ExitStack() as p3:
            def sb3(n, sh, d):
                return p3.enter_context(nc.sbuf_tensor(n + tag, sh, d))
            def pb3(n, sh, d):
                return p3.enter_context(nc.psum_tensor(n + tag, sh, d))
            w1_sb = [sb3(f"mo_w1_{i}", [128, 4096], BF16) for i in range(8)]
            w2_sb = [sb3(f"mo_w2_{i}", [128, 1024], BF16) for i in range(32)]
            xb = [sb3(f"mo_xb{i}", [128, 512], BF16) for i in range(8)]
            h_act = [sb3(f"mo_h{i}", [128, 512], BF16) for i in range(32)]
            mwrep = sb3("mo_mwrep", [128, 512], F32)
            yt = sb3("mo_yt", [128, 512], F32)
            ot = sb3("mo_ot", [128, 512], F32)
            p_h = [pb3(f"mo_p_h{i}", [128, 512], F32) for i in range(2)]
            p_o = [pb3(f"mo_p_o{i}", [128, 512], F32) for i in range(2)]
            p_mw = pb3("mo_p_mw", [128, 512], F32)

            for e in range(8):
                gp.wait("te")
                gp.dma_start(out=b2e[:],
                             in_=w["b2row"][0:1, e * 1024:(e + 1) * 1024]).inc("dma", 16)
                sy.wait("te")
                for i in range(8):
                    sy.dma_start(out=w1_sb[i][:],
                                 in_=scr["w1c"][(e * 8 + i) * 128:(e * 8 + i + 1) * 128,
                                                :]).inc("dma", 16)
                for i in range(32):
                    sy.dma_start(out=w2_sb[i][:],
                                 in_=scr["w2c"][(e * 8 + i // 4) * 128 + (i % 4) * 0:
                                                (e * 8 + i // 4) * 128 + 128,
                                                (i % 4) * 1024:(i % 4) * 1024 + 1024]
                                 ).inc("dma", 16)
                hsilu = [0, 0]
                for tblk in range(8):
                    tbl = slice(tblk * 512, (tblk + 1) * 512)
                    sy.wait("te")
                    for kt in range(8):
                        sy.dma_start(out=xb[kt][:],
                                     in_=scr["xh"][kt * 128:(kt + 1) * 128, tbl]).inc("dma", 16)
                    te.wait("dma")
                    te.wait("se")
                    te.wait("ve")
                    te.matmul(p_mw[:], w["sel8_sb"][:, e * 128:(e + 1) * 128],
                              maskT[:, tbl], start=True, stop=True).inc("te")
                    sc.wait("te")
                    sc.copy(out=mwrep[:], in_=p_mw[:]).inc("se")
                    te.wait("dma")
                    for hg in range(32):
                        pb = p_h[hg % 2]
                        if hg >= 2:
                            te.wait("se", hsilu[hg % 2])
                        for kt in range(8):
                            mi = te.matmul(pb[:], w1_sb[kt][:, hg * 128:(hg + 1) * 128],
                                           xb[kt][:], start=(kt == 0), stop=(kt == 7))
                        mi.inc("te")
                        sc.wait("te")
                        hsilu[hg % 2] = sc.activation(
                            out=h_act[hg][:], in_=pb[:], func=AF.Silu,
                            bias=b1c[:, e * 32 + hg: e * 32 + hg + 1]).inc("se")
                    te.wait("se")
                    for jt in range(8):
                        po = p_o[jt % 2]
                        te.wait("ve")
                        for i in range(32):
                            te.matmul(po[:], w2_sb[i][:, jt * 128:(jt + 1) * 128],
                                      h_act[i][:], start=(i == 0), stop=False)
                        mi = te.matmul(po[:], b2e[0:1, jt * 128:(jt + 1) * 128],
                                       onesrow512[:], start=False, stop=True)
                        mi.inc("te")
                        sy.wait("ve")
                        sy.dma_start(out=yt[:] if False else yt[:],
                                     in_=y_hbm[jt * 128:(jt + 1) * 128, tbl]).inc("dma", 16)
                        ve.wait("te")
                        ve.wait("se")
                        ve.tensor_mul(out=ot[:], in0=po[:], in1=mwrep[:]).inc("ve")
                        ve.wait("dma")
                        ve.tensor_add(out=yt[:], in0=yt[:], in1=ot[:]).inc("ve")
                        sy.wait("ve")
                        sy.dma_start(out=y_hbm[jt * 128:(jt + 1) * 128, tbl],
                                     in_=yt[:]).inc("dst", 16)
                        ve.wait("dst")
            ctx.fence()


def build_moe(debug=False):
    nc = bass.Bass("TRN2", target_bir_lowering=False)
    ctx = Ctx(nc)
    ctx.consts = declare_consts(nc)
    x_hbm = nc.dram_tensor("x", [D, T], F32, kind="ExternalInput").ap()
    w = {}
    w["Wg"] = nc.dram_tensor("Wg", [1024, 8], F32, kind="ExternalInput").ap()
    w["bg"] = nc.dram_tensor("bg", [1, 8], F32, kind="ExternalInput").ap()
    w["W1"] = nc.dram_tensor("W1", [8, 1024, 4096], F32, kind="ExternalInput").ap()
    w["b1c"] = nc.dram_tensor("b1c", [128, 256], F32, kind="ExternalInput").ap()
    w["W2"] = nc.dram_tensor("W2", [8, 4096, 1024], F32, kind="ExternalInput").ap()
    w["b2row"] = nc.dram_tensor("b2row", [1, 8192], F32, kind="ExternalInput").ap()
    w["identf"] = nc.dram_tensor("identf", [128, 128], F32, kind="ExternalInput").ap()
    w["ones512b"] = nc.dram_tensor("ones512b", [1, 512], BF16, kind="ExternalInput").ap()
    w["sel8"] = nc.dram_tensor("sel8", [8, 1024], F32, kind="ExternalInput").ap()
    y_hbm = nc.dram_tensor("y", [D, T], F32, kind="ExternalOutput").ap()
    scr = {}
    kind = "ExternalOutput" if debug else "Internal"
    scr["xh"] = nc.dram_tensor("scr_xh", [1024, T], BF16, kind=kind).ap()
    scr["w1c"] = nc.dram_tensor("scr_w1c", [8192, 4096], BF16, kind=kind).ap()
    scr["w2c"] = nc.dram_tensor("scr_w2c", [8192, 4096], BF16, kind=kind).ap()
    if debug:
        ctx.dbg_maskT = nc.dram_tensor("dbg_maskT", [8, T], F32, kind="ExternalOutput").ap()
        ctx.dbg_lg = nc.dram_tensor("dbg_lg", [128, 8], F32, kind="ExternalOutput").ap()
        ctx.dbg_m = nc.dram_tensor("dbg_m", [128, 1], F32, kind="ExternalOutput").ap()

    from contextlib import ExitStack
    with ExitStack() as es:
        def sb(name, shape, dtype):
            return es.enter_context(nc.sbuf_tensor(name, shape, dtype))
        ct = load_consts(ctx, sb)
        sel8_sb = sb("mo_sel8", [8, 1024], F32)
        ctx.sy.dma_start(out=sel8_sb[:], in_=w["sel8"]).inc("dma", 16)
        w["sel8_sb"] = sel8_sb
        emit_moe(ctx, ct, x_hbm, w, y_hbm, scr)
    ctx.sy.wait("dma")
    ctx.sy.wait("dst")
    ctx.emit()
    return nc


def moe_host_inputs(inputs):
    m = {}
    m["Wg"] = np.ascontiguousarray(np.asarray(inputs["L1_ffn_Wg"], np.float32))
    m["bg"] = np.ascontiguousarray(np.asarray(inputs["L1_ffn_bg"], np.float32).reshape(1, 8))
    m["W1"] = np.ascontiguousarray(np.asarray(inputs["L1_ffn_W1"], np.float32))
    m["b1c"] = np.ascontiguousarray(np.asarray(inputs["L1_ffn_b1"], np.float32)
                                    .reshape(8, 32, 128).transpose(2, 0, 1).reshape(128, 256))
    m["W2"] = np.ascontiguousarray(np.asarray(inputs["L1_ffn_W2"], np.float32))
    m["b2row"] = np.ascontiguousarray(np.asarray(inputs["L1_ffn_b2"], np.float32).reshape(1, 8192))
    m["identf"] = np.eye(128, dtype=np.float32)
    m["ones512b"] = np.ones((1, 512), BF16NP)
    sel8 = np.zeros((8, 1024), np.float32)
    for e in range(8):
        sel8[e, e * 128:(e + 1) * 128] = 1.0
    m["sel8"] = sel8
    return m



def emit_l1b(ctx, ct, x_hbm, w, y_hbm, scr):
    from contextlib import ExitStack
    nc = ctx.nc
    sy, te, ve, sc, gp = ctx.sy, ctx.te, ctx.ve, ctx.sc, ctx.gp
    QSC = 1.0 / 32.0

    with ExitStack() as es:
        def sb(name, shape, dtype):
            return es.enter_context(nc.sbuf_tensor(name, shape, dtype))

        xhat = [sb(f"b1_xhat{i}", [128, T], BF16) for i in range(8)]
        with ExitStack() as ln_es:
            emit_ln(ctx, ct, x_hbm, xhat,
                    lambda n, s, d: ln_es.enter_context(nc.sbuf_tensor(n, s, d)),
                    lambda n, s, d: ln_es.enter_context(nc.psum_tensor(n, s, d)))

        with ExitStack() as ap:
            def sba(n, sh, d):
                return ap.enter_context(nc.sbuf_tensor(n, sh, d))
            def pba(n, sh, d):
                return ap.enter_context(nc.psum_tensor(n, sh, d))
            wstage = sba("b1_wstage", [128, 512], F32)
            wq_sb = sba("b1_wq", [128, 512], BF16)
            wk_sb = sba("b1_wk", [128, 512], BF16)
            wv_sb = sba("b1_wv", [128, 512], BF16)
            qop = sba("b1_qop", [65, L], BF16)
            kop = sba("b1_kop", [65, L], BF16)
            vop_all = sba("b1_vop", [128, 16 * 65], BF16)
            sqn = sba("b1_sqn", [64, L], BF16)
            n_k = sba("b1_nk", [1, L], F32)
            b2 = sba("b1_b2", [1, 1], F32)
            c_row = sba("b1_crow", [1, L], F32)
            dinv = sba("b1_dinv", [1, 1024], F32)
            attnT = [sba(f"b1_attnT{i}", [128, 512], BF16) for i in range(4)]
            o_sb = sba("b1_osb", [65, 1024], F32)
            drep = sba("b1_drep", [64, 1024], F32)
            gstage = sba("b1_gstage", [128, 128], F32)
            gbf = sba("b1_gbf", [128, 128], BF16)
            adg = sba("b1_adg", [128, 128], BF16)
            tblk = sba("b1_tblk", [128, 31 * 128], BF16)
            hot = sba("b1_hot", [64, 1024], F32)
            hot_bf = sba("b1_hotbf", [64, 1024], BF16)

            p_qv = pba("b1_p_qv", [128, 512], F32)
            p_q2 = pba("b1_p_q2", [128, 512], F32)
            p_rot = [p_qv, p_q2]
            p_s = [pba("b1_p_sc0", [128, 512], F32), pba("b1_p_sc1", [128, 512], F32)]
            p_o = pba("b1_p_o", [65, 1024], F32)
            p_b = pba("b1_p_b", [64, 1024], F32)
            p_dr = p_qv
            p_n = p_qv

            vop = [vop_all[:, tt * 65:(tt + 1) * 65] for tt in range(16)]
            gp.memset(kop[64:65, :], 1.0)
            for tt in range(16):
                g = gp.memset(vop_all[:, tt * 65 + 64: tt * 65 + 65], 1.0)
            g.inc("gp")
            te.wait("gp")
            sc.wait("gp")
            sy.dma_start(out=adg[:], in_=w["antidiag"]).inc("dma", 16)

            for h in range(16):
                for dlt in range(31):
                    c0 = 128 * (dlt - 15) + 1920
                    sy.wait("ve")
                    src = bass.AP(w["tableT"].tensor, h * 4095 + c0, [[1, 128], [1, 128]])
                    sy.dma_start(out=gstage[:], in_=src).inc("dma", 16)
                    ve.wait("dma")
                    ve.wait("te")
                    ve.tensor_copy(out=gbf[:], in_=gstage[:]).inc("ve")
                    te.wait("ve")
                    te.wait("se")
                    te.matmul(p_dr[:, 0:128], adg[:], gbf[:], start=True,
                              stop=True).inc("te")
                    sc.wait("te")
                    sc.copy(out=tblk[:, dlt * 128:(dlt + 1) * 128],
                            in_=p_dr[:, 0:128]).inc("se")
                for wdst, whbm in ((wq_sb, w["wqT"]), (wk_sb, w["wkT"]), (wv_sb, w["wvT"])):
                    sy.wait("te")
                    sy.wait("ve")
                    src = bass.AP(whbm.tensor, h * 64,
                                  [[1024, 128], [128 * 1024, 8], [1, 64]])
                    sy.dma_start(out=wstage[:], in_=src).inc("dma", 16)
                    ve.wait("dma")
                    ve.wait("te")
                    ve.tensor_copy(out=wdst[:], in_=wstage[:]).inc("ve")
                wrdy = ctx.cnt["ve"]
                for s in range(SEQ):
                    ts0 = s * L
                    te.wait("ve", wrdy)
                    rot_free = [0, 0]
                    ri = 0
                    jobs = []
                    for qc in range(4):
                        jobs.append(("q", qc))
                        jobs.append(("k", qc))
                    for tt in range(16):
                        jobs.append(("v", tt))
                    for kind, idx in jobs:
                        pbuf = p_rot[ri]
                        te.wait("se", rot_free[ri])
                        if kind in ("q", "k"):
                            qsl = slice(ts0 + idx * 512, ts0 + (idx + 1) * 512)
                            lsl = slice(idx * 512, (idx + 1) * 512)
                            wsb = wq_sb if kind == "q" else wk_sb
                            for kt in range(8):
                                mi = te.matmul(pbuf[0:64, :], wsb[:, kt * 64:(kt + 1) * 64],
                                               xhat[kt][:, qsl], start=(kt == 0), stop=(kt == 7))
                            mi.inc("te")
                            sc.wait("te")
                            dst = qop if kind == "q" else kop
                            rot_free[ri] = sc.activation(
                                out=dst[0:64, lsl], in_=pbuf[0:64, :], func=AF.Copy,
                                scale=(QSC if kind == "q" else 1.0)).inc("se")
                        else:
                            tsl = slice(ts0 + idx * 128, ts0 + (idx + 1) * 128)
                            for kt in range(8):
                                mi = te.matmul(pbuf[:, 0:64], xhat[kt][:, tsl],
                                               wv_sb[:, kt * 64:(kt + 1) * 64],
                                               start=(kt == 0), stop=(kt == 7))
                            mi.inc("te")
                            sc.wait("te")
                            rot_free[ri] = sc.copy(out=vop[idx][:, 0:64],
                                                   in_=pbuf[:, 0:64]).inc("se")
                        ri = (ri + 1) % 2
                    sc.activation(out=sqn[:], in_=kop[0:64, :], func=AF.Square).inc("se")
                    te.wait("se")
                    for qc in range(4):
                        te.wait("ve")
                        te.matmul(p_n[0:1, 0:512], ct["ones_b"][0:64, :],
                                  sqn[:, qc * 512:(qc + 1) * 512],
                                  start=True, stop=True).inc("te")
                        ve.wait("te")
                        ve.tensor_copy(out=n_k[0:1, qc * 512:(qc + 1) * 512],
                                       in_=p_n[0:1, 0:512]).inc("ve")
                    ctx.vsep()
                    ve.reduce_max(out=b2[:], in_=n_k[:], axis=AX.X).inc("ve")
                    sc.wait("te")
                    sc.activation(out=sqn[:], in_=qop[0:64, :], func=AF.Square).inc("se")
                    te.wait("se")
                    te.wait("ve")
                    for qc in range(4):
                        te.wait("se")
                        te.matmul(p_n[0:1, 0:512], ct["ones_b"][0:64, :],
                                  sqn[:, qc * 512:(qc + 1) * 512],
                                  start=True, stop=True).inc("te")
                        sc.wait("te")
                        sc.wait("ve")
                        sc.activation(out=c_row[0:1, qc * 512:(qc + 1) * 512],
                                      in_=p_n[0:1, 0:512], func=AF.Sqrt,
                                      scale=b2[:]).inc("se")
                    sc.mul(out=qop[64:65, :], in_=c_row[:], mul=-1.0).inc("se")
                    p_sx = [p_s[0], p_s[1], p_q2, p_qv]
                    exp_cnt = [0, 0, 0, 0]
                    pv_cnt = [0, 0, 0, 0]
                    cbase = ctx.cnt.get("se", 0)
                    for qh in range(2):
                        for kt in range(16):
                            for qq in range(2):
                                buf = (kt * 2 + qq) % 4
                                te.wait("se", cbase)
                                te.wait("se", exp_cnt[buf])
                                te.matmul(p_sx[buf][:],
                                          kop[:, kt * 128:(kt + 1) * 128],
                                          qop[:, qh * 1024 + qq * 512:
                                              qh * 1024 + (qq + 1) * 512],
                                          start=True, stop=True).inc("te")
                                tmm = ctx.cnt["te"]
                                sc.wait("te", pv_cnt[buf])
                                sc.wait("te", tmm)
                                exp_cnt[buf] = sc.activation(out=attnT[buf][:],
                                                             in_=p_sx[buf][:],
                                                             func=AF.Exp).inc("se")
                                te.wait("se", exp_cnt[buf])
                                mi = te.matmul(p_o[:, qq * 512:(qq + 1) * 512],
                                               vop[kt][:, 0:65],
                                               attnT[buf][:],
                                               start=(kt == 0), stop=(kt == 15))
                                pv_cnt[buf] = mi.inc("te")
                            for qb in range(8):
                                dlt = qh * 8 + qb - kt
                                mi = te.matmul(p_b[:, qb * 128:(qb + 1) * 128],
                                               vop[kt][:, 0:64],
                                               tblk[:, (dlt + 15) * 128:(dlt + 16) * 128],
                                               start=(kt == 0), stop=(kt == 15))
                        mi.inc("te")
                        ve.wait("te")
                        ve.reciprocal(out=dinv[:], in_=p_o[64:65, :]).inc("ve")
                        te.wait("ve")
                        te.matmul(p_dr[:], ct["onesrow_f"][0:1, :], dinv[0:1, 0:512],
                                  start=True, stop=True).inc("te")
                        sc.wait("te")
                        sc.copy(out=drep[:, 0:512], in_=p_dr[0:64, :]).inc("se")
                        te.wait("se")
                        te.matmul(p_dr[:], ct["onesrow_f"][0:1, :],
                                  dinv[0:1, 512:1024], start=True, stop=True).inc("te")
                        sc.wait("te")
                        sc.copy(out=drep[:, 512:1024], in_=p_dr[0:64, :])
                        sc.copy(out=o_sb[:], in_=p_o[:]).inc("se")
                        ve.wait("se")
                        ve.tensor_mul(out=hot[:], in0=o_sb[0:64, :], in1=drep[:])
                        ve.tensor_add(out=hot_bf[:], in0=hot[:], in1=p_b[:]).inc("ve")
                        sy.wait("ve")
                        sy.dma_start(out=scr["ho"][h * 64:(h + 1) * 64,
                                                   ts0 + qh * 1024: ts0 + (qh + 1) * 1024],
                                     in_=hot_bf[:]).inc("dst", 16)
                        ve.wait("dst")
                        te.wait("ve")
            ctx.fence()

        with ExitStack() as lp:
            def sbl(n, sh, d):
                return lp.enter_context(nc.sbuf_tensor(n, sh, d))
            def pbl(n, sh, d):
                return lp.enter_context(nc.psum_tensor(n, sh, d))
            hot2 = [sbl(f"b1_ho{i}", [128, 512], BF16) for i in range(8)]
            sq8 = sbl("b1_sq8", [128, 4096], BF16)
            s_mean = sbl("b1_mean", [1, 512], F32)
            s_var = sbl("b1_var", [1, 512], F32)
            s_rstd = sbl("b1_rstd", [1, 512], F32)
            s_mr = sbl("b1_mr", [1, 512], F32)
            gcol = sbl("b1_gcol", [128, 8], F32)
            bcol = sbl("b1_bcol", [128, 8], F32)
            xt = sbl("b1_xt", [128, 512], F32)
            yt = sbl("b1_yt", [128, 512], F32)
            p_s1 = pbl("b1_p_s1", [1, 512], F32)
            p_s2 = pbl("b1_p_s2", [1, 512], F32)
            p_mr = pbl("b1_p_mr", [128, 512], F32)
            p_rs = pbl("b1_p_rs", [128, 512], F32)
            sy.dma_start(out=gcol[:], in_=w["gcol"]).inc("dma", 16)
            sy.dma_start(out=bcol[:], in_=w["bcol"]).inc("dma", 16)
            for tc in range(8):
                sl = slice(tc * 512, (tc + 1) * 512)
                sy.wait("ve")
                for ft in range(8):
                    sy.dma_start(out=hot2[ft][:],
                                 in_=scr["ho"][ft * 128:(ft + 1) * 128, sl]).inc("dma", 16)
                te.wait("dma")
                te.wait("ve")
                for ft in range(8):
                    te.matmul(p_s1[:], ct["ones_b"][:], hot2[ft][:],
                              start=(ft == 0), stop=(ft == 7))
                sc.wait("dma")
                sc.wait("te")
                for ft in range(8):
                    si = sc.activation(out=sq8[:, ft * 512:(ft + 1) * 512],
                                       in_=hot2[ft][:], func=AF.Square)
                si.inc("se")
                te.wait("se")
                for ft in range(8):
                    mi = te.matmul(p_s2[:], ct["ones_b"][:],
                                   sq8[:, ft * 512:(ft + 1) * 512],
                                   start=(ft == 0), stop=(ft == 7))
                mi.inc("te")
                ve.wait("te")
                ve.tensor_scalar_mul(out=s_mean[:], in0=p_s1[:], scalar1=1.0 / D)
                ve.tensor_mul(out=s_var[:], in0=s_mean[:], in1=s_mean[:])
                ve.scalar_tensor_tensor(out=s_var[:], in0=p_s2[:], scalar=1.0 / D,
                                        in1=s_var[:], op0=ALU.mult,
                                        op1=ALU.subtract).inc("ve")
                sc.wait("ve")
                sc.activation(out=s_var[:], in_=s_var[:], func=AF.Ln, bias=ct["eps"])
                sc.activation(out=s_rstd[:], in_=s_var[:], func=AF.Exp,
                              scale=-0.5).inc("se")
                ve.wait("se")
                ve.tensor_mul(out=s_mr[:], in0=s_mean[:], in1=s_rstd[:]).inc("ve")
                te.wait("ve")
                te.matmul(p_mr[:], ct["onesrow_f"][0:1, :], s_mr[:], start=True, stop=True)
                te.matmul(p_rs[:], ct["onesrow_f"][0:1, :], s_rstd[:], start=True,
                          stop=True).inc("te")
                ve.wait("te")
                for ft in range(8):
                    sy.wait("ve")
                    sy.dma_start(out=xt[:], in_=x_hbm[ft * 128:(ft + 1) * 128, sl]).inc("dma", 16)
                    ve.wait("dma")
                    ve.tensor_mul(out=yt[:], in0=hot2[ft][:], in1=p_rs[:])
                    ve.tensor_sub(out=yt[:], in0=yt[:], in1=p_mr[:])
                    ve.tensor_scalar(out=yt[:], in0=yt[:], scalar1=gcol[:, ft:ft + 1],
                                     scalar2=None, op0=ALU.mult)
                    ve.tensor_scalar(out=yt[:], in0=yt[:], scalar1=bcol[:, ft:ft + 1],
                                     scalar2=None, op0=ALU.add)
                    ve.tensor_add(out=yt[:], in0=yt[:], in1=xt[:]).inc("ve")
                    sy.wait("ve")
                    sy.dma_start(out=y_hbm[ft * 128:(ft + 1) * 128, sl],
                                 in_=yt[:]).inc("dst", 16)
                    ve.wait("dst")
            ctx.fence()


def build_l1b():
    nc = bass.Bass("TRN2", target_bir_lowering=False)
    ctx = Ctx(nc)
    ctx.consts = declare_consts(nc)
    x_hbm = nc.dram_tensor("x", [D, T], F32, kind="ExternalInput").ap()
    w = {}
    w["wqT"] = nc.dram_tensor("wqT", [1024, 1024], F32, kind="ExternalInput").ap()
    w["wkT"] = nc.dram_tensor("wkT", [1024, 1024], F32, kind="ExternalInput").ap()
    w["wvT"] = nc.dram_tensor("wvT", [1024, 1024], F32, kind="ExternalInput").ap()
    w["tableT"] = nc.dram_tensor("tableT", [16, 4095], F32, kind="ExternalInput").ap()
    w["antidiag"] = nc.dram_tensor("antidiag", [128, 128], BF16, kind="ExternalInput").ap()
    w["gcol"] = nc.dram_tensor("gcol", [128, 8], F32, kind="ExternalInput").ap()
    w["bcol"] = nc.dram_tensor("bcol", [128, 8], F32, kind="ExternalInput").ap()
    y_hbm = nc.dram_tensor("y", [D, T], F32, kind="ExternalOutput").ap()
    scr = {"ho": nc.dram_tensor("scr_ho", [1024, T], BF16).ap()}

    from contextlib import ExitStack
    with ExitStack() as es:
        def sb(name, shape, dtype):
            return es.enter_context(nc.sbuf_tensor(name, shape, dtype))
        ct = load_consts(ctx, sb)
        emit_l1b(ctx, ct, x_hbm, w, y_hbm, scr)
    ctx.sy.wait("dma")
    ctx.sy.wait("dst")
    ctx.emit()
    return nc


def l1b_host_inputs(inputs):
    m = {}
    m["wqT"] = np.ascontiguousarray(np.asarray(inputs["L1_attn_b_Wq"], np.float32).T)
    m["wkT"] = np.ascontiguousarray(np.asarray(inputs["L1_attn_b_Wk"], np.float32).T)
    m["wvT"] = np.ascontiguousarray(np.asarray(inputs["L1_attn_b_Wv"], np.float32).T)
    tab = np.asarray(inputs["L1_attn_b_bias_table"], np.float32)
    idx = np.minimum(np.arange(4095), tab.shape[0] - 1)
    m["tableT"] = np.ascontiguousarray(tab[idx].T)
    m["antidiag"] = np.ascontiguousarray(np.eye(128, dtype=BF16NP)[::-1])
    m["gcol"] = np.ascontiguousarray(np.asarray(inputs["L1_attn_b_ln_gamma"], np.float32).reshape(8, 128).T)
    m["bcol"] = np.ascontiguousarray(np.asarray(inputs["L1_attn_b_ln_beta"], np.float32).reshape(8, 128).T)
    return m


def build_l0a(debug=False):
    nc = bass.Bass("TRN2", target_bir_lowering=False)
    ctx = Ctx(nc)
    ctx.dbg = debug
    ctx.consts = declare_consts(nc)
    if debug:
        ctx.dbg_xhat = nc.dram_tensor("dbg_xhat", [1024, T], BF16, kind="ExternalOutput").ap()
        ctx.dbg_qop = nc.dram_tensor("dbg_qop", [65, L], BF16, kind="ExternalOutput").ap()
        ctx.dbg_kop = nc.dram_tensor("dbg_kop", [65, L], BF16, kind="ExternalOutput").ap()
        ctx.dbg_att = nc.dram_tensor("dbg_att", [128, 1024], BF16, kind="ExternalOutput").ap()
        ctx.dbg_den = nc.dram_tensor("dbg_den", [1, 1024], F32, kind="ExternalOutput").ap()
        ctx.dbg_nk = nc.dram_tensor("dbg_nk", [1, L], F32, kind="ExternalOutput").ap()
        ctx.dbg_b2 = nc.dram_tensor("dbg_b2", [1, 1], F32, kind="ExternalOutput").ap()
        ctx.dbg_sqn = nc.dram_tensor("dbg_sqn", [64, L], BF16, kind="ExternalOutput").ap()
    x_hbm = nc.dram_tensor("x", [D, T], F32, kind="ExternalInput").ap()
    wqkv = nc.dram_tensor("wqkv", [16, 3, 1024, 64], F32, kind="ExternalInput").ap()
    wo = nc.dram_tensor("wo", [1024, 1024], F32, kind="ExternalInput").ap()
    y_hbm = nc.dram_tensor("y", [D, T], F32, kind="ExternalOutput").ap()

    from contextlib import ExitStack
    with ExitStack() as es:
        def sb(name, shape, dtype):
            return es.enter_context(nc.sbuf_tensor(name, shape, dtype))
        ct = load_consts(ctx, sb)
        emit_l0a(ctx, ct, x_hbm, wqkv, wo, y_hbm)
    ctx.sy.wait("dma")
    ctx.sy.wait("dst")
    ctx.emit()
    return nc



def build_fused():
    nc = bass.Bass("TRN2", target_bir_lowering=False)
    ctx = Ctx(nc)
    ctx.consts = declare_consts(nc)
    ctx.dbg = None

    x_hbm = nc.dram_tensor("x", [D, T], F32, kind="ExternalInput").ap()
    y_hbm = nc.dram_tensor("y", [D, T], F32, kind="ExternalOutput").ap()
    xi = [nc.dram_tensor(f"xi{i}", [D, T], F32).ap() for i in range(7)]

    wqkv = nc.dram_tensor("wqkv", [16, 3, 1024, 64], F32, kind="ExternalInput").ap()
    wo = nc.dram_tensor("wo", [1024, 1024], F32, kind="ExternalInput").ap()
    wgT = nc.dram_tensor("wgT", [1024, 3072], I32, kind="ExternalInput").ap()
    wuT = nc.dram_tensor("wuT", [1024, 3072], I32, kind="ExternalInput").ap()
    wdT = nc.dram_tensor("wdT", [3072, 1024], I32, kind="ExternalInput").ap()
    ytmp = nc.dram_tensor("ytmp", [D, T], F32).ap()
    wm = {}
    wm["win"] = nc.dram_tensor("win", [1024, 4096], F32, kind="ExternalInput").ap()
    wm["convw"] = nc.dram_tensor("convw", [128, 64], F32, kind="ExternalInput").ap()
    wm["convb"] = nc.dram_tensor("convb", [128, 16], F32, kind="ExternalInput").ap()
    wm["wxdt"] = nc.dram_tensor("wxdt", [2048, 64], F32, kind="ExternalInput").ap()
    wm["wxB"] = nc.dram_tensor("wxB", [2048, 16], F32, kind="ExternalInput").ap()
    wm["wxC"] = nc.dram_tensor("wxC", [2048, 16], F32, kind="ExternalInput").ap()
    wm["wdt"] = nc.dram_tensor("wdt", [64, 2048], F32, kind="ExternalInput").ap()
    wm["bdt"] = nc.dram_tensor("bdt", [128, 16], F32, kind="ExternalInput").ap()
    wm["acols"] = nc.dram_tensor("acols", [128, 256], F32, kind="ExternalInput").ap()
    wm["dcol"] = nc.dram_tensor("dcol", [128, 16], F32, kind="ExternalInput").ap()
    wm["wout"] = nc.dram_tensor("wout", [2048, 1024], F32, kind="ExternalInput").ap()
    wm["sel16"] = nc.dram_tensor("sel16", [16, 2048], BF16, kind="ExternalInput").ap()
    scr_m = {}
    scr_m["u"] = nc.dram_tensor("scr_u", [2048, T], BF16).ap()
    scr_m["sil"] = nc.dram_tensor("scr_sil", [2048, T], BF16).ap()
    scr_m["y3"] = nc.dram_tensor("scr_y3", [2048, T], BF16).ap()
    scr_m["dtb"] = nc.dram_tensor("scr_dtb", [64, T], BF16).ap()
    scr_m["bc"] = nc.dram_tensor("scr_bc", [32, T], BF16).ap()
    wa = {}
    wa["qkvwT"] = nc.dram_tensor("qkvwT", [1024, 3072], F32, kind="ExternalInput").ap()
    wa["qkvb_row"] = nc.dram_tensor("qkvb_row", [1, 3072], F32, kind="ExternalInput").ap()
    wa["qkvb_cols"] = nc.dram_tensor("qkvb_cols", [128, 24], F32, kind="ExternalInput").ap()
    wa["offwT"] = nc.dram_tensor("offwT", [1024, 256], F32, kind="ExternalInput").ap()
    wa["lin2wT"] = nc.dram_tensor("lin2wT", [128, 2], F32, kind="ExternalInput").ap()
    wa["lin2_b"] = nc.dram_tensor("lin2_b", [1, 1], F32, kind="ExternalInput").ap()
    wa["r3"] = nc.dram_tensor("r3", [128, 4096], F32, kind="ExternalInput").ap()
    we = {}
    we["Wg"] = nc.dram_tensor("Wg", [1024, 8], F32, kind="ExternalInput").ap()
    we["bg"] = nc.dram_tensor("bg", [1, 8], F32, kind="ExternalInput").ap()
    we["W1"] = nc.dram_tensor("W1", [8, 1024, 4096], F32, kind="ExternalInput").ap()
    we["b1c"] = nc.dram_tensor("b1c", [128, 256], F32, kind="ExternalInput").ap()
    we["W2"] = nc.dram_tensor("W2", [8, 4096, 1024], F32, kind="ExternalInput").ap()
    we["b2row"] = nc.dram_tensor("b2row", [1, 8192], F32, kind="ExternalInput").ap()
    we["identf"] = nc.dram_tensor("identf", [128, 128], F32, kind="ExternalInput").ap()
    we["ones512b"] = nc.dram_tensor("ones512b", [1, 512], BF16, kind="ExternalInput").ap()
    we["sel8"] = nc.dram_tensor("sel8", [8, 1024], F32, kind="ExternalInput").ap()
    scr_e = {}
    scr_e["xh"] = nc.dram_tensor("scr_xh", [1024, T], BF16).ap()
    scr_e["w1c"] = nc.dram_tensor("scr_w1c", [8192, 4096], BF16).ap()
    scr_e["w2c"] = nc.dram_tensor("scr_w2c", [8192, 4096], BF16).ap()
    wb = {}
    wb["wqT"] = nc.dram_tensor("wqT", [1024, 1024], F32, kind="ExternalInput").ap()
    wb["wkT"] = nc.dram_tensor("wkT", [1024, 1024], F32, kind="ExternalInput").ap()
    wb["wvT"] = nc.dram_tensor("wvT", [1024, 1024], F32, kind="ExternalInput").ap()
    wb["tableT"] = nc.dram_tensor("tableT", [16, 4095], F32, kind="ExternalInput").ap()
    wb["antidiag"] = nc.dram_tensor("antidiag", [128, 128], BF16, kind="ExternalInput").ap()
    wb["gcol"] = nc.dram_tensor("gcol", [128, 8], F32, kind="ExternalInput").ap()
    wb["bcol"] = nc.dram_tensor("bcol", [128, 8], F32, kind="ExternalInput").ap()
    scr_b = {"ho": nc.dram_tensor("scr_ho", [1024, T], BF16).ap()}

    from contextlib import ExitStack
    with ExitStack() as es:
        def sb(name, shape, dtype):
            return es.enter_context(nc.sbuf_tensor(name, shape, dtype))
        ct = load_consts(ctx, sb)
        sel8_sb = sb("mo_sel8", [8, 1024], F32)
        ctx.sy.dma_start(out=sel8_sb[:], in_=we["sel8"]).inc("dma", 16)
        we["sel8_sb"] = sel8_sb

        emit_l0a(ctx, ct, x_hbm, wqkv, wo, xi[0])
        ctx.fence()
        emit_ffn0(ctx, ct, xi[0], wgT, wuT, wdT, xi[1], ytmp, tag="A")
        ctx.fence()
        emit_mamba(ctx, ct, xi[1], wm, xi[2], scr_m)
        ctx.fence()
        emit_ffn0(ctx, ct, xi[2], wgT, wuT, wdT, xi[3], ytmp, tag="B")
        ctx.fence()
        emit_l1a(ctx, ct, xi[3], wa, xi[4])
        ctx.fence()
        emit_moe(ctx, ct, xi[4], we, xi[5], scr_e, tag="A")
        ctx.fence()
        emit_l1b(ctx, ct, xi[5], wb, xi[6], scr_b)
        ctx.fence()
        emit_moe(ctx, ct, xi[6], we, y_hbm, scr_e, tag="B", skip_wconv=True)
    ctx.sy.wait("dma")
    ctx.sy.wait("dst")
    ctx.emit()
    return nc


def fused_host_inputs(inputs):
    m = {}
    m.update(host_consts())
    m["wqkv"] = np.ascontiguousarray(np.asarray(inputs["L0_attn_a_w_qkv"], np.float32))
    m["wo"] = np.ascontiguousarray(np.asarray(inputs["L0_attn_a_w_o"], np.float32))
    m["wgT"] = np.ascontiguousarray(np.asarray(inputs["L0_ffn_w_gate"], np.int32).T)
    m["wuT"] = np.ascontiguousarray(np.asarray(inputs["L0_ffn_w_up"], np.int32).T)
    m["wdT"] = np.ascontiguousarray(np.asarray(inputs["L0_ffn_w_down"], np.int32).T)
    m.update(mamba_host_inputs(inputs))
    m.update(l1a_host_inputs(inputs))
    m.update(moe_host_inputs(inputs))
    m.update(l1b_host_inputs(inputs))
    return m


def _to_feat_tok(x):
    b, t, f = x.shape
    return np.ascontiguousarray(x.transpose(2, 0, 1).reshape(f, b * t))


def _from_feat_tok(y):
    return np.ascontiguousarray(np.asarray(y).reshape(D, SEQ, L).transpose(1, 2, 0))


import os
_DBGFIN = os.environ.get("KDBG", "") == "1"


def _ckfin(tag, xcur):
    if _DBGFIN:
        import sys as _s
        for c, a in enumerate(xcur):
            f = np.isfinite(a).mean()
            if f < 1.0:
                print(f"[KDBG] {tag} core{c} finite={f}", file=_s.stderr)
        print(f"[KDBG] {tag} checked", file=_s.stderr)


def kernel_multi(**inputs):
    consts = host_consts()
    x = np.asarray(inputs["L0_attn_a_inputs"], np.float32)

    nc = build_l0a()
    wqkv_h = np.ascontiguousarray(np.asarray(inputs["L0_attn_a_w_qkv"], np.float32))
    wo_h = np.ascontiguousarray(np.asarray(inputs["L0_attn_a_w_o"], np.float32))
    in_maps = []
    for c in range(NCORES):
        m = dict(consts)
        m["x"] = _to_feat_tok(x[2 * c:2 * c + 2])
        m["wqkv"] = wqkv_h
        m["wo"] = wo_h
        in_maps.append(m)
    res = run_bass_kernel_spmd(nc, in_maps, core_ids=list(range(NCORES)))
    xcur = [np.asarray(r["y"]) for r in res.results]
    _ckfin("l0a", xcur)

    nc = build_ffn0()
    wgT_h = np.ascontiguousarray(np.asarray(inputs["L0_ffn_w_gate"], np.int32).T)
    wuT_h = np.ascontiguousarray(np.asarray(inputs["L0_ffn_w_up"], np.int32).T)
    wdT_h = np.ascontiguousarray(np.asarray(inputs["L0_ffn_w_down"], np.int32).T)
    in_maps = []
    for c in range(NCORES):
        m = dict(consts)
        m["x"] = xcur[c]
        m["wgT"] = wgT_h
        m["wuT"] = wuT_h
        m["wdT"] = wdT_h
        in_maps.append(m)
    res = run_bass_kernel_spmd(nc, in_maps, core_ids=list(range(NCORES)))
    xcur = [np.asarray(r["y"]) for r in res.results]
    _ckfin("ffn0a", xcur)

    nc = build_mamba()
    mh = mamba_host_inputs(inputs)
    in_maps = []
    for c in range(NCORES):
        m = dict(consts)
        m.update(mh)
        m["x"] = xcur[c]
        in_maps.append(m)
    res = run_bass_kernel_spmd(nc, in_maps, core_ids=list(range(NCORES)))
    xcur = [np.asarray(r["y"]) for r in res.results]
    _ckfin("mamba", xcur)

    nc = build_ffn0()
    in_maps = []
    for c in range(NCORES):
        m = dict(consts)
        m["x"] = xcur[c]
        m["wgT"] = wgT_h
        m["wuT"] = wuT_h
        m["wdT"] = wdT_h
        in_maps.append(m)
    res = run_bass_kernel_spmd(nc, in_maps, core_ids=list(range(NCORES)))
    xcur = [np.asarray(r["y"]) for r in res.results]
    _ckfin("ffn0b", xcur)

    nc = build_l1a()
    ah = l1a_host_inputs(inputs)
    in_maps = []
    for c in range(NCORES):
        m = dict(consts)
        m.update(ah)
        m["x"] = xcur[c]
        in_maps.append(m)
    res = run_bass_kernel_spmd(nc, in_maps, core_ids=list(range(NCORES)))
    xcur = [np.asarray(r["y"]) for r in res.results]
    _ckfin("l1a", xcur)

    nc = build_moe()
    eh = moe_host_inputs(inputs)
    in_maps = []
    for c in range(NCORES):
        m = dict(consts)
        m.update(eh)
        m["x"] = xcur[c]
        in_maps.append(m)
    res = run_bass_kernel_spmd(nc, in_maps, core_ids=list(range(NCORES)))
    xcur = [np.asarray(r["y"]) for r in res.results]
    _ckfin("moe1", xcur)

    nc = build_l1b()
    bh = l1b_host_inputs(inputs)
    in_maps = []
    for c in range(NCORES):
        m = dict(consts)
        m.update(bh)
        m["x"] = xcur[c]
        in_maps.append(m)
    res = run_bass_kernel_spmd(nc, in_maps, core_ids=list(range(NCORES)))
    xcur = [np.asarray(r["y"]) for r in res.results]
    _ckfin("l1b", xcur)

    nc = build_moe()
    in_maps = []
    for c in range(NCORES):
        m = dict(consts)
        m.update(eh)
        m["x"] = xcur[c]
        in_maps.append(m)
    res = run_bass_kernel_spmd(nc, in_maps, core_ids=list(range(NCORES)))
    xcur = [np.asarray(r["y"]) for r in res.results]
    _ckfin("moe2", xcur)

    x = np.concatenate([_from_feat_tok(y) for y in xcur], axis=0)
    return x


def kernel(**inputs):
    x = np.asarray(inputs["L0_attn_a_inputs"], np.float32)
    nc = build_fused()
    shared = fused_host_inputs(inputs)
    in_maps = []
    for c in range(NCORES):
        m = dict(shared)
        m["x"] = _to_feat_tok(x[2 * c:2 * c + 2])
        in_maps.append(m)
    res = run_bass_kernel_spmd(nc, in_maps, core_ids=list(range(NCORES)))
    return np.concatenate([_from_feat_tok(np.asarray(r["y"])) for r in res.results], axis=0)
